# version 09
# baseline (speedup 1.0000x reference)
import contextlib
import numpy as np
import concourse.bass as bass
import concourse.mybir as mybir
from concourse.bass_utils import run_bass_kernel_spmd

F32 = mybir.dt.float32
BF16 = mybir.dt.bfloat16
AF = mybir.ActivationFunctionType
ALU = mybir.AluOpType

NCORE = 8
BC = 2
L = 2048
DM = 1024
DI = 2048
NST = 16
DR = 64
TC = 512
NCH = BC * L // TC
NTI = TC // 128
NM = DI // 128
NKB = DM // 128
RNE_C = float(3 * 2.0 ** 22)
EPS = 1e-5

ENGS = ("sy", "gp", "ve", "sc", "te")

DEV_DRAIN = False
PH = 7
PHE = 6
ALV = 0
DBG = False


class Sem:
    def __init__(self, stack, nc, name):
        self.h = stack.enter_context(nc.semaphore(name))
        self.n = 0
        self.name = name


class Sched:

    def __init__(self):
        self.q = {e: [] for e in ENGS}

    def o(self, eng, fn, inc=None, dma=False, fence=False):
        incs = []
        if inc is not None:
            if not isinstance(inc, list):
                inc = [inc]
            for s in inc:
                k = 16 if dma else 1
                s.n += k
                incs.append((s, k))
        self.q[eng].append(("op", fn, incs))
        if fence:
            self.fence(eng)

    def fence(self, eng, inc=None):
        incs = []
        if inc is not None:
            if not isinstance(inc, list):
                inc = [inc]
            for s in inc:
                s.n += 1
                incs.append((s, 1))
        jb = self.junk[eng]
        if eng == "sc":
            self.q[eng].append(("op", lambda e: e.copy(jb[:, 0:1], jb[:, 2:3]), []))
            self.q[eng].append(("op", lambda e: e.copy(jb[:, 1:2], jb[:, 3:4]), incs))
        else:
            self.q[eng].append(("op", lambda e: e.tensor_copy(jb[:, 0:1], jb[:, 2:3]), []))
            self.q[eng].append(("op", lambda e: e.tensor_copy(jb[:, 1:2], jb[:, 3:4]), incs))

    def w(self, eng, sem, target=None):
        t = sem.n if target is None else target
        self.q[eng].append(("wait", sem, t))

    def replay(self, eng_name, eng):
        for item in self.q[eng_name]:
            if item[0] == "op":
                ins = item[1](eng)
                for (s, k) in item[2]:
                    ins.then_inc(s.h, k)
                if DEV_DRAIN and eng_name in ("ve", "sc", "te"):
                    eng.drain()
            else:
                eng.wait_ge(item[1].h, item[2])


class Ring:

    def __init__(self, n):
        self.n = n
        self.slots = [[] for _ in range(n)]

    def acquire(self, S, eng, idx):
        for (sem, t) in self.slots[idx % self.n]:
            S.w(eng, sem, t)
        self.slots[idx % self.n] = []

    def release(self, idx, sem, target=None):
        self.slots[idx % self.n].append((sem, sem.n if target is None else target))


def build_nc():
    nc = bass.Bass("TRN2", target_bir_lowering=False,
                   detect_race_conditions=DEV_DRAIN)

    ap_x = nc.dram_tensor("x", [BC, L, DM], F32, kind="ExternalInput").ap()
    ap_adj = nc.dram_tensor("adj_pm", [NM, 128, NM, 128], F32, kind="ExternalInput").ap()
    ap_w = nc.dram_tensor("w_pm", [2 * NM, 128, NKB, 128], F32, kind="ExternalInput").ap()
    ap_op = nc.dram_tensor("op_w", [DI, DM], F32, kind="ExternalInput").ap()
    ap_xp = nc.dram_tensor("xp_pm", [128, NM * 96], F32, kind="ExternalInput").ap()
    ap_dtw = nc.dram_tensor("dt_w", [DR, DI], F32, kind="ExternalInput").ap()
    ap_apm = nc.dram_tensor("a_pm", [128, NM * NST], F32, kind="ExternalInput").ap()
    ap_cw = nc.dram_tensor("convw_pm", [128, NM * 4], F32, kind="ExternalInput").ap()
    ap_cb = nc.dram_tensor("convb_pm", [128, NM], F32, kind="ExternalInput").ap()
    ap_dtb = nc.dram_tensor("dtb_pm", [128, NM], F32, kind="ExternalInput").ap()
    ap_dd = nc.dram_tensor("d_pm", [128, NM], F32, kind="ExternalInput").ap()
    ap_fw = nc.dram_tensor("ffn_w", [DM, DM], F32, kind="ExternalInput").ap()
    ap_h = nc.dram_tensor("h_bf", [DM, DM], BF16, kind="ExternalInput").ap()
    ap_eye = nc.dram_tensor("eye_bf", [128, 128], BF16, kind="ExternalInput").ap()

    ap_out = nc.dram_tensor("out", [BC, L, DM], F32, kind="ExternalOutput").ap()
    dbg_aps = {}
    if DBG:
        for nm, shape in [("d_xn", [128, NTI * DM]), ("d_xnT", [128, NKB * TC]),
                          ("d_xi", [128, NM * TC]), ("d_dT", [128, NM * TC]),
                          ("d_yf", [128, NM * TC]), ("d_xq", [128, NTI * DM]),
                          ("d_hc", [128, NM * NST]), ("d_brep", [128, 8 * TC]),
                          ("d_crep", [128, 8 * TC]), ("d_asb", [128, NM * NST]),
                          ("d_su", [128, TC]), ("d_al", [128, 8 * TC]),
                          ("d_hh", [128, 8 * TC]), ("d_wv", [128, 8 * TC]),
                          ("d_stat", [128, 16]), ("d_ln", [128, 16])]:
            dbg_aps[nm] = nc.dram_tensor(nm, shape, F32, kind="ExternalOutput").ap()
    d_adjbf = nc.dram_tensor("adj_bf", [NM, 128, NM, 128], BF16, kind="Internal").ap()
    d_wbf = nc.dram_tensor("w_bf", [2 * NM, 128, NKB, 128], BF16, kind="Internal").ap()
    d_opbf = nc.dram_tensor("op_bf", [DI, DM], BF16, kind="Internal").ap()
    d_ternbf = nc.dram_tensor("tern_bf", [DM, DM], BF16, kind="Internal").ap()
    d_dtw = nc.dram_tensor("dtw_bf", [DR, DI], BF16, kind="Internal").ap()
    d_bc = nc.dram_tensor("bc_scr", [32, TC], BF16, kind="Internal").ap()

    with contextlib.ExitStack() as stack:
        def sb(name, shape, dtype):
            return stack.enter_context(nc.sbuf_tensor(name, shape, dtype))

        def ps_(name, shape, dtype):
            return stack.enter_context(nc.psum_tensor(name, shape, dtype))

        eye_sb = sb("eye_sb", [128, 128], BF16)
        xpw_sb = sb("xpw_sb", [128, NM * 96], BF16)
        dtbuf = [sb(f"dtbuf{i}", [DR, 128], BF16) for i in range(2)]
        a_sb = sb("a_sb", [128, NM * NST], F32)
        cw_sb = sb("cw_sb", [128, NM * 4], F32)
        cb_sb = sb("cb_sb", [128, NM], F32)
        dtb_sb = sb("dtb_sb", [128, NM], F32)
        dd_sb = sb("dd_sb", [128, NM], F32)
        hcarry = sb("hcarry", [128, NM * NST], F32)
        halo_sb = sb("halo_sb", [128, NM * 3], F32)
        stat = sb("stat", [128, 16], F32)
        cst = sb("cst", [128, 2], F32)
        junk_v = sb("junk_v", [128, 4], F32)
        junk_s = sb("junk_s", [128, 4], F32)

        ln_snap = sb("ln_snap", [128, 16], F32)
        x_ld = [sb(f"x_ld{i}", [128, DM], F32) for i in range(2)]
        xn_sb = sb("xn_sb", [128, NTI * DM], BF16)
        xnT_sb = sb("xnT_sb", [128, NKB * TC], BF16)
        xq_sb = sb("xq_sb", [128, NTI * DM], BF16)
        xqT_sb = sb("xqT_sb", [128, NKB * TC], BF16)
        hbuf = sb("hbuf", [128, NKB * DM], BF16)
        wibuf = [sb(f"wibuf{i}", [128, NKB * 128], BF16) for i in range(2)]
        rwbuf = [sb(f"rwbuf{i}", [128, NKB * 128], BF16) for i in range(2)]
        adjbuf = [sb(f"adjbuf{i}", [128, DI], BF16) for i in range(2)]
        opbuf = [sb(f"opbuf{i}", [128, 512], BF16) for i in range(2)]
        ternbuf = [sb(f"ternbuf{i}", [128, 512], BF16) for i in range(2)]
        xip_sb = [sb(f"xip{i}", [128, TC + 3], BF16) for i in range(2)]
        cacc = [sb(f"cacc{i}", [128, TC], F32) for i in range(2)]
        xi_sb = sb("xi_sb", [128, NM * TC], BF16)
        dinT_sb = sb("dinT_sb", [DR, TC], BF16)
        deltaT_sb = sb("deltaT_sb", [128, NM * TC], BF16)
        su_sb = [sb(f"su{i}", [128, TC], BF16) for i in range(2)]
        brep = [sb(f"brep{g}", [128, 8 * TC], BF16) for g in range(2)]
        crep = [sb(f"crep{g}", [128, 8 * TC], BF16) for g in range(2)]
        al_sb = sb("al_sb", [128, 8 * TC], BF16)
        wv_sb = sb("wv_sb", [128, 8 * TC], BF16)
        hh_sb = sb("hh_sb", [128, 8 * TC], BF16)
        yg_sb = sb("yg_sb", [128, 2 * TC], BF16)
        t2_sb = sb("t2_sb", [128, TC], BF16)
        yf_sb = sb("yf_sb", [128, NM * TC], BF16)
        out_sb = [sb(f"out_sb{i}", [128, 512], F32) for i in range(2)]
        bc_sb = sb("bc_sb", [96, TC], BF16)

        ps_tr = [ps_(f"ps_tr{i}", [128, 1024], BF16) for i in range(2)]
        ps_a = [ps_(f"ps_a{i}", [128, 512], F32) for i in range(2)]
        ps_ip = [ps_(f"ps_ip{i}", [128, 512], F32) for i in range(2)]
        ps_dp = [ps_(f"ps_dp{i}", [128, 512], F32) for i in range(2)]

        sems = {}

        def S_(name):
            sems[name] = Sem(stack, nc, name)
            return sems[name]

        L_xld = [S_(f"L_xld{i}") for i in range(2)]
        L_wib = [S_(f"L_wib{i}") for i in range(2)]
        L_rwb = [S_(f"L_rwb{i}") for i in range(2)]
        L_adjb = [S_(f"L_adjb{i}") for i in range(2)]
        L_opb = [S_(f"L_opb{i}") for i in range(2)]
        L_ternb = [S_(f"L_ternb{i}") for i in range(2)]
        L_dtb = [S_(f"L_dtb{i}") for i in range(2)]
        P_wib = [S_(f"P_wib{i}") for i in range(2)]
        P_adjb = [S_(f"P_adjb{i}") for i in range(2)]
        L_hb = S_("L_hb"); L_small = S_("L_small"); L_h = S_("L_h")
        L_bcd = S_("L_bcd"); L_rep = S_("L_rep")
        S_os = [S_(f"S_os{i}") for i in range(2)]
        T_trn = S_("T_trn"); T_trq = S_("T_trq"); T_xh = S_("T_xh")
        T_ip = S_("T_ip"); T_res = S_("T_res"); T_xdbl = S_("T_xdbl")
        T_dt = S_("T_dt"); T_dp = S_("T_dp"); T_opk = S_("T_opk")
        T_tek = S_("T_tek"); T_out = S_("T_out"); T_ptr = S_("T_ptr")
        V_xn = S_("V_xn"); V_q = S_("V_q"); V_conv = S_("V_conv"); V_bc = S_("V_bc")
        G_y = S_("G_y"); V_tree = S_("V_tree")
        V_su = S_("V_su"); V_w = S_("V_w"); V_scan = S_("V_scan"); V_y = S_("V_y")
        V_oev = S_("V_oev"); V_cast = S_("V_cast"); V_pt = S_("V_pt")
        A_rstd = S_("A_rstd"); A_evxn = S_("A_evxn"); A_evxq = S_("A_evxq")
        A_evxi = S_("A_evxi"); A_silu = S_("A_silu"); A_res = S_("A_res")
        A_dinT = S_("A_dinT"); A_sp = S_("A_sp"); A_al = S_("A_al"); A_pev = S_("A_pev")

        S = Sched()
        S.junk = {"ve": junk_v, "sc": junk_s}

        r_xld = Ring(2); r_wib = Ring(2); r_rwb = Ring(2)
        r_adjb = Ring(2); r_opb = Ring(2); r_ternb = Ring(2)
        r_xip = Ring(2); r_cacc = Ring(2); r_su = Ring(2); r_out = Ring(2)
        r_pstr = Ring(2)
        r_psa = Ring(2); r_psip = Ring(2); r_psdp = Ring(2)
        r_xn = Ring(1); r_xnT = Ring(1); r_xq = Ring(1); r_xqT = Ring(1)
        r_xi = Ring(1); r_dT = Ring(1); r_dinT = Ring(1); r_yf = Ring(1)
        r_al = Ring(3); r_rep = Ring(1); r_bcs = Ring(1); r_hh = Ring(2)
        r_ternsc = Ring(1); r_hbsc = Ring(1)

        S.o("sy", lambda e: e.dma_start(out=eye_sb[:], in_=ap_eye), inc=L_small, dma=True)

        for m in range(NM):
            for piece in range(2):
                idx = m * 2 + piece
                r_xld.acquire(S, "sy", idx)
                src = ap_adj[m].rearrange("p a b -> p (a b)")[:, piece * 1024:(piece + 1) * 1024]
                S.o("sy", lambda e, s=src, d=x_ld[idx % 2]: e.dma_start(out=d[:], in_=s),
                    inc=L_xld[idx % 2], dma=True)
                r_adjb.acquire(S, "ve", m) if piece == 0 else None
                S.w("ve", L_xld[idx % 2])
                S.o("ve", lambda e, sl=x_ld[idx % 2], d=adjbuf[m % 2], p=piece:
                    e.tensor_copy(d[:, p * 1024:(p + 1) * 1024], sl[:]), inc=V_cast)
                r_xld.release(idx, V_cast)
            dst = d_adjbf[m].rearrange("p a b -> p (a b)")
            S.w("gp", V_cast)
            S.o("gp", lambda e, s=adjbuf[m % 2], d=dst: e.dma_start(out=d, in_=s[:]),
                inc=P_adjb[m % 2], dma=True)
            r_adjb.release(m, P_adjb[m % 2])
        for m in range(2 * NM):
            idx = 32 + m
            r_xld.acquire(S, "sy", idx)
            src = ap_w[m].rearrange("p a b -> p (a b)")
            S.o("sy", lambda e, s=src, d=x_ld[idx % 2]: e.dma_start(out=d[:], in_=s),
                inc=L_xld[idx % 2], dma=True)
            r_wib.acquire(S, "ve", m)
            S.w("ve", L_xld[idx % 2])
            S.o("ve", lambda e, sl=x_ld[idx % 2], d=wibuf[m % 2]:
                e.tensor_copy(d[:], sl[:]), inc=V_cast)
            r_xld.release(idx, V_cast)
            dst = d_wbf[m].rearrange("p a b -> p (a b)")
            S.w("gp", V_cast)
            S.o("gp", lambda e, s=wibuf[m % 2], d=dst: e.dma_start(out=d, in_=s[:]),
                inc=P_wib[m % 2], dma=True)
            r_wib.release(m, P_wib[m % 2])
        for kb in range(NM):
            idx = 64 + kb
            r_xld.acquire(S, "sy", idx)
            src = ap_op[kb * 128:(kb + 1) * 128, :]
            S.o("sy", lambda e, s=src, d=x_ld[idx % 2]: e.dma_start(out=d[:], in_=s),
                inc=L_xld[idx % 2], dma=True)
            r_wib.acquire(S, "ve", 32 + kb)
            S.w("ve", L_xld[idx % 2])
            S.o("ve", lambda e, sl=x_ld[idx % 2], d=wibuf[kb % 2]:
                e.tensor_copy(d[:], sl[:]), inc=V_cast)
            r_xld.release(idx, V_cast)
            dst = d_opbf[kb * 128:(kb + 1) * 128, :]
            S.w("gp", V_cast)
            S.o("gp", lambda e, s=wibuf[kb % 2], d=dst: e.dma_start(out=d, in_=s[:]),
                inc=P_wib[kb % 2], dma=True)
            r_wib.release(32 + kb, P_wib[kb % 2])
        for piece in range(2):
            idx = 80 + piece
            r_xld.acquire(S, "sy", idx)
            S.o("sy", lambda e, p=piece, d=x_ld[idx % 2]:
                e.dma_start(out=d[:, 0:768], in_=ap_xp[:, p * 768:(p + 1) * 768]),
                inc=L_xld[idx % 2], dma=True)
            S.w("ve", L_xld[idx % 2])
            S.o("ve", lambda e, p=piece, sl=x_ld[idx % 2]:
                e.tensor_copy(xpw_sb[:, p * 768:(p + 1) * 768], sl[:, 0:768]), inc=V_cast)
            r_xld.release(idx, V_cast)
        for piece in range(2):
            idx = 82 + piece
            r_xld.acquire(S, "sy", idx)
            S.o("sy", lambda e, p=piece, d=x_ld[idx % 2]:
                e.dma_start(out=d[0:DR, :], in_=ap_dtw[:, p * 1024:(p + 1) * 1024]),
                inc=L_xld[idx % 2], dma=True)
            r_wib.acquire(S, "ve", 48 + piece)
            S.w("ve", L_xld[idx % 2])
            S.o("ve", lambda e, p=piece, sl=x_ld[idx % 2]:
                e.tensor_copy(wibuf[p % 2][0:DR, 0:1024], sl[0:DR, :]), inc=V_cast)
            r_xld.release(idx, V_cast)
            S.w("gp", V_cast)
            S.o("gp", lambda e, p=piece: e.dma_start(
                out=d_dtw[:, p * 1024:(p + 1) * 1024], in_=wibuf[p % 2][0:DR, 0:1024]),
                inc=P_wib[piece % 2], dma=True)
            r_wib.release(48 + piece, P_wib[piece % 2])
        S.o("sy", lambda e: e.dma_start(out=a_sb[:], in_=ap_apm), inc=L_small, dma=True)
        S.o("sy", lambda e: e.dma_start(out=cw_sb[:], in_=ap_cw), inc=L_small, dma=True)
        S.o("sy", lambda e: e.dma_start(out=cb_sb[:], in_=ap_cb), inc=L_small, dma=True)
        S.o("sy", lambda e: e.dma_start(out=dtb_sb[:], in_=ap_dtb), inc=L_small, dma=True)
        S.o("sy", lambda e: e.dma_start(out=dd_sb[:], in_=ap_dd), inc=L_small, dma=True)
        S.w("sc", L_small, 96)
        S.o("sc", lambda e: e.activation(a_sb[:], a_sb[:], AF.Exp), inc=A_al)
        S.w("ve", A_al)
        S.o("ve", lambda e: e.tensor_scalar_mul(a_sb[:], a_sb[:], -1.0), inc=V_cast)
        S.o("ve", lambda e: e.memset(hcarry[:], 0.0), inc=V_cast)
        S.o("ve", lambda e: e.memset(halo_sb[:], 0.0), inc=V_cast)
        S.o("ve", lambda e: e.memset(cst[:, 0:1], EPS), inc=V_cast)
        S.o("ve", lambda e: e.memset(cst[:, 1:2], -0.5), inc=V_cast)
        S.o("ve", lambda e: e.memset(junk_v[:], 0.0))
        S.o("sc", lambda e: e.memzero(junk_s[:]))

        for ob in range(NKB):
            idx = 84 + ob
            r_xld.acquire(S, "sy", idx)
            S.o("sy", lambda e, o=ob, d=x_ld[idx % 2]:
                e.dma_start(out=d[:], in_=ap_fw[o * 128:(o + 1) * 128, :]),
                inc=L_xld[idx % 2], dma=True)
            S.w("ve", L_xld[idx % 2])
            xl = x_ld[idx % 2]
            for hh in range(2):
                r_cacc.acquire(S, "ve", hh)
                S.o("ve", lambda e, sl=xl, h=hh: e.tensor_scalar(
                    cacc[h][:], sl[:, h * 512:(h + 1) * 512], -1.0, None, op0=ALU.mult),
                    fence=True)
                S.o("ve", lambda e, sl=xl, h=hh: e.tensor_max(
                    cacc[h][:], cacc[h][:], sl[:, h * 512:(h + 1) * 512]), fence=True)
                S.o("ve", lambda e, h=hh: e.tensor_reduce(
                    stat[:, 3 + h:4 + h], cacc[h][:], axis=mybir.AxisListType.X,
                    op=ALU.add), fence=True)
            S.o("ve", lambda e: e.tensor_add(stat[:, 0:1], stat[:, 3:4], stat[:, 4:5]), fence=True)
            S.o("ve", lambda e: e.tensor_scalar_mul(stat[:, 0:1], stat[:, 0:1], 1.0 / DM), fence=True)
            S.o("ve", lambda e: e.tensor_scalar_max(stat[:, 0:1], stat[:, 0:1], 1e-5), fence=True)
            S.o("ve", lambda e: e.tensor_scalar_mul(stat[:, 14:15], stat[:, 0:1], 0.5), fence=True)
            S.o("ve", lambda e: e.tensor_scalar_mul(stat[:, 15:16], stat[:, 14:15], -1.0), fence=True)
            r_ternsc.acquire(S, "ve", 0)
            for hh in range(2):
                cs = slice(hh * 512, (hh + 1) * 512)
                r_cacc.acquire(S, "ve", hh)
                S.o("ve", lambda e, sl=xl, c=cs: e.tensor_scalar(
                    cacc[0][:], sl[:, c], stat[:, 14:15], None, op0=ALU.is_gt))
                S.o("ve", lambda e, sl=xl, c=cs: e.tensor_scalar(
                    cacc[1][:], sl[:, c], stat[:, 15:16], None, op0=ALU.is_lt), fence=True)
                S.o("ve", lambda e: e.tensor_sub(cacc[0][:], cacc[0][:], cacc[1][:]), fence=True)
                S.o("ve", lambda e, o=ob, c=cs: e.tensor_scalar(
                    xq_sb[:, c], cacc[0][:], stat[:, 0:1], None, op0=ALU.mult))
                S.fence("ve", inc=V_pt)
                r_cacc.release(hh, V_pt)
            r_xld.release(idx, V_pt)
            r_pstr.acquire(S, "te", 0); r_pstr.acquire(S, "te", 1)
            S.w("te", V_pt)
            S.w("te", L_small, 96)
            for ib in range(NKB):
                S.o("te", lambda e, i=ib: e.transpose(
                    ps_tr[i // 4][:, (i % 4) * 128:(i % 4 + 1) * 128],
                    xq_sb[:, i * 128:(i + 1) * 128],
                    eye_sb[:]), inc=(T_ptr if ib == NKB - 1 else None))
            r_ternsc.release(0, T_ptr)
            r_hbsc.acquire(S, "sc", 0)
            S.w("sc", T_ptr)
            S.o("sc", lambda e: e.copy(hbuf[:, 0:512], ps_tr[0][:, 0:512]))
            S.o("sc", lambda e: e.copy(hbuf[:, 512:1024], ps_tr[1][:, 0:512]), inc=A_pev)
            r_pstr.release(0, A_pev); r_pstr.release(1, A_pev)
            dst = d_ternbf.rearrange("(ib p) o -> p ib o", ib=NKB)[:, :, ob * 128:(ob + 1) * 128]
            S.w("gp", A_pev)
            S.o("gp", lambda e, d=dst: e.dma_start(out=d, in_=hbuf[:, 0:1024].rearrange(
                "p (ib c) -> p ib c", ib=NKB)), inc=L_hb, dma=True)
            r_hbsc.release(0, L_hb)
        S.w("sy", L_hb)
        S.o("sy", lambda e: e.dma_start(
            out=hbuf[:].rearrange("p (kb c) -> p kb c", kb=NKB),
            in_=ap_h.rearrange("(kb p) c -> p kb c", p=128)),
            inc=L_h, dma=True)
        r_xq.release(0, T_ptr)

        PRO_WR = [(P_wib[0], P_wib[0].n), (P_wib[1], P_wib[1].n),
                  (P_adjb[0], P_adjb[0].n), (P_adjb[1], P_adjb[1].n),
                  (L_hb, L_hb.n)]

        for ci in range(NCH):
            b = ci // (L // TC)
            cs = ci % (L // TC)
            t0 = cs * TC

            if cs == 0:
                S.w("ve", G_y, G_y.n)
                S.o("ve", lambda e: e.memset(hcarry[:], 0.0))
                S.o("ve", lambda e: e.memset(halo_sb[:], 0.0))
            for ti in range(NTI):
                gi = ci * NTI + ti
                r_xld.acquire(S, "sc", 100 + gi)
                src = ap_x[b, t0 + ti * 128: t0 + (ti + 1) * 128, :]
                S.o("sc", lambda e, s=src, d=x_ld[gi % 2]: e.dma_start(out=d[:], in_=s),
                    inc=L_xld[gi % 2], dma=True)
                xl = x_ld[gi % 2]
                S.w("ve", L_xld[gi % 2])
                S.o("ve", lambda e, sl=xl: e.reduce_sum(stat[:, 0:1], sl[:],
                                                        axis=mybir.AxisListType.X),
                    fence=True)
                r_cacc.acquire(S, "sc", 2 * gi)
                r_cacc.acquire(S, "sc", 2 * gi + 1)
                S.w("sc", L_xld[gi % 2])
                S.w("sc", V_xn)
                S.o("sc", lambda e, sl=xl: e.activation(
                    cacc[0][:], sl[:, 0:512], AF.Square, accum_out=stat[:, 1:2]))
                S.o("sc", lambda e, sl=xl: e.activation(
                    cacc[1][:], sl[:, 512:1024], AF.Square, accum_out=stat[:, 2:3]))
                S.fence("sc", inc=A_rstd)
                r_cacc.release(2 * gi, A_rstd)
                r_cacc.release(2 * gi + 1, A_rstd)
                S.w("ve", A_rstd)
                S.o("ve", lambda e: e.tensor_add(stat[:, 3:4], stat[:, 1:2], stat[:, 2:3]), fence=True)
                S.o("ve", lambda e: e.tensor_scalar_mul(stat[:, 4:5], stat[:, 0:1], 1.0 / DM), fence=True)
                S.o("ve", lambda e: e.tensor_scalar_mul(stat[:, 5:6], stat[:, 3:4], 1.0 / DM), fence=True)
                S.o("ve", lambda e: e.tensor_mul(stat[:, 6:7], stat[:, 4:5], stat[:, 4:5]), fence=True)
                S.o("ve", lambda e: e.tensor_sub(stat[:, 5:6], stat[:, 5:6], stat[:, 6:7]))
                S.fence("ve", inc=V_xn)
                S.w("sc", V_xn)
                S.o("sc", lambda e: e.activation(stat[:, 7:8], stat[:, 5:6], AF.Ln,
                                                 bias=cst[:, 0:1]), fence=True)
                S.o("sc", lambda e: e.activation(stat[:, 7:8], stat[:, 7:8], AF.Exp,
                                                 scale=cst[:, 1:2]))
                S.fence("sc", inc=A_rstd)
                r_xn.acquire(S, "ve", 0) if ti == 0 else None
                S.w("ve", A_rstd)
                if DBG and ci == 0 and ti == 0:
                    S.o("ve", lambda e: e.tensor_copy(ln_snap[:], stat[:]))
                S.o("ve", lambda e, sl=xl, t=ti: e.tensor_scalar(
                    xn_sb[:, t * DM:(t + 1) * DM], sl[:], stat[:, 4:5], stat[:, 7:8],
                    op0=ALU.subtract, op1=ALU.mult))
                S.fence("ve", inc=V_xn)
                r_xld.release(100 + gi, V_xn)

            if PH < 2:
                continue
            r_xnT.acquire(S, "te", 0)
            for kb in range(NKB):
                r_pstr.acquire(S, "te", kb)
                if kb == 0:
                    S.w("te", V_xn)
                for ti in range(NTI):
                    S.o("te", lambda e, t=ti, k=kb: e.transpose(
                        ps_tr[k % 2][:, t * 128:(t + 1) * 128],
                        xn_sb[:, t * DM + k * 128: t * DM + (k + 1) * 128], eye_sb[:]),
                        inc=(T_trn if ti == NTI - 1 else None))
                S.w("sc", T_trn)
                S.o("sc", lambda e, k=kb: e.copy(
                    xnT_sb[:, k * TC:(k + 1) * TC],
                    ps_tr[k % 2][:, 0:512]), inc=A_evxn)
                r_pstr.release(kb, A_evxn)
            r_xn.release(0, T_trn)

            if PH < 3:
                continue
            r_xq.acquire(S, "ve", 0)
            for ti in range(NTI):
                for half in range(2):
                    r_psa.acquire(S, "te", half)
                    if ti == 0 and half == 0:
                        S.w("te", L_h, 16)
                        S.w("te", A_evxn)
                    for kb in range(NKB):
                        S.o("te", lambda e, t=ti, h=half, k=kb: e.matmul(
                            ps_a[h][:],
                            xnT_sb[:, k * TC + t * 128: k * TC + (t + 1) * 128],
                            hbuf[:, k * DM + h * 512: k * DM + (h + 1) * 512],
                            start=(k == 0), stop=(k == NKB - 1)),
                            inc=(T_xh if kb == NKB - 1 else None))
                S.w("ve", T_xh)
                for half in range(2):
                    r_cacc.acquire(S, "ve", 40 + 8 * ci + 2 * ti + half)
                    S.o("ve", lambda e, h=half: e.tensor_copy(cacc[h][:], ps_a[h][:]),
                        fence=(half == 1))
                    r_psa.release(half, V_q, V_q.n + 1)
                S.o("ve", lambda e: e.tensor_reduce(
                    stat[:, 8:9], cacc[0][:], axis=mybir.AxisListType.X, op=ALU.max))
                S.o("ve", lambda e: e.tensor_reduce(
                    stat[:, 9:10], cacc[1][:], axis=mybir.AxisListType.X, op=ALU.max))
                S.o("ve", lambda e: e.tensor_reduce(
                    stat[:, 11:12], cacc[0][:], axis=mybir.AxisListType.X, op=ALU.min))
                S.o("ve", lambda e: e.tensor_reduce(
                    stat[:, 12:13], cacc[1][:], axis=mybir.AxisListType.X, op=ALU.min),
                    fence=True)
                S.o("ve", lambda e: e.tensor_max(stat[:, 8:9], stat[:, 8:9], stat[:, 9:10]), fence=True)
                S.o("ve", lambda e: e.tensor_tensor(stat[:, 11:12], stat[:, 11:12], stat[:, 12:13], op=ALU.min), fence=True)
                S.o("ve", lambda e: e.tensor_scalar(stat[:, 11:12], stat[:, 11:12], -1.0, None, op0=ALU.mult), fence=True)
                S.o("ve", lambda e: e.tensor_max(stat[:, 8:9], stat[:, 8:9], stat[:, 11:12]), fence=True)
                S.o("ve", lambda e: e.tensor_scalar_max(stat[:, 8:9], stat[:, 8:9], 1e-5), fence=True)
                S.o("ve", lambda e: e.tensor_scalar_mul(stat[:, 8:9], stat[:, 8:9], 1.0 / 7.0))
                S.fence("ve", inc=V_q)
                S.w("sc", V_q)
                S.o("sc", lambda e: e.activation(stat[:, 10:11], stat[:, 8:9], AF.Ln),
                    fence=True)
                S.o("sc", lambda e: e.activation(stat[:, 9:10], stat[:, 10:11], AF.Exp,
                                                 scale=-1.0))
                S.fence("sc", inc=A_rstd)
                S.w("ve", A_rstd)
                for half in range(2):
                    S.o("ve", lambda e, h=half: e.tensor_scalar(
                        cacc[h][:], cacc[h][:], stat[:, 9:10], RNE_C,
                        op0=ALU.mult, op1=ALU.add), inc=V_q, fence=True)
                    S.o("ve", lambda e, h=half, t=ti: e.tensor_scalar(
                        xq_sb[:, t * DM + h * 512: t * DM + (h + 1) * 512],
                        cacc[h][:], RNE_C, stat[:, 8:9],
                        op0=ALU.subtract, op1=ALU.mult))
                    S.fence("ve", inc=V_q)
                    r_cacc.release(40 + 8 * ci + 2 * ti + half, V_q)

            if PH < 4:
                continue
            r_xi.acquire(S, "sc", 0)
            r_yf.acquire(S, "sc", 0)
            for m in range(NM):
                r_wib.acquire(S, "sy", m)
                if ci == 0 and m == 0:
                    for (ps_sem, ps_t) in PRO_WR:
                        S.w("sy", ps_sem, ps_t)
                src = d_wbf[m].rearrange("p a b -> p (a b)")
                S.o("sy", lambda e, s=src, d=wibuf[m % 2]: e.dma_start(out=d[:], in_=s),
                    inc=L_wib[m % 2], dma=True)
                r_rwb.acquire(S, "sy", m)
                src2 = d_wbf[NM + m].rearrange("p a b -> p (a b)")
                S.o("sy", lambda e, s=src2, d=rwbuf[m % 2]: e.dma_start(out=d[:], in_=s),
                    inc=L_rwb[m % 2], dma=True)
                r_psip.acquire(S, "te", m)
                S.w("te", L_wib[m % 2])
                if m == 0:
                    S.w("te", A_evxn)
                for kb in range(NKB):
                    S.o("te", lambda e, k=kb, mm=m: e.matmul(
                        ps_ip[mm % 2][:], wibuf[mm % 2][:, k * 128:(k + 1) * 128],
                        xnT_sb[:, k * TC:(k + 1) * TC],
                        start=(k == 0), stop=(k == NKB - 1)),
                        inc=(T_ip if kb == NKB - 1 else None))
                r_wib.release(m, T_ip)
                r_psdp.acquire(S, "te", m)
                S.w("te", L_rwb[m % 2])
                for kb in range(NKB):
                    S.o("te", lambda e, k=kb, mm=m: e.matmul(
                        ps_dp[mm % 2][:], rwbuf[mm % 2][:, k * 128:(k + 1) * 128],
                        xnT_sb[:, k * TC:(k + 1) * TC],
                        start=(k == 0), stop=(k == NKB - 1)),
                        inc=(T_res if kb == NKB - 1 else None))
                r_rwb.release(m, T_res)
                r_xip.acquire(S, "sc", m)
                S.w("sc", T_ip)
                S.o("sc", lambda e, mm=m: e.copy(xip_sb[mm % 2][:, 3:3 + TC],
                                                 ps_ip[mm % 2][:]))
                S.fence("sc", inc=A_evxi)
                r_psip.release(m, A_evxi)
                S.w("sc", T_res)
                S.o("sc", lambda e, mm=m: e.activation(
                    yf_sb[:, mm * TC:(mm + 1) * TC], ps_dp[mm % 2][:], AF.Silu),
                    inc=A_res)
                r_psdp.release(m, A_res)
                r_cacc.acquire(S, "ve", m)
                S.w("ve", A_evxi)
                S.o("ve", lambda e, mm=m: e.tensor_copy(
                    xip_sb[mm % 2][:, 0:3],
                    halo_sb[:, mm * 3:(mm + 1) * 3]))
                S.o("ve", lambda e, mm=m: e.tensor_scalar(
                    cacc[mm % 2][:], xip_sb[mm % 2][:, 0:TC], cw_sb[:, mm * 4:mm * 4 + 1],
                    None, op0=ALU.mult))
                for k in range(1, 4):
                    S.o("ve", lambda e, mm=m, kk=k: e.scalar_tensor_tensor(
                        cacc[mm % 2][:], xip_sb[mm % 2][:, kk:kk + TC],
                        cw_sb[:, mm * 4 + kk:mm * 4 + kk + 1], cacc[mm % 2][:],
                        op0=ALU.mult, op1=ALU.add))
                S.o("ve", lambda e, mm=m: e.tensor_copy(
                    halo_sb[:, mm * 3:(mm + 1) * 3],
                    xip_sb[mm % 2][:, TC:TC + 3]))
                S.fence("ve", inc=V_conv)
                r_xip.release(m, V_conv)
                S.w("sc", V_conv)
                S.o("sc", lambda e, mm=m: e.activation(
                    xi_sb[:, mm * TC:(mm + 1) * TC], cacc[mm % 2][:], AF.Silu,
                    bias=cb_sb[:, mm:mm + 1]), inc=A_silu)
                r_cacc.release(m, A_silu)

            if PH < 5:
                continue
            r_xqT.acquire(S, "te", 0)
            for kb in range(NKB):
                r_pstr.acquire(S, "te", kb)
                if kb == 0:
                    S.w("te", V_q)
                for ti in range(NTI):
                    S.o("te", lambda e, t=ti, k=kb: e.transpose(
                        ps_tr[k % 2][:, t * 128:(t + 1) * 128],
                        xq_sb[:, t * DM + k * 128: t * DM + (k + 1) * 128], eye_sb[:]),
                        inc=(T_trq if ti == NTI - 1 else None))
                S.w("sc", T_trq)
                S.o("sc", lambda e, k=kb: e.copy(
                    xqT_sb[:, k * TC:(k + 1) * TC],
                    ps_tr[k % 2][:, 0:512]), inc=A_evxq)
                r_pstr.release(kb, A_evxq)
            r_xq.release(0, T_trq)

            r_psip.acquire(S, "te", 16)
            r_dinT.acquire(S, "sc", 0)
            S.w("te", A_silu)
            for m in range(NM):
                S.o("te", lambda e, mm=m: e.matmul(
                    ps_ip[0][0:96, :], xpw_sb[:, mm * 96:(mm + 1) * 96],
                    xi_sb[:, mm * TC:(mm + 1) * TC],
                    start=(mm == 0), stop=(mm == NM - 1)),
                    inc=(T_xdbl if m == NM - 1 else None))
            S.w("sc", T_xdbl)
            S.o("sc", lambda e: e.copy(dinT_sb[:], ps_ip[0][0:DR, :]))
            S.fence("sc", inc=A_dinT)
            r_bcs.acquire(S, "ve", 0)
            S.w("ve", T_xdbl)
            S.w("ve", A_dinT)
            S.o("ve", lambda e: e.tensor_copy(bc_sb[DR:96, :], ps_ip[0][DR:96, :]))
            S.fence("ve", inc=V_bc)
            r_psip.release(16, A_dinT)
            r_psip.slots[0].append((V_bc, V_bc.n))
            S.w("gp", V_bc)
            S.o("gp", lambda e: e.dma_start(out=d_bc, in_=bc_sb[DR:96, :]), inc=L_bcd, dma=True)
            r_bcs.release(0, L_bcd)
            r_rep.acquire(S, "gp", 0)
            S.w("gp", L_bcd)
            for g in range(2):
                for j in range(8):
                    S.o("gp", lambda e, gg=g, jj=j: e.dma_start(
                        out=brep[gg][:, jj * TC:(jj + 1) * TC],
                        in_=d_bc[gg * 8 + jj:gg * 8 + jj + 1, :].broadcast_to([128, TC])),
                        inc=L_rep, dma=True)
                    S.o("gp", lambda e, gg=g, jj=j: e.dma_start(
                        out=crep[gg][:, jj * TC:(jj + 1) * TC],
                        in_=d_bc[16 + gg * 8 + jj:16 + gg * 8 + jj + 1, :].broadcast_to([128, TC])),
                        inc=L_rep, dma=True)
            r_dT.acquire(S, "sc", 0)
            for m in range(NM):
                S.w("sy", T_dt, max(0, T_dt.n - 1))
                S.o("sy", lambda e, mm=m: e.dma_start(
                    out=dtbuf[mm % 2][:], in_=d_dtw[:, mm * 128:(mm + 1) * 128]),
                    inc=L_dtb[m % 2], dma=True)
                r_psip.acquire(S, "te", 17 + m)
                S.w("te", L_dtb[m % 2])
                if m == 0:
                    S.w("te", A_dinT)
                S.o("te", lambda e, mm=m: e.matmul(
                    ps_ip[(17 + mm) % 2][:], dtbuf[mm % 2][:],
                    dinT_sb[:], start=True, stop=True), inc=T_dt)
                r_cacc.acquire(S, "sc", 17 + m)
                S.w("sc", T_dt)
                S.o("sc", lambda e, mm=m: e.activation(
                    cacc[(17 + mm) % 2][:], ps_ip[(17 + mm) % 2][:], AF.Exp,
                    bias=dtb_sb[:, mm:mm + 1]), fence=True)
                r_psip.release(17 + m, A_sp, A_sp.n + 1)
                S.o("sc", lambda e, mm=m: e.activation(
                    deltaT_sb[:, mm * TC:(mm + 1) * TC], cacc[(17 + mm) % 2][:],
                    AF.Ln, bias=1.0), inc=A_sp)
                r_cacc.release(17 + m, A_sp)
            if PH < 6:
                continue
            vy0 = V_y.n

            def emit_F_loads(half):
                gi0 = ci * 8 + half * NTI
                v_base = V_oev.n
                tgt = []
                for ti in range(NTI):
                    gi2 = gi0 + ti
                    r_xld.acquire(S, "sc", 500 + gi2)
                    src = ap_x[b, t0 + ti * 128: t0 + (ti + 1) * 128,
                               half * 512:(half + 1) * 512]
                    S.o("sc", lambda e, s=src, d=x_ld[gi2 % 2]:
                        e.dma_start(out=d[:, 0:512], in_=s), inc=L_xld[gi2 % 2], dma=True)
                    tgt.append(L_xld[gi2 % 2].n)
                    r_xld.release(500 + gi2, V_oev, v_base + ti + 1)
                return tgt

            def emit_F_mms(half):
                slots = [ps_a[0], ps_a[1], ps_ip[0], ps_ip[1]]
                tout0 = T_out.n
                for kb in range(NM):
                    r_opb.acquire(S, "sy", kb)
                    src = d_opbf[kb * 128:(kb + 1) * 128, half * 512:(half + 1) * 512]
                    S.o("sy", lambda e, s=src, d=opbuf[kb % 2]: e.dma_start(out=d[:], in_=s),
                        inc=L_opb[kb % 2], dma=True)
                    S.w("te", L_opb[kb % 2])
                    S.w("te", V_y, vy0 + kb + 1)
                    if kb == 0:
                        for ti in range(NTI):
                            r_psa.acquire(S, "te", ti) if ti < 2 else \
                                r_psip.acquire(S, "te", 40 + ti)
                    for ti in range(NTI):
                        S.o("te", lambda e, k=kb, t=ti, h=half: e.matmul(
                            slots[t][:],
                            yf_sb[:, k * TC + t * 128: k * TC + (t + 1) * 128],
                            opbuf[k % 2][:], start=(k == 0), stop=False),
                            inc=(T_opk if ti == NTI - 1 else None))
                    r_opb.release(kb, T_opk)
                for ib in range(NKB):
                    r_ternb.acquire(S, "sy", ib)
                    src = d_ternbf[ib * 128:(ib + 1) * 128, half * 512:(half + 1) * 512]
                    S.o("sy", lambda e, s=src, d=ternbuf[ib % 2]: e.dma_start(out=d[:], in_=s),
                        inc=L_ternb[ib % 2], dma=True)
                    S.w("te", L_ternb[ib % 2])
                    if ib == 0:
                        S.w("te", A_evxq)
                    for ti in range(NTI):
                        S.o("te", lambda e, k=ib, t=ti: e.matmul(
                            slots[t][:],
                            xqT_sb[:, k * TC + t * 128: k * TC + (t + 1) * 128],
                            ternbuf[k % 2][:], start=False, stop=(k == NKB - 1)),
                            inc=((T_tek if ti == NTI - 1 else None) if ib < NKB - 1
                                 else T_out))
                    r_ternb.release(ib, T_tek if ib < NKB - 1 else T_out)
                return tout0

            def emit_F_evac(half, tout0, xre_tgt):
                slots = [ps_a[0], ps_a[1], ps_ip[0], ps_ip[1]]
                for ti in range(NTI):
                    gi2 = ci * 8 + half * NTI + ti
                    r_out.acquire(S, "ve", gi2)
                    S.w("ve", T_out, tout0 + ti + 1)
                    S.w("ve", L_xld[gi2 % 2], xre_tgt[ti])
                    S.o("ve", lambda e, t=ti, g2=gi2: e.tensor_add(
                        out_sb[g2 % 2][:], slots[t][:], x_ld[g2 % 2][:, 0:512]))
                    S.fence("ve", inc=V_oev)
                    if ti < 2:
                        r_psa.release(ti, V_oev)
                    else:
                        r_psip.slots[ti % 2].append((V_oev, V_oev.n))
                    dst = ap_out[b, t0 + ti * 128: t0 + (ti + 1) * 128,
                                 half * 512:(half + 1) * 512]
                    S.w("gp", V_oev)
                    S.o("gp", lambda e, d=dst, g2=gi2: e.dma_start(out=d, in_=out_sb[g2 % 2][:]),
                        inc=S_os[gi2 % 2], dma=True)
                    r_out.release(gi2, S_os[gi2 % 2])


            tdp_tgt = {}
            vsu_tgt = {}

            def emit_dp(m):
                r_adjb.acquire(S, "sy", 16 + m)
                if ci == 0 and m == 0:
                    for (ps_sem, ps_t) in PRO_WR:
                        S.w("sy", ps_sem, ps_t)
                src = d_adjbf[m].rearrange("p a b -> p (a b)")
                S.o("sy", lambda e, s=src, d=adjbuf[m % 2]: e.dma_start(out=d[:], in_=s),
                    inc=L_adjb[m % 2], dma=True)
                r_psdp.acquire(S, "te", 16 + m)
                S.w("te", L_adjb[m % 2])
                if m == 0:
                    S.w("te", A_sp)
                for kb in range(NM):
                    S.o("te", lambda e, k=kb, mm=m: e.matmul(
                        ps_dp[mm % 2][:], adjbuf[mm % 2][:, k * 128:(k + 1) * 128],
                        deltaT_sb[:, k * TC:(k + 1) * TC],
                        start=(k == 0), stop=(k == NM - 1)),
                        inc=(T_dp if kb == NM - 1 else None))
                tdp_tgt[m] = T_dp.n
                r_adjb.release(16 + m, T_dp)

            def emit_evac(m):
                r_cacc.acquire(S, "ve", 70 + 16 * ci + m)
                S.w("ve", T_dp, tdp_tgt[m])
                S.o("ve", lambda e, mm=m: e.tensor_copy(cacc[mm % 2][:], ps_dp[mm % 2][:]))
                S.fence("ve", inc=V_su)
                vsu_tgt[m] = V_su.n
                r_psdp.release(16 + m, V_su)

            albufs = [al_sb, xq_sb, xnT_sb]
            hhbufs = [hh_sb, xn_sb]
            gy_tgt = {}
            vscan_tgt = {}

            def e_su(m):
                r_su.acquire(S, "ve", m)
                S.o("ve", lambda e, mm=m: e.tensor_mul(
                    su_sb[mm % 2][:], cacc[mm % 2][:], xi_sb[:, mm * TC:(mm + 1) * TC]))

            def e_alpha(m, g):
                albuf = albufs[(2 * m + g) % 3]
                r_al.acquire(S, "sc", 2 * m + g)
                if ci == 0 and m == 0:
                    S.w("sc", T_trq)
                if g == 0:
                    S.w("sc", V_su, vsu_tgt[m])
                for j in range(8):
                    n = g * 8 + j
                    S.o("sc", lambda e, ab=albuf, mm=m, nn=n, jj=j: e.activation(
                        ab[:, jj * TC:(jj + 1) * TC], cacc[mm % 2][:], AF.Exp,
                        scale=a_sb[:, mm * NST + nn:mm * NST + nn + 1]),
                        inc=(A_al if j == 7 else None))
                if g == 1:
                    r_cacc.release(70 + 16 * ci + m, A_al)

            def e_w(m, g):
                if m == 0 and g == 0:
                    S.w("ve", L_rep)
                su_b = su_sb[m % 2][:].unsqueeze(1).broadcast_to([128, 8, TC])
                bv = brep[g][:].rearrange("p (j t) -> p j t", j=8)
                wvv = wv_sb[:].rearrange("p (j t) -> p j t", j=8)
                S.o("ve", lambda e, sb_=su_b, b_=bv, w_=wvv: e.tensor_tensor(
                    w_, sb_, b_, op=ALU.mult), inc=V_w)
                if g == 1:
                    r_su.release(m, V_w)

            def e_scan(m, g):
                albuf = albufs[(2 * m + g) % 3]
                hb = hhbufs[g]
                r_hh.acquire(S, "ve", g)
                if m == 0 and g == 1:
                    S.w("ve", T_trn)
                S.w("ve", A_al)
                for j in range(8):
                    n = g * 8 + j
                    S.o("ve", lambda e, ab=albuf, hb_=hb, mm=m, nn=n, jj=j:
                        e.tensor_tensor_scan(
                            hb_[:, jj * TC:(jj + 1) * TC],
                            ab[:, jj * TC:(jj + 1) * TC],
                            wv_sb[:, jj * TC:(jj + 1) * TC],
                            hcarry[:, mm * NST + nn:mm * NST + nn + 1],
                            op0=ALU.mult, op1=ALU.add))
                S.fence("ve", inc=V_scan)
                vscan_tgt[(m, g)] = V_scan.n

            def g_yterm(m, g):
                albuf = albufs[(2 * m + g) % 3]
                hb = hhbufs[g]
                S.w("gp", V_scan, vscan_tgt[(m, g)])
                if m == 0 and g == 0:
                    S.w("gp", L_rep)
                S.o("gp", lambda e, ab=albuf, hb_=hb, gg=g: e.tensor_tensor(
                    ab[:], hb_[:], crep[gg][:], op=ALU.mult))
                S.o("gp", lambda e, hb_=hb, mm=m, gg=g: e.tensor_copy(
                    hcarry[:, mm * NST + gg * 8: mm * NST + gg * 8 + 8],
                    hb_[:, TC - 1::TC]), inc=G_y)
                gy_tgt[(m, g)] = G_y.n
                r_hh.release(g, G_y)

            def e_tree(m, g):
                albuf = albufs[(2 * m + g) % 3]
                S.w("ve", G_y, gy_tgt[(m, g)])
                S.o("ve", lambda e, ab=albuf: e.tensor_add(
                    ab[:, 0:4 * TC], ab[:, 0:4 * TC], ab[:, 4 * TC:8 * TC]), fence=True)
                S.o("ve", lambda e, ab=albuf: e.tensor_add(
                    ab[:, 4 * TC:6 * TC], ab[:, 0:2 * TC], ab[:, 2 * TC:4 * TC]), fence=True)
                S.o("ve", lambda e, ab=albuf, gg=g: e.tensor_add(
                    yg_sb[:, gg * TC:(gg + 1) * TC], ab[:, 4 * TC:5 * TC],
                    ab[:, 5 * TC:6 * TC]))
                S.fence("ve", inc=V_tree)
                r_al.release(2 * m + g, V_tree)

            def e_final(m):
                S.o("ve", lambda e: e.tensor_add(
                    yg_sb[:, 0:TC], yg_sb[:, 0:TC], yg_sb[:, TC:2 * TC]), fence=True)
                S.o("ve", lambda e, mm=m: e.scalar_tensor_tensor(
                    t2_sb[:], xi_sb[:, mm * TC:(mm + 1) * TC], dd_sb[:, mm:mm + 1],
                    yg_sb[:, 0:TC], op0=ALU.mult, op1=ALU.add), fence=True)
                S.w("ve", A_res)
                S.o("ve", lambda e, mm=m: e.tensor_mul(
                    yf_sb[:, mm * TC:(mm + 1) * TC], yf_sb[:, mm * TC:(mm + 1) * TC],
                    t2_sb[:]))
                S.fence("ve", inc=V_y)

            def emit_E_pipelined(m):
                e_su(m)
                e_alpha(m, 0)
                e_w(m, 0)
                e_scan(m, 0)
                g_yterm(m, 0)
                if m >= 1:
                    e_tree(m - 1, 1)
                    e_final(m - 1)
                e_alpha(m, 1)
                e_w(m, 1)
                e_scan(m, 1)
                g_yterm(m, 1)
                e_tree(m, 0)

            S.w("ve", G_y, G_y.n)
            for m in range(NM):
                emit_dp(m)
                emit_evac(m)
                if m >= 1:
                    emit_E_pipelined(m - 1)
            if PH >= 7:
                tout_h0 = emit_F_mms(0)
            emit_E_pipelined(NM - 1)
            e_tree(NM - 1, 1)
            e_final(NM - 1)
            if PH >= 7:
                xre0 = emit_F_loads(0)

            if PH >= 7:
                emit_F_evac(0, tout_h0, xre0)
                xre1 = emit_F_loads(1)
                tout_h1 = emit_F_mms(1)
                emit_F_evac(1, tout_h1, xre1)

            r_rep.release(0, G_y)
            r_xi.release(0, V_y)
            r_dT.release(0, T_dp)
            r_dinT.release(0, T_dt)
            r_xnT.release(0, T_res)
            r_xnT.release(0, V_tree)
            r_xn.release(0, G_y)
            r_xqT.release(0, T_out)
            r_yf.release(0, T_out)
        if DBG:
            dbg_src = {"d_xn": xn_sb, "d_xnT": xnT_sb, "d_xi": xi_sb,
                       "d_dT": deltaT_sb, "d_yf": yf_sb, "d_xq": xq_sb,
                       "d_hc": hcarry, "d_brep": brep[0], "d_crep": crep[0],
                       "d_asb": a_sb, "d_su": su_sb[1], "d_al": al_sb,
                       "d_hh": hh_sb, "d_wv": wv_sb, "d_stat": stat,
                       "d_ln": ln_snap}
            dsem = S_("dsem"); dwr = S_("dwr")
            S.w("ve", S_os[0], S_os[0].n)
            S.w("ve", S_os[1], S_os[1].n)
            for nm, t in dbg_src.items():
                cols = t.ap().shape[1]
                for p0 in range(0, cols, 1024):
                    pc = min(1024, cols - p0)
                    S.w("ve", dwr, dwr.n)
                    S.o("ve", lambda e, tt=t, a=p0, c=pc: e.tensor_copy(
                        x_ld[0][:, 0:c], tt[:, a:a + c]), inc=dsem)
                    S.w("gp", dsem, dsem.n)
                    S.o("gp", lambda e, nm_=nm, a=p0, c=pc: e.dma_start(
                        out=dbg_aps[nm_][:, a:a + c], in_=x_ld[0][:, 0:c]),
                        inc=dwr, dma=True)
            S.w("gp", dwr, dwr.n)

        S.w("gp", S_os[0], S_os[0].n)
        S.w("gp", S_os[1], S_os[1].n)

        block = stack.enter_context(nc.Block())
        block.sync(lambda e: S.replay("sy", e))
        block.gpsimd(lambda e: S.replay("gp", e))
        block.vector(lambda e: S.replay("ve", e))
        block.scalar(lambda e: S.replay("sc", e))
        block.tensor(lambda e: S.replay("te", e))

    return nc


_CACHE = {}


def _hadamard(size):
    h = np.array([[1.0]], dtype=np.float64)
    while h.shape[0] < size:
        h = np.block([[h, h], [h, -h]])
    return (h / np.sqrt(size)).astype(np.float32)


def kernel(L0_attn_x, L0_attn_adj, L0_attn_in_proj_w, L0_attn_conv_w, L0_attn_conv_b,
           L0_attn_x_proj_w, L0_attn_dt_proj_w, L0_attn_dt_proj_b, L0_attn_A_log,
           L0_attn_D, L0_attn_out_proj_w, L0_ffn_weight):
    import ml_dtypes
    x = np.ascontiguousarray(np.asarray(L0_attn_x, dtype=np.float32))
    adj = np.asarray(L0_attn_adj, np.float32)
    W = np.asarray(L0_attn_in_proj_w, np.float32)
    adj_pm = np.ascontiguousarray(adj.reshape(16, 128, 16, 128).transpose(2, 1, 0, 3))
    w_pm = np.ascontiguousarray(W.reshape(8, 128, 32, 128).transpose(2, 1, 0, 3))
    op_w = np.ascontiguousarray(np.asarray(L0_attn_out_proj_w, np.float32))
    xp_pm = np.ascontiguousarray(
        np.asarray(L0_attn_x_proj_w, np.float32).reshape(16, 128, 96)
        .transpose(1, 0, 2).reshape(128, 16 * 96))
    dt_w = np.ascontiguousarray(np.asarray(L0_attn_dt_proj_w, np.float32))
    a_pm = np.ascontiguousarray(
        np.asarray(L0_attn_A_log, np.float32).reshape(16, 128, 16)
        .transpose(1, 0, 2).reshape(128, 256))
    convw_pm = np.ascontiguousarray(
        np.asarray(L0_attn_conv_w, np.float32)[:, 0, :].reshape(16, 128, 4)
        .transpose(1, 0, 2).reshape(128, 64))
    convb_pm = np.ascontiguousarray(np.asarray(L0_attn_conv_b, np.float32).reshape(16, 128).T)
    dtb_pm = np.ascontiguousarray(np.asarray(L0_attn_dt_proj_b, np.float32).reshape(16, 128).T)
    d_pm = np.ascontiguousarray(np.asarray(L0_attn_D, np.float32).reshape(16, 128).T)
    ffn_w = np.ascontiguousarray(np.asarray(L0_ffn_weight, np.float32))
    h_bf = np.ascontiguousarray(_hadamard(DM).astype(ml_dtypes.bfloat16))
    eye_bf = np.ascontiguousarray(np.eye(128, dtype=np.float32).astype(ml_dtypes.bfloat16))

    if "nc" not in _CACHE:
        _CACHE["nc"] = build_nc()
    nc = _CACHE["nc"]

    shared = dict(adj_pm=adj_pm, w_pm=w_pm, op_w=op_w, xp_pm=xp_pm, dt_w=dt_w,
                  a_pm=a_pm, convw_pm=convw_pm, convb_pm=convb_pm, dtb_pm=dtb_pm,
                  d_pm=d_pm, ffn_w=ffn_w, h_bf=h_bf, eye_bf=eye_bf)
    in_maps = []
    for c in range(NCORE):
        m = dict(shared)
        m["x"] = np.ascontiguousarray(x[c * BC:(c + 1) * BC])
        in_maps.append(m)
    res = run_bass_kernel_spmd(nc, in_maps, core_ids=list(range(NCORE)))
    out = np.concatenate([r["out"] for r in res.results], axis=0)
    return out.astype(np.float32)


# version 10
# speedup vs baseline: 1.1981x; 1.1981x over previous
import contextlib
import numpy as np
import concourse.bass as bass
import concourse.mybir as mybir
from concourse.bass_utils import run_bass_kernel_spmd

F32 = mybir.dt.float32
BF16 = mybir.dt.bfloat16
AF = mybir.ActivationFunctionType
ALU = mybir.AluOpType

NCORE = 8
BC = 2
L = 2048
DM = 1024
DI = 2048
NST = 16
DR = 64
TC = 512
NCH = BC * L // TC
NTI = TC // 128
NM = DI // 128
NKB = DM // 128
RNE_C = float(3 * 2.0 ** 22)
EPS = 1e-5

ENGS = ("sy", "gp", "ve", "sc", "te")

DEV_DRAIN = False
PH = 7
PHE = 6
ALV = 0
DBG = False


class Sem:
    def __init__(self, stack, nc, name):
        self.h = stack.enter_context(nc.semaphore(name))
        self.n = 0
        self.name = name


class Sched:

    def __init__(self):
        self.q = {e: [] for e in ENGS}

    def o(self, eng, fn, inc=None, dma=False, fence=False):
        incs = []
        if inc is not None:
            if not isinstance(inc, list):
                inc = [inc]
            for s in inc:
                k = 16 if dma else 1
                s.n += k
                incs.append((s, k))
        self.q[eng].append(("op", fn, incs))
        if fence:
            self.fence(eng)

    def fence(self, eng, inc=None):
        incs = []
        if inc is not None:
            if not isinstance(inc, list):
                inc = [inc]
            for s in inc:
                s.n += 1
                incs.append((s, 1))
        jb = self.junk[eng]
        if eng == "sc":
            self.q[eng].append(("op", lambda e: e.copy(jb[:, 0:1], jb[:, 2:3]), []))
            self.q[eng].append(("op", lambda e: e.copy(jb[:, 1:2], jb[:, 3:4]), incs))
        else:
            self.q[eng].append(("op", lambda e: e.tensor_copy(jb[:, 0:1], jb[:, 2:3]), []))
            self.q[eng].append(("op", lambda e: e.tensor_copy(jb[:, 1:2], jb[:, 3:4]), incs))

    def w(self, eng, sem, target=None):
        t = sem.n if target is None else target
        self.q[eng].append(("wait", sem, t))

    def replay(self, eng_name, eng):
        for item in self.q[eng_name]:
            if item[0] == "op":
                ins = item[1](eng)
                for (s, k) in item[2]:
                    ins.then_inc(s.h, k)
                if DEV_DRAIN and eng_name in ("ve", "sc", "te"):
                    eng.drain()
            else:
                eng.wait_ge(item[1].h, item[2])


class Ring:

    def __init__(self, n):
        self.n = n
        self.slots = [[] for _ in range(n)]

    def acquire(self, S, eng, idx):
        for (sem, t) in self.slots[idx % self.n]:
            S.w(eng, sem, t)
        self.slots[idx % self.n] = []

    def release(self, idx, sem, target=None):
        self.slots[idx % self.n].append((sem, sem.n if target is None else target))


def build_nc():
    nc = bass.Bass("TRN2", target_bir_lowering=False,
                   detect_race_conditions=DEV_DRAIN)

    ap_x = nc.dram_tensor("x", [BC, L, DM], F32, kind="ExternalInput").ap()
    ap_adj = nc.dram_tensor("adj_pm", [NM, 128, NM, 128], F32, kind="ExternalInput").ap()
    ap_w = nc.dram_tensor("w_pm", [2 * NM, 128, NKB, 128], F32, kind="ExternalInput").ap()
    ap_op = nc.dram_tensor("op_w", [DI, DM], F32, kind="ExternalInput").ap()
    ap_xp = nc.dram_tensor("xp_pm", [128, NM * 96], F32, kind="ExternalInput").ap()
    ap_dtw = nc.dram_tensor("dt_w", [DR, DI], F32, kind="ExternalInput").ap()
    ap_apm = nc.dram_tensor("a_pm", [128, NM * NST], F32, kind="ExternalInput").ap()
    ap_cw = nc.dram_tensor("convw_pm", [128, NM * 4], F32, kind="ExternalInput").ap()
    ap_cb = nc.dram_tensor("convb_pm", [128, NM], F32, kind="ExternalInput").ap()
    ap_dtb = nc.dram_tensor("dtb_pm", [128, NM], F32, kind="ExternalInput").ap()
    ap_dd = nc.dram_tensor("d_pm", [128, NM], F32, kind="ExternalInput").ap()
    ap_fw = nc.dram_tensor("ffn_w", [DM, DM], F32, kind="ExternalInput").ap()
    ap_h = nc.dram_tensor("h_bf", [DM, DM], BF16, kind="ExternalInput").ap()
    ap_eye = nc.dram_tensor("eye_bf", [128, 128], BF16, kind="ExternalInput").ap()

    ap_out = nc.dram_tensor("out", [BC, L, DM], F32, kind="ExternalOutput").ap()
    dbg_aps = {}
    if DBG:
        for nm, shape in [("d_xn", [128, NTI * DM]), ("d_xnT", [128, NKB * TC]),
                          ("d_xi", [128, NM * TC]), ("d_dT", [128, NM * TC]),
                          ("d_yf", [128, NM * TC]), ("d_xq", [128, NTI * DM]),
                          ("d_hc", [128, NM * NST]), ("d_brep", [128, 8 * TC]),
                          ("d_crep", [128, 8 * TC]), ("d_asb", [128, NM * NST]),
                          ("d_su", [128, TC]), ("d_al", [128, 8 * TC]),
                          ("d_hh", [128, 8 * TC]), ("d_wv", [128, 8 * TC]),
                          ("d_stat", [128, 16]), ("d_ln", [128, 16])]:
            dbg_aps[nm] = nc.dram_tensor(nm, shape, F32, kind="ExternalOutput").ap()
    d_adjbf = nc.dram_tensor("adj_bf", [NM, 128, NM, 128], BF16, kind="Internal").ap()
    d_wbf = nc.dram_tensor("w_bf", [2 * NM, 128, NKB, 128], BF16, kind="Internal").ap()
    d_opbf = nc.dram_tensor("op_bf", [DI, DM], BF16, kind="Internal").ap()
    d_ternbf = nc.dram_tensor("tern_bf", [DM, DM], BF16, kind="Internal").ap()
    d_dtw = nc.dram_tensor("dtw_bf", [DR, DI], BF16, kind="Internal").ap()
    d_bc = nc.dram_tensor("bc_scr", [32, TC], BF16, kind="Internal").ap()

    with contextlib.ExitStack() as stack:
        def sb(name, shape, dtype):
            return stack.enter_context(nc.sbuf_tensor(name, shape, dtype))

        def ps_(name, shape, dtype):
            return stack.enter_context(nc.psum_tensor(name, shape, dtype))

        eye_sb = sb("eye_sb", [128, 128], BF16)
        xpw_sb = sb("xpw_sb", [128, NM * 96], BF16)
        dtbuf = [sb(f"dtbuf{i}", [DR, 128], BF16) for i in range(2)]
        a_sb = sb("a_sb", [128, NM * NST], F32)
        cw_sb = sb("cw_sb", [128, NM * 4], F32)
        cb_sb = sb("cb_sb", [128, NM], F32)
        dtb_sb = sb("dtb_sb", [128, NM], F32)
        dd_sb = sb("dd_sb", [128, NM], F32)
        hcarry = sb("hcarry", [128, NM * NST], F32)
        halo_sb = sb("halo_sb", [128, NM * 3], F32)
        stat = sb("stat", [128, 16], F32)
        cst = sb("cst", [128, 2], F32)
        junk_v = sb("junk_v", [128, 4], F32)
        junk_s = sb("junk_s", [128, 4], F32)

        ln_snap = sb("ln_snap", [128, 16], F32)
        x_ld = [sb(f"x_ld{i}", [128, DM], F32) for i in range(2)]
        xn_sb = sb("xn_sb", [128, NTI * DM], BF16)
        xnT_sb = sb("xnT_sb", [128, NKB * TC], BF16)
        xq_sb = sb("xq_sb", [128, NTI * DM], BF16)
        xqT_sb = sb("xqT_sb", [128, NKB * TC], BF16)
        hbuf = sb("hbuf", [128, NKB * DM], BF16)
        wibuf = [sb(f"wibuf{i}", [128, NKB * 128], BF16) for i in range(2)]
        rwbuf = [sb(f"rwbuf{i}", [128, NKB * 128], BF16) for i in range(2)]
        adjbuf = [sb(f"adjbuf{i}", [128, DI], BF16) for i in range(2)]
        opbuf = [sb(f"opbuf{i}", [128, 512], BF16) for i in range(2)]
        ternbuf = [sb(f"ternbuf{i}", [128, 512], BF16) for i in range(2)]
        xip_sb = [sb(f"xip{i}", [128, TC + 3], BF16) for i in range(2)]
        cacc = [sb(f"cacc{i}", [128, TC], F32) for i in range(2)]
        xi_sb = sb("xi_sb", [128, NM * TC], BF16)
        dinT_sb = sb("dinT_sb", [DR, TC], BF16)
        deltaT_sb = sb("deltaT_sb", [128, NM * TC], BF16)
        su_sb = [sb(f"su{i}", [128, TC], BF16) for i in range(2)]
        brep = [sb(f"brep{g}", [128, 8 * TC], BF16) for g in range(2)]
        crep = [sb(f"crep{g}", [128, 8 * TC], BF16) for g in range(2)]
        al_sb = sb("al_sb", [128, 8 * TC], BF16)
        wv_sb = sb("wv_sb", [128, 8 * TC], BF16)
        hh_sb = sb("hh_sb", [128, 8 * TC], BF16)
        yg_sb = sb("yg_sb", [128, 2 * TC], BF16)
        t2_sb = sb("t2_sb", [128, TC], BF16)
        yf_sb = sb("yf_sb", [128, NM * TC], BF16)
        out_sb = [sb(f"out_sb{i}", [128, 512], F32) for i in range(2)]
        bc_sb = sb("bc_sb", [96, TC], BF16)

        ps_tr = [ps_(f"ps_tr{i}", [128, 1024], BF16) for i in range(2)]
        ps_a = [ps_(f"ps_a{i}", [128, 512], F32) for i in range(2)]
        ps_ip = [ps_(f"ps_ip{i}", [128, 512], F32) for i in range(2)]
        ps_dp = [ps_(f"ps_dp{i}", [128, 512], F32) for i in range(2)]

        sems = {}

        def S_(name):
            sems[name] = Sem(stack, nc, name)
            return sems[name]

        L_xld = [S_(f"L_xld{i}") for i in range(2)]
        L_wib = [S_(f"L_wib{i}") for i in range(2)]
        L_rwb = [S_(f"L_rwb{i}") for i in range(2)]
        L_adjb = [S_(f"L_adjb{i}") for i in range(2)]
        L_opb = [S_(f"L_opb{i}") for i in range(2)]
        L_ternb = [S_(f"L_ternb{i}") for i in range(2)]
        L_dtb = [S_(f"L_dtb{i}") for i in range(2)]
        P_wib = [S_(f"P_wib{i}") for i in range(2)]
        P_adjb = [S_(f"P_adjb{i}") for i in range(2)]
        L_hb = S_("L_hb"); L_small = S_("L_small"); L_h = S_("L_h")
        L_bcd = S_("L_bcd"); L_rep = S_("L_rep")
        S_os = [S_(f"S_os{i}") for i in range(2)]
        T_trn = S_("T_trn"); T_trq = S_("T_trq"); T_xh = S_("T_xh")
        T_ip = S_("T_ip"); T_res = S_("T_res"); T_xdbl = S_("T_xdbl")
        T_dt = S_("T_dt"); T_dp = S_("T_dp"); T_opk = S_("T_opk")
        T_tek = S_("T_tek"); T_out = S_("T_out"); T_ptr = S_("T_ptr")
        V_xn = S_("V_xn"); V_q = S_("V_q"); V_conv = S_("V_conv"); V_bc = S_("V_bc")
        V_su = S_("V_su"); V_w = S_("V_w"); V_scan = S_("V_scan"); V_y = S_("V_y")
        V_oev = S_("V_oev"); V_cast = S_("V_cast"); V_pt = S_("V_pt")
        A_rstd = S_("A_rstd"); A_evxn = S_("A_evxn"); A_evxq = S_("A_evxq")
        A_evxi = S_("A_evxi"); A_silu = S_("A_silu"); A_res = S_("A_res")
        A_dinT = S_("A_dinT"); A_sp = S_("A_sp"); A_al = S_("A_al"); A_pev = S_("A_pev")

        S = Sched()
        S.junk = {"ve": junk_v, "sc": junk_s}

        r_xld = Ring(2); r_wib = Ring(2); r_rwb = Ring(2)
        r_adjb = Ring(2); r_opb = Ring(2); r_ternb = Ring(2)
        r_xip = Ring(2); r_cacc = Ring(2); r_su = Ring(2); r_out = Ring(2)
        r_pstr = Ring(2)
        r_psa = Ring(2); r_psip = Ring(2); r_psdp = Ring(2)
        r_xn = Ring(1); r_xnT = Ring(1); r_xq = Ring(1); r_xqT = Ring(1)
        r_xi = Ring(1); r_dT = Ring(1); r_dinT = Ring(1); r_yf = Ring(1)
        r_al = Ring(2); r_rep = Ring(1); r_bcs = Ring(1)
        r_ternsc = Ring(1); r_hbsc = Ring(1)

        S.o("sy", lambda e: e.dma_start(out=eye_sb[:], in_=ap_eye), inc=L_small, dma=True)

        for m in range(NM):
            for piece in range(2):
                idx = m * 2 + piece
                r_xld.acquire(S, "sy", idx)
                src = ap_adj[m].rearrange("p a b -> p (a b)")[:, piece * 1024:(piece + 1) * 1024]
                S.o("sy", lambda e, s=src, d=x_ld[idx % 2]: e.dma_start(out=d[:], in_=s),
                    inc=L_xld[idx % 2], dma=True)
                r_adjb.acquire(S, "ve", m) if piece == 0 else None
                S.w("ve", L_xld[idx % 2])
                S.o("ve", lambda e, sl=x_ld[idx % 2], d=adjbuf[m % 2], p=piece:
                    e.tensor_copy(d[:, p * 1024:(p + 1) * 1024], sl[:]), inc=V_cast)
                r_xld.release(idx, V_cast)
            dst = d_adjbf[m].rearrange("p a b -> p (a b)")
            S.w("gp", V_cast)
            S.o("gp", lambda e, s=adjbuf[m % 2], d=dst: e.dma_start(out=d, in_=s[:]),
                inc=P_adjb[m % 2], dma=True)
            r_adjb.release(m, P_adjb[m % 2])
        for m in range(2 * NM):
            idx = 32 + m
            r_xld.acquire(S, "sy", idx)
            src = ap_w[m].rearrange("p a b -> p (a b)")
            S.o("sy", lambda e, s=src, d=x_ld[idx % 2]: e.dma_start(out=d[:], in_=s),
                inc=L_xld[idx % 2], dma=True)
            r_wib.acquire(S, "ve", m)
            S.w("ve", L_xld[idx % 2])
            S.o("ve", lambda e, sl=x_ld[idx % 2], d=wibuf[m % 2]:
                e.tensor_copy(d[:], sl[:]), inc=V_cast)
            r_xld.release(idx, V_cast)
            dst = d_wbf[m].rearrange("p a b -> p (a b)")
            S.w("gp", V_cast)
            S.o("gp", lambda e, s=wibuf[m % 2], d=dst: e.dma_start(out=d, in_=s[:]),
                inc=P_wib[m % 2], dma=True)
            r_wib.release(m, P_wib[m % 2])
        for kb in range(NM):
            idx = 64 + kb
            r_xld.acquire(S, "sy", idx)
            src = ap_op[kb * 128:(kb + 1) * 128, :]
            S.o("sy", lambda e, s=src, d=x_ld[idx % 2]: e.dma_start(out=d[:], in_=s),
                inc=L_xld[idx % 2], dma=True)
            r_wib.acquire(S, "ve", 32 + kb)
            S.w("ve", L_xld[idx % 2])
            S.o("ve", lambda e, sl=x_ld[idx % 2], d=wibuf[kb % 2]:
                e.tensor_copy(d[:], sl[:]), inc=V_cast)
            r_xld.release(idx, V_cast)
            dst = d_opbf[kb * 128:(kb + 1) * 128, :]
            S.w("gp", V_cast)
            S.o("gp", lambda e, s=wibuf[kb % 2], d=dst: e.dma_start(out=d, in_=s[:]),
                inc=P_wib[kb % 2], dma=True)
            r_wib.release(32 + kb, P_wib[kb % 2])
        for piece in range(2):
            idx = 80 + piece
            r_xld.acquire(S, "sy", idx)
            S.o("sy", lambda e, p=piece, d=x_ld[idx % 2]:
                e.dma_start(out=d[:, 0:768], in_=ap_xp[:, p * 768:(p + 1) * 768]),
                inc=L_xld[idx % 2], dma=True)
            S.w("ve", L_xld[idx % 2])
            S.o("ve", lambda e, p=piece, sl=x_ld[idx % 2]:
                e.tensor_copy(xpw_sb[:, p * 768:(p + 1) * 768], sl[:, 0:768]), inc=V_cast)
            r_xld.release(idx, V_cast)
        for piece in range(2):
            idx = 82 + piece
            r_xld.acquire(S, "sy", idx)
            S.o("sy", lambda e, p=piece, d=x_ld[idx % 2]:
                e.dma_start(out=d[0:DR, :], in_=ap_dtw[:, p * 1024:(p + 1) * 1024]),
                inc=L_xld[idx % 2], dma=True)
            r_wib.acquire(S, "ve", 48 + piece)
            S.w("ve", L_xld[idx % 2])
            S.o("ve", lambda e, p=piece, sl=x_ld[idx % 2]:
                e.tensor_copy(wibuf[p % 2][0:DR, 0:1024], sl[0:DR, :]), inc=V_cast)
            r_xld.release(idx, V_cast)
            S.w("gp", V_cast)
            S.o("gp", lambda e, p=piece: e.dma_start(
                out=d_dtw[:, p * 1024:(p + 1) * 1024], in_=wibuf[p % 2][0:DR, 0:1024]),
                inc=P_wib[piece % 2], dma=True)
            r_wib.release(48 + piece, P_wib[piece % 2])
        S.o("sy", lambda e: e.dma_start(out=a_sb[:], in_=ap_apm), inc=L_small, dma=True)
        S.o("sy", lambda e: e.dma_start(out=cw_sb[:], in_=ap_cw), inc=L_small, dma=True)
        S.o("sy", lambda e: e.dma_start(out=cb_sb[:], in_=ap_cb), inc=L_small, dma=True)
        S.o("sy", lambda e: e.dma_start(out=dtb_sb[:], in_=ap_dtb), inc=L_small, dma=True)
        S.o("sy", lambda e: e.dma_start(out=dd_sb[:], in_=ap_dd), inc=L_small, dma=True)
        S.w("sc", L_small, 96)
        S.o("sc", lambda e: e.activation(a_sb[:], a_sb[:], AF.Exp), inc=A_al)
        S.w("ve", A_al)
        S.o("ve", lambda e: e.tensor_scalar_mul(a_sb[:], a_sb[:], -1.0), inc=V_cast)
        S.o("ve", lambda e: e.memset(hcarry[:], 0.0), inc=V_cast)
        S.o("ve", lambda e: e.memset(halo_sb[:], 0.0), inc=V_cast)
        S.o("ve", lambda e: e.memset(cst[:, 0:1], EPS), inc=V_cast)
        S.o("ve", lambda e: e.memset(cst[:, 1:2], -0.5), inc=V_cast)
        S.o("ve", lambda e: e.memset(junk_v[:], 0.0))
        S.o("sc", lambda e: e.memzero(junk_s[:]))

        for ob in range(NKB):
            idx = 84 + ob
            r_xld.acquire(S, "sy", idx)
            S.o("sy", lambda e, o=ob, d=x_ld[idx % 2]:
                e.dma_start(out=d[:], in_=ap_fw[o * 128:(o + 1) * 128, :]),
                inc=L_xld[idx % 2], dma=True)
            S.w("ve", L_xld[idx % 2])
            xl = x_ld[idx % 2]
            for hh in range(2):
                r_cacc.acquire(S, "ve", hh)
                S.o("ve", lambda e, sl=xl, h=hh: e.tensor_scalar(
                    cacc[h][:], sl[:, h * 512:(h + 1) * 512], -1.0, None, op0=ALU.mult),
                    fence=True)
                S.o("ve", lambda e, sl=xl, h=hh: e.tensor_max(
                    cacc[h][:], cacc[h][:], sl[:, h * 512:(h + 1) * 512]), fence=True)
                S.o("ve", lambda e, h=hh: e.tensor_reduce(
                    stat[:, 3 + h:4 + h], cacc[h][:], axis=mybir.AxisListType.X,
                    op=ALU.add), fence=True)
            S.o("ve", lambda e: e.tensor_add(stat[:, 0:1], stat[:, 3:4], stat[:, 4:5]), fence=True)
            S.o("ve", lambda e: e.tensor_scalar_mul(stat[:, 0:1], stat[:, 0:1], 1.0 / DM), fence=True)
            S.o("ve", lambda e: e.tensor_scalar_max(stat[:, 0:1], stat[:, 0:1], 1e-5), fence=True)
            S.o("ve", lambda e: e.tensor_scalar_mul(stat[:, 14:15], stat[:, 0:1], 0.5), fence=True)
            S.o("ve", lambda e: e.tensor_scalar_mul(stat[:, 15:16], stat[:, 14:15], -1.0), fence=True)
            r_ternsc.acquire(S, "ve", 0)
            for hh in range(2):
                cs = slice(hh * 512, (hh + 1) * 512)
                r_cacc.acquire(S, "ve", hh)
                S.o("ve", lambda e, sl=xl, c=cs: e.tensor_scalar(
                    cacc[0][:], sl[:, c], stat[:, 14:15], None, op0=ALU.is_gt))
                S.o("ve", lambda e, sl=xl, c=cs: e.tensor_scalar(
                    cacc[1][:], sl[:, c], stat[:, 15:16], None, op0=ALU.is_lt), fence=True)
                S.o("ve", lambda e: e.tensor_sub(cacc[0][:], cacc[0][:], cacc[1][:]), fence=True)
                S.o("ve", lambda e, o=ob, c=cs: e.tensor_scalar(
                    xq_sb[:, c], cacc[0][:], stat[:, 0:1], None, op0=ALU.mult))
                S.fence("ve", inc=V_pt)
                r_cacc.release(hh, V_pt)
            r_xld.release(idx, V_pt)
            r_pstr.acquire(S, "te", 0); r_pstr.acquire(S, "te", 1)
            S.w("te", V_pt)
            S.w("te", L_small, 96)
            for ib in range(NKB):
                S.o("te", lambda e, i=ib: e.transpose(
                    ps_tr[i // 4][:, (i % 4) * 128:(i % 4 + 1) * 128],
                    xq_sb[:, i * 128:(i + 1) * 128],
                    eye_sb[:]), inc=(T_ptr if ib == NKB - 1 else None))
            r_ternsc.release(0, T_ptr)
            r_hbsc.acquire(S, "sc", 0)
            S.w("sc", T_ptr)
            S.o("sc", lambda e: e.copy(hbuf[:, 0:512], ps_tr[0][:, 0:512]))
            S.o("sc", lambda e: e.copy(hbuf[:, 512:1024], ps_tr[1][:, 0:512]), inc=A_pev)
            r_pstr.release(0, A_pev); r_pstr.release(1, A_pev)
            dst = d_ternbf.rearrange("(ib p) o -> p ib o", ib=NKB)[:, :, ob * 128:(ob + 1) * 128]
            S.w("gp", A_pev)
            S.o("gp", lambda e, d=dst: e.dma_start(out=d, in_=hbuf[:, 0:1024].rearrange(
                "p (ib c) -> p ib c", ib=NKB)), inc=L_hb, dma=True)
            r_hbsc.release(0, L_hb)
        S.w("sy", L_hb)
        S.o("sy", lambda e: e.dma_start(
            out=hbuf[:].rearrange("p (kb c) -> p kb c", kb=NKB),
            in_=ap_h.rearrange("(kb p) c -> p kb c", p=128)),
            inc=L_h, dma=True)
        r_xq.release(0, T_ptr)

        PRO_WR = [(P_wib[0], P_wib[0].n), (P_wib[1], P_wib[1].n),
                  (P_adjb[0], P_adjb[0].n), (P_adjb[1], P_adjb[1].n),
                  (L_hb, L_hb.n)]

        for ci in range(NCH):
            b = ci // (L // TC)
            cs = ci % (L // TC)
            t0 = cs * TC

            if cs == 0:
                S.o("ve", lambda e: e.memset(hcarry[:], 0.0))
                S.o("ve", lambda e: e.memset(halo_sb[:], 0.0))
            def emit_xload(ti):
                gi = ci * NTI + ti
                r_xld.acquire(S, "sc", 100 + gi)
                src = ap_x[b, t0 + ti * 128: t0 + (ti + 1) * 128, :]
                S.o("sc", lambda e, s=src, d=x_ld[gi % 2]: e.dma_start(out=d[:], in_=s),
                    inc=L_xld[gi % 2], dma=True)

            emit_xload(0)
            emit_xload(1)
            vxn_tgt_a = []
            for ti in range(NTI):
                gi = ci * NTI + ti
                xl = x_ld[gi % 2]
                S.w("ve", L_xld[gi % 2])
                S.o("ve", lambda e, sl=xl: e.reduce_sum(stat[:, 0:1], sl[:],
                                                        axis=mybir.AxisListType.X),
                    fence=True)
                r_cacc.acquire(S, "sc", 2 * gi)
                r_cacc.acquire(S, "sc", 2 * gi + 1)
                S.w("sc", L_xld[gi % 2])
                S.w("sc", V_xn)
                S.o("sc", lambda e, sl=xl: e.activation(
                    cacc[0][:], sl[:, 0:512], AF.Square, accum_out=stat[:, 1:2]))
                S.o("sc", lambda e, sl=xl: e.activation(
                    cacc[1][:], sl[:, 512:1024], AF.Square, accum_out=stat[:, 2:3]))
                S.fence("sc", inc=A_rstd)
                r_cacc.release(2 * gi, A_rstd)
                r_cacc.release(2 * gi + 1, A_rstd)
                S.w("ve", A_rstd)
                S.o("ve", lambda e: e.tensor_add(stat[:, 3:4], stat[:, 1:2], stat[:, 2:3]), fence=True)
                S.o("ve", lambda e: e.tensor_scalar_mul(stat[:, 4:5], stat[:, 0:1], 1.0 / DM), fence=True)
                S.o("ve", lambda e: e.tensor_scalar_mul(stat[:, 5:6], stat[:, 3:4], 1.0 / DM), fence=True)
                S.o("ve", lambda e: e.tensor_mul(stat[:, 6:7], stat[:, 4:5], stat[:, 4:5]), fence=True)
                S.o("ve", lambda e: e.tensor_sub(stat[:, 5:6], stat[:, 5:6], stat[:, 6:7]))
                S.fence("ve", inc=V_xn)
                S.w("sc", V_xn)
                S.o("sc", lambda e: e.activation(stat[:, 7:8], stat[:, 5:6], AF.Ln,
                                                 bias=cst[:, 0:1]), fence=True)
                S.o("sc", lambda e: e.activation(stat[:, 7:8], stat[:, 7:8], AF.Exp,
                                                 scale=cst[:, 1:2]))
                S.fence("sc", inc=A_rstd)
                r_xn.acquire(S, "ve", 0) if ti == 0 else None
                S.w("ve", A_rstd)
                if DBG and ci == 0 and ti == 0:
                    S.o("ve", lambda e: e.tensor_copy(ln_snap[:], stat[:]))
                S.o("ve", lambda e, sl=xl, t=ti: e.tensor_scalar(
                    xn_sb[:, t * DM:(t + 1) * DM], sl[:], stat[:, 4:5], stat[:, 7:8],
                    op0=ALU.subtract, op1=ALU.mult))
                S.fence("ve", inc=V_xn)
                vxn_tgt_a.append(V_xn.n)
                r_xld.release(100 + gi, V_xn)
                if ti + 2 < NTI:
                    emit_xload(ti + 2)

            if PH < 2:
                continue
            r_xnT.acquire(S, "te", 0)
            for kb in range(NKB):
                r_pstr.acquire(S, "te", kb)
                for ti in range(NTI):
                    if kb == 0:
                        S.w("te", V_xn, vxn_tgt_a[ti])
                    S.o("te", lambda e, t=ti, k=kb: e.transpose(
                        ps_tr[k % 2][:, t * 128:(t + 1) * 128],
                        xn_sb[:, t * DM + k * 128: t * DM + (k + 1) * 128], eye_sb[:]),
                        inc=(T_trn if ti == NTI - 1 else None))
                S.w("sc", T_trn)
                S.o("sc", lambda e, k=kb: e.copy(
                    xnT_sb[:, k * TC:(k + 1) * TC],
                    ps_tr[k % 2][:, 0:512]), inc=A_evxn)
                r_pstr.release(kb, A_evxn)
            r_xn.release(0, T_trn)

            if PH < 3:
                continue
            r_xq.acquire(S, "ve", 0)
            for ti in range(NTI):
                for half in range(2):
                    r_psa.acquire(S, "te", half)
                    if ti == 0 and half == 0:
                        S.w("te", L_h, 16)
                        S.w("te", A_evxn)
                    for kb in range(NKB):
                        S.o("te", lambda e, t=ti, h=half, k=kb: e.matmul(
                            ps_a[h][:],
                            xnT_sb[:, k * TC + t * 128: k * TC + (t + 1) * 128],
                            hbuf[:, k * DM + h * 512: k * DM + (h + 1) * 512],
                            start=(k == 0), stop=(k == NKB - 1)),
                            inc=(T_xh if kb == NKB - 1 else None))
                S.w("ve", T_xh)
                for half in range(2):
                    r_cacc.acquire(S, "ve", 40 + 8 * ci + 2 * ti + half)
                    S.o("ve", lambda e, h=half: e.tensor_copy(cacc[h][:], ps_a[h][:]),
                        fence=(half == 1))
                    r_psa.release(half, V_q, V_q.n + 1)
                S.o("ve", lambda e: e.tensor_reduce(
                    stat[:, 8:9], cacc[0][:], axis=mybir.AxisListType.X, op=ALU.max))
                S.o("ve", lambda e: e.tensor_reduce(
                    stat[:, 9:10], cacc[1][:], axis=mybir.AxisListType.X, op=ALU.max))
                S.o("ve", lambda e: e.tensor_reduce(
                    stat[:, 11:12], cacc[0][:], axis=mybir.AxisListType.X, op=ALU.min))
                S.o("ve", lambda e: e.tensor_reduce(
                    stat[:, 12:13], cacc[1][:], axis=mybir.AxisListType.X, op=ALU.min),
                    fence=True)
                S.o("ve", lambda e: e.tensor_max(stat[:, 8:9], stat[:, 8:9], stat[:, 9:10]), fence=True)
                S.o("ve", lambda e: e.tensor_tensor(stat[:, 11:12], stat[:, 11:12], stat[:, 12:13], op=ALU.min), fence=True)
                S.o("ve", lambda e: e.tensor_scalar(stat[:, 11:12], stat[:, 11:12], -1.0, None, op0=ALU.mult), fence=True)
                S.o("ve", lambda e: e.tensor_max(stat[:, 8:9], stat[:, 8:9], stat[:, 11:12]), fence=True)
                S.o("ve", lambda e: e.tensor_scalar_max(stat[:, 8:9], stat[:, 8:9], 1e-5), fence=True)
                S.o("ve", lambda e: e.tensor_scalar_mul(stat[:, 8:9], stat[:, 8:9], 1.0 / 7.0))
                S.fence("ve", inc=V_q)
                S.w("sc", V_q)
                S.o("sc", lambda e: e.activation(stat[:, 10:11], stat[:, 8:9], AF.Ln),
                    fence=True)
                S.o("sc", lambda e: e.activation(stat[:, 9:10], stat[:, 10:11], AF.Exp,
                                                 scale=-1.0))
                S.fence("sc", inc=A_rstd)
                S.w("ve", A_rstd)
                for half in range(2):
                    S.o("ve", lambda e, h=half: e.tensor_scalar(
                        cacc[h][:], cacc[h][:], stat[:, 9:10], RNE_C,
                        op0=ALU.mult, op1=ALU.add), inc=V_q, fence=True)
                    S.o("ve", lambda e, h=half, t=ti: e.tensor_scalar(
                        xq_sb[:, t * DM + h * 512: t * DM + (h + 1) * 512],
                        cacc[h][:], RNE_C, stat[:, 8:9],
                        op0=ALU.subtract, op1=ALU.mult))
                    S.fence("ve", inc=V_q)
                    r_cacc.release(40 + 8 * ci + 2 * ti + half, V_q)

            if PH < 4:
                continue
            r_xi.acquire(S, "sc", 0)
            r_yf.acquire(S, "sc", 0)
            def emit_ip(m):
                r_wib.acquire(S, "sy", m)
                if ci == 0 and m == 0:
                    for (ps_sem, ps_t) in PRO_WR:
                        S.w("sy", ps_sem, ps_t)
                src = d_wbf[m].rearrange("p a b -> p (a b)")
                S.o("sy", lambda e, s=src, d=wibuf[m % 2]: e.dma_start(out=d[:], in_=s),
                    inc=L_wib[m % 2], dma=True)
                r_psip.acquire(S, "te", m)
                S.w("te", L_wib[m % 2])
                if m == 0:
                    S.w("te", A_evxn)
                for kb in range(NKB):
                    S.o("te", lambda e, k=kb, mm=m: e.matmul(
                        ps_ip[mm % 2][:], wibuf[mm % 2][:, k * 128:(k + 1) * 128],
                        xnT_sb[:, k * TC:(k + 1) * TC],
                        start=(k == 0), stop=(k == NKB - 1)),
                        inc=(T_ip if kb == NKB - 1 else None))
                r_wib.release(m, T_ip)

            def emit_res(m):
                r_rwb.acquire(S, "sy", m)
                src2 = d_wbf[NM + m].rearrange("p a b -> p (a b)")
                S.o("sy", lambda e, s=src2, d=rwbuf[m % 2]: e.dma_start(out=d[:], in_=s),
                    inc=L_rwb[m % 2], dma=True)
                r_psdp.acquire(S, "te", m)
                S.w("te", L_rwb[m % 2])
                for kb in range(NKB):
                    S.o("te", lambda e, k=kb, mm=m: e.matmul(
                        ps_dp[mm % 2][:], rwbuf[mm % 2][:, k * 128:(k + 1) * 128],
                        xnT_sb[:, k * TC:(k + 1) * TC],
                        start=(k == 0), stop=(k == NKB - 1)),
                        inc=(T_res if kb == NKB - 1 else None))
                r_rwb.release(m, T_res)

            def emit_conv(m):
                r_xip.acquire(S, "sc", m)
                S.w("sc", T_ip)
                S.o("sc", lambda e, mm=m: e.copy(xip_sb[mm % 2][:, 3:3 + TC],
                                                 ps_ip[mm % 2][:]))
                S.fence("sc", inc=A_evxi)
                r_psip.release(m, A_evxi)
                r_cacc.acquire(S, "ve", m)
                S.w("ve", A_evxi)
                S.o("ve", lambda e, mm=m: e.tensor_copy(
                    xip_sb[mm % 2][:, 0:3],
                    halo_sb[:, mm * 3:(mm + 1) * 3]))
                S.o("ve", lambda e, mm=m: e.tensor_scalar(
                    cacc[mm % 2][:], xip_sb[mm % 2][:, 0:TC], cw_sb[:, mm * 4:mm * 4 + 1],
                    None, op0=ALU.mult))
                for k in range(1, 4):
                    S.o("ve", lambda e, mm=m, kk=k: e.scalar_tensor_tensor(
                        cacc[mm % 2][:], xip_sb[mm % 2][:, kk:kk + TC],
                        cw_sb[:, mm * 4 + kk:mm * 4 + kk + 1], cacc[mm % 2][:],
                        op0=ALU.mult, op1=ALU.add))
                S.o("ve", lambda e, mm=m: e.tensor_copy(
                    halo_sb[:, mm * 3:(mm + 1) * 3],
                    xip_sb[mm % 2][:, TC:TC + 3]))
                S.fence("ve", inc=V_conv)
                r_xip.release(m, V_conv)
                S.w("sc", V_conv)
                S.o("sc", lambda e, mm=m: e.activation(
                    xi_sb[:, mm * TC:(mm + 1) * TC], cacc[mm % 2][:], AF.Silu,
                    bias=cb_sb[:, mm:mm + 1]), inc=A_silu)
                r_cacc.release(m, A_silu)

            def emit_silures(m):
                S.w("sc", T_res)
                S.o("sc", lambda e, mm=m: e.activation(
                    yf_sb[:, mm * TC:(mm + 1) * TC], ps_dp[mm % 2][:], AF.Silu),
                    inc=A_res)
                r_psdp.release(m, A_res)

            for m in range(NM):
                emit_ip(m)
                if m >= 1:
                    emit_res(m - 1)
                emit_conv(m)
                if m >= 1:
                    emit_silures(m - 1)
            emit_res(NM - 1)
            emit_silures(NM - 1)

            if PH < 5:
                continue
            r_xqT.acquire(S, "te", 0)
            for kb in range(NKB):
                r_pstr.acquire(S, "te", kb)
                if kb == 0:
                    S.w("te", V_q)
                for ti in range(NTI):
                    S.o("te", lambda e, t=ti, k=kb: e.transpose(
                        ps_tr[k % 2][:, t * 128:(t + 1) * 128],
                        xq_sb[:, t * DM + k * 128: t * DM + (k + 1) * 128], eye_sb[:]),
                        inc=(T_trq if ti == NTI - 1 else None))
                S.w("sc", T_trq)
                S.o("sc", lambda e, k=kb: e.copy(
                    xqT_sb[:, k * TC:(k + 1) * TC],
                    ps_tr[k % 2][:, 0:512]), inc=A_evxq)
                r_pstr.release(kb, A_evxq)
            r_xq.release(0, T_trq)

            r_psip.acquire(S, "te", 16)
            r_dinT.acquire(S, "sc", 0)
            S.w("te", A_silu)
            for m in range(NM):
                S.o("te", lambda e, mm=m: e.matmul(
                    ps_ip[0][0:96, :], xpw_sb[:, mm * 96:(mm + 1) * 96],
                    xi_sb[:, mm * TC:(mm + 1) * TC],
                    start=(mm == 0), stop=(mm == NM - 1)),
                    inc=(T_xdbl if m == NM - 1 else None))
            S.w("sc", T_xdbl)
            S.o("sc", lambda e: e.copy(dinT_sb[:], ps_ip[0][0:DR, :]))
            S.fence("sc", inc=A_dinT)
            r_bcs.acquire(S, "ve", 0)
            S.w("ve", T_xdbl)
            S.w("ve", A_dinT)
            S.o("ve", lambda e: e.tensor_copy(bc_sb[DR:96, :], ps_ip[0][DR:96, :]))
            S.fence("ve", inc=V_bc)
            r_psip.release(16, A_dinT)
            r_psip.slots[0].append((V_bc, V_bc.n))
            S.w("gp", V_bc)
            S.o("gp", lambda e: e.dma_start(out=d_bc, in_=bc_sb[DR:96, :]), inc=L_bcd, dma=True)
            r_bcs.release(0, L_bcd)
            r_rep.acquire(S, "gp", 0)
            S.w("gp", L_bcd)
            for g in range(2):
                for j in range(8):
                    S.o("gp", lambda e, gg=g, jj=j: e.dma_start(
                        out=brep[gg][:, jj * TC:(jj + 1) * TC],
                        in_=d_bc[gg * 8 + jj:gg * 8 + jj + 1, :].broadcast_to([128, TC])),
                        inc=L_rep, dma=True)
                    S.o("gp", lambda e, gg=g, jj=j: e.dma_start(
                        out=crep[gg][:, jj * TC:(jj + 1) * TC],
                        in_=d_bc[16 + gg * 8 + jj:16 + gg * 8 + jj + 1, :].broadcast_to([128, TC])),
                        inc=L_rep, dma=True)
            r_dT.acquire(S, "sc", 0)
            for m in range(NM):
                S.w("sy", T_dt, max(0, T_dt.n - 1))
                S.o("sy", lambda e, mm=m: e.dma_start(
                    out=dtbuf[mm % 2][:], in_=d_dtw[:, mm * 128:(mm + 1) * 128]),
                    inc=L_dtb[m % 2], dma=True)
                r_psip.acquire(S, "te", 17 + m)
                S.w("te", L_dtb[m % 2])
                if m == 0:
                    S.w("te", A_dinT)
                S.o("te", lambda e, mm=m: e.matmul(
                    ps_ip[(17 + mm) % 2][:], dtbuf[mm % 2][:],
                    dinT_sb[:], start=True, stop=True), inc=T_dt)
                r_cacc.acquire(S, "sc", 17 + m)
                S.w("sc", T_dt)
                S.o("sc", lambda e, mm=m: e.activation(
                    cacc[(17 + mm) % 2][:], ps_ip[(17 + mm) % 2][:], AF.Exp,
                    bias=dtb_sb[:, mm:mm + 1]), fence=True)
                r_psip.release(17 + m, A_sp, A_sp.n + 1)
                S.o("sc", lambda e, mm=m: e.activation(
                    deltaT_sb[:, mm * TC:(mm + 1) * TC], cacc[(17 + mm) % 2][:],
                    AF.Ln, bias=1.0), inc=A_sp)
                r_cacc.release(17 + m, A_sp)
            if PH < 6:
                continue
            vy0 = V_y.n

            def emit_F_loads(half):
                gi0 = ci * 8 + half * NTI
                v_base = V_oev.n
                tgt = []
                for ti in range(NTI):
                    gi2 = gi0 + ti
                    r_xld.acquire(S, "sc", 500 + gi2)
                    src = ap_x[b, t0 + ti * 128: t0 + (ti + 1) * 128,
                               half * 512:(half + 1) * 512]
                    S.o("sc", lambda e, s=src, d=x_ld[gi2 % 2]:
                        e.dma_start(out=d[:, 0:512], in_=s), inc=L_xld[gi2 % 2], dma=True)
                    tgt.append(L_xld[gi2 % 2].n)
                    r_xld.release(500 + gi2, V_oev, v_base + ti + 1)
                return tgt

            def emit_F_mms(half):
                slots = [ps_a[0], ps_a[1], ps_ip[0], ps_ip[1]]
                tout0 = T_out.n
                for kb in range(NM):
                    r_opb.acquire(S, "sy", kb)
                    src = d_opbf[kb * 128:(kb + 1) * 128, half * 512:(half + 1) * 512]
                    S.o("sy", lambda e, s=src, d=opbuf[kb % 2]: e.dma_start(out=d[:], in_=s),
                        inc=L_opb[kb % 2], dma=True)
                    S.w("te", L_opb[kb % 2])
                    S.w("te", V_y, vy0 + kb + 1)
                    if kb == 0:
                        for ti in range(NTI):
                            r_psa.acquire(S, "te", ti) if ti < 2 else \
                                r_psip.acquire(S, "te", 40 + ti)
                    for ti in range(NTI):
                        S.o("te", lambda e, k=kb, t=ti, h=half: e.matmul(
                            slots[t][:],
                            yf_sb[:, k * TC + t * 128: k * TC + (t + 1) * 128],
                            opbuf[k % 2][:], start=(k == 0), stop=False),
                            inc=(T_opk if ti == NTI - 1 else None))
                    r_opb.release(kb, T_opk)
                for ib in range(NKB):
                    r_ternb.acquire(S, "sy", ib)
                    src = d_ternbf[ib * 128:(ib + 1) * 128, half * 512:(half + 1) * 512]
                    S.o("sy", lambda e, s=src, d=ternbuf[ib % 2]: e.dma_start(out=d[:], in_=s),
                        inc=L_ternb[ib % 2], dma=True)
                    S.w("te", L_ternb[ib % 2])
                    if ib == 0:
                        S.w("te", A_evxq)
                    for ti in range(NTI):
                        S.o("te", lambda e, k=ib, t=ti: e.matmul(
                            slots[t][:],
                            xqT_sb[:, k * TC + t * 128: k * TC + (t + 1) * 128],
                            ternbuf[k % 2][:], start=False, stop=(k == NKB - 1)),
                            inc=((T_tek if ti == NTI - 1 else None) if ib < NKB - 1
                                 else T_out))
                    r_ternb.release(ib, T_tek if ib < NKB - 1 else T_out)
                return tout0

            def emit_F_evac(half, tout0, xre_tgt):
                slots = [ps_a[0], ps_a[1], ps_ip[0], ps_ip[1]]
                for ti in range(NTI):
                    gi2 = ci * 8 + half * NTI + ti
                    r_out.acquire(S, "ve", gi2)
                    S.w("ve", T_out, tout0 + ti + 1)
                    S.w("ve", L_xld[gi2 % 2], xre_tgt[ti])
                    S.o("ve", lambda e, t=ti, g2=gi2: e.tensor_add(
                        out_sb[g2 % 2][:], slots[t][:], x_ld[g2 % 2][:, 0:512]))
                    S.fence("ve", inc=V_oev)
                    if ti < 2:
                        r_psa.release(ti, V_oev)
                    else:
                        r_psip.slots[ti % 2].append((V_oev, V_oev.n))
                    dst = ap_out[b, t0 + ti * 128: t0 + (ti + 1) * 128,
                                 half * 512:(half + 1) * 512]
                    S.w("gp", V_oev)
                    S.o("gp", lambda e, d=dst, g2=gi2: e.dma_start(out=d, in_=out_sb[g2 % 2][:]),
                        inc=S_os[gi2 % 2], dma=True)
                    r_out.release(gi2, S_os[gi2 % 2])


            tdp_tgt = {}
            vsu_tgt = {}

            def emit_dp(m):
                r_adjb.acquire(S, "sy", 16 + m)
                if ci == 0 and m == 0:
                    for (ps_sem, ps_t) in PRO_WR:
                        S.w("sy", ps_sem, ps_t)
                src = d_adjbf[m].rearrange("p a b -> p (a b)")
                S.o("sy", lambda e, s=src, d=adjbuf[m % 2]: e.dma_start(out=d[:], in_=s),
                    inc=L_adjb[m % 2], dma=True)
                r_psdp.acquire(S, "te", 16 + m)
                S.w("te", L_adjb[m % 2])
                if m == 0:
                    S.w("te", A_sp)
                for kb in range(NM):
                    S.o("te", lambda e, k=kb, mm=m: e.matmul(
                        ps_dp[mm % 2][:], adjbuf[mm % 2][:, k * 128:(k + 1) * 128],
                        deltaT_sb[:, k * TC:(k + 1) * TC],
                        start=(k == 0), stop=(k == NM - 1)),
                        inc=(T_dp if kb == NM - 1 else None))
                tdp_tgt[m] = T_dp.n
                r_adjb.release(16 + m, T_dp)

            def emit_evac(m):
                r_cacc.acquire(S, "ve", 70 + 16 * ci + m)
                S.w("ve", T_dp, tdp_tgt[m])
                S.o("ve", lambda e, mm=m: e.tensor_copy(cacc[mm % 2][:], ps_dp[mm % 2][:]))
                S.fence("ve", inc=V_su)
                vsu_tgt[m] = V_su.n
                r_psdp.release(16 + m, V_su)

            def emit_E(m):
                r_su.acquire(S, "ve", m)
                S.o("ve", lambda e, mm=m: e.tensor_mul(
                    su_sb[mm % 2][:], cacc[mm % 2][:], xi_sb[:, mm * TC:(mm + 1) * TC]))
                for g in range(2):
                    albuf = [al_sb, xq_sb][(2 * m + g) % 2]
                    r_al.acquire(S, "sc", 2 * m + g)
                    if ci == 0 and m == 0:
                        S.w("sc", T_trq)
                    if g == 0:
                        S.w("sc", V_su, vsu_tgt[m])
                    for j in range(8):
                        n = g * 8 + j
                        S.o("sc", lambda e, ab=albuf, mm=m, nn=n, jj=j: e.activation(
                            ab[:, jj * TC:(jj + 1) * TC], cacc[mm % 2][:], AF.Exp,
                            scale=a_sb[:, mm * NST + nn:mm * NST + nn + 1]),
                            inc=(A_al if j == 7 else None))
                    if g == 1:
                        r_cacc.release(70 + 16 * ci + m, A_al)
                    if m == 0 and g == 0:
                        S.w("ve", L_rep)
                    su_b = su_sb[m % 2][:].unsqueeze(1).broadcast_to([128, 8, TC])
                    bv = brep[g][:].rearrange("p (j t) -> p j t", j=8)
                    wvv = wv_sb[:].rearrange("p (j t) -> p j t", j=8)
                    S.o("ve", lambda e, sb_=su_b, b_=bv, w_=wvv: e.tensor_tensor(
                        w_, sb_, b_, op=ALU.mult), inc=V_w)
                    if g == 1:
                        r_su.release(m, V_w)
                    S.w("ve", A_al)
                    for j in range(8):
                        n = g * 8 + j
                        S.o("ve", lambda e, ab=albuf, mm=m, nn=n, jj=j: e.tensor_tensor_scan(
                            hh_sb[:, jj * TC:(jj + 1) * TC],
                            ab[:, jj * TC:(jj + 1) * TC],
                            wv_sb[:, jj * TC:(jj + 1) * TC],
                            hcarry[:, mm * NST + nn:mm * NST + nn + 1],
                            op0=ALU.mult, op1=ALU.add),
                            inc=(V_scan if j == 7 else None))
                    r_al.release(2 * m + g, V_scan)
                    S.fence("ve")
                    S.o("ve", lambda e, mm=m, gg=g: e.tensor_copy(
                        hcarry[:, mm * NST + gg * 8: mm * NST + gg * 8 + 8],
                        hh_sb[:, TC - 1::TC]))
                    S.o("ve", lambda e, gg=g: e.tensor_mul(wv_sb[:], hh_sb[:], crep[gg][:]))
                    S.o("ve", lambda e: e.tensor_add(
                        wv_sb[:, 0:4 * TC], wv_sb[:, 0:4 * TC], wv_sb[:, 4 * TC:8 * TC]))
                    S.o("ve", lambda e: e.tensor_add(
                        wv_sb[:, 4 * TC:6 * TC], wv_sb[:, 0:2 * TC], wv_sb[:, 2 * TC:4 * TC]))
                    S.o("ve", lambda e, gg=g: e.tensor_add(
                        yg_sb[:, gg * TC:(gg + 1) * TC], wv_sb[:, 4 * TC:5 * TC],
                        wv_sb[:, 5 * TC:6 * TC]))
                S.fence("ve")
                S.o("ve", lambda e: e.tensor_add(
                    yg_sb[:, 0:TC], yg_sb[:, 0:TC], yg_sb[:, TC:2 * TC]), fence=True)
                S.o("ve", lambda e, mm=m: e.scalar_tensor_tensor(
                    t2_sb[:], xi_sb[:, mm * TC:(mm + 1) * TC], dd_sb[:, mm:mm + 1],
                    yg_sb[:, 0:TC], op0=ALU.mult, op1=ALU.add), fence=True)
                S.w("ve", A_res)
                S.o("ve", lambda e, mm=m: e.tensor_mul(
                    yf_sb[:, mm * TC:(mm + 1) * TC], yf_sb[:, mm * TC:(mm + 1) * TC],
                    t2_sb[:]))
                S.fence("ve", inc=V_y)

            for m in range(NM):
                emit_dp(m)
                emit_evac(m)
                if m >= 1:
                    emit_E(m - 1)
            if PH >= 7:
                tout_h0 = emit_F_mms(0)
            emit_E(NM - 1)
            if PH >= 7:
                xre0 = emit_F_loads(0)

            if PH >= 7:
                emit_F_evac(0, tout_h0, xre0)
                xre1 = emit_F_loads(1)
                tout_h1 = emit_F_mms(1)
                emit_F_evac(1, tout_h1, xre1)

            r_rep.release(0, V_y)
            r_xi.release(0, V_y)
            r_dT.release(0, T_dp)
            r_dinT.release(0, T_dt)
            r_xnT.release(0, T_res)
            r_xqT.release(0, T_out)
            r_yf.release(0, T_out)
        if DBG:
            dbg_src = {"d_xn": xn_sb, "d_xnT": xnT_sb, "d_xi": xi_sb,
                       "d_dT": deltaT_sb, "d_yf": yf_sb, "d_xq": xq_sb,
                       "d_hc": hcarry, "d_brep": brep[0], "d_crep": crep[0],
                       "d_asb": a_sb, "d_su": su_sb[1], "d_al": al_sb,
                       "d_hh": hh_sb, "d_wv": wv_sb, "d_stat": stat,
                       "d_ln": ln_snap}
            dsem = S_("dsem"); dwr = S_("dwr")
            S.w("ve", S_os[0], S_os[0].n)
            S.w("ve", S_os[1], S_os[1].n)
            for nm, t in dbg_src.items():
                cols = t.ap().shape[1]
                for p0 in range(0, cols, 1024):
                    pc = min(1024, cols - p0)
                    S.w("ve", dwr, dwr.n)
                    S.o("ve", lambda e, tt=t, a=p0, c=pc: e.tensor_copy(
                        x_ld[0][:, 0:c], tt[:, a:a + c]), inc=dsem)
                    S.w("gp", dsem, dsem.n)
                    S.o("gp", lambda e, nm_=nm, a=p0, c=pc: e.dma_start(
                        out=dbg_aps[nm_][:, a:a + c], in_=x_ld[0][:, 0:c]),
                        inc=dwr, dma=True)
            S.w("gp", dwr, dwr.n)

        S.w("gp", S_os[0], S_os[0].n)
        S.w("gp", S_os[1], S_os[1].n)

        block = stack.enter_context(nc.Block())
        block.sync(lambda e: S.replay("sy", e))
        block.gpsimd(lambda e: S.replay("gp", e))
        block.vector(lambda e: S.replay("ve", e))
        block.scalar(lambda e: S.replay("sc", e))
        block.tensor(lambda e: S.replay("te", e))

    return nc


_CACHE = {}


def _hadamard(size):
    h = np.array([[1.0]], dtype=np.float64)
    while h.shape[0] < size:
        h = np.block([[h, h], [h, -h]])
    return (h / np.sqrt(size)).astype(np.float32)


def kernel(L0_attn_x, L0_attn_adj, L0_attn_in_proj_w, L0_attn_conv_w, L0_attn_conv_b,
           L0_attn_x_proj_w, L0_attn_dt_proj_w, L0_attn_dt_proj_b, L0_attn_A_log,
           L0_attn_D, L0_attn_out_proj_w, L0_ffn_weight):
    import ml_dtypes
    x = np.ascontiguousarray(np.asarray(L0_attn_x, dtype=np.float32))
    adj = np.asarray(L0_attn_adj, np.float32)
    W = np.asarray(L0_attn_in_proj_w, np.float32)
    adj_pm = np.ascontiguousarray(adj.reshape(16, 128, 16, 128).transpose(2, 1, 0, 3))
    w_pm = np.ascontiguousarray(W.reshape(8, 128, 32, 128).transpose(2, 1, 0, 3))
    op_w = np.ascontiguousarray(np.asarray(L0_attn_out_proj_w, np.float32))
    xp_pm = np.ascontiguousarray(
        np.asarray(L0_attn_x_proj_w, np.float32).reshape(16, 128, 96)
        .transpose(1, 0, 2).reshape(128, 16 * 96))
    dt_w = np.ascontiguousarray(np.asarray(L0_attn_dt_proj_w, np.float32))
    a_pm = np.ascontiguousarray(
        np.asarray(L0_attn_A_log, np.float32).reshape(16, 128, 16)
        .transpose(1, 0, 2).reshape(128, 256))
    convw_pm = np.ascontiguousarray(
        np.asarray(L0_attn_conv_w, np.float32)[:, 0, :].reshape(16, 128, 4)
        .transpose(1, 0, 2).reshape(128, 64))
    convb_pm = np.ascontiguousarray(np.asarray(L0_attn_conv_b, np.float32).reshape(16, 128).T)
    dtb_pm = np.ascontiguousarray(np.asarray(L0_attn_dt_proj_b, np.float32).reshape(16, 128).T)
    d_pm = np.ascontiguousarray(np.asarray(L0_attn_D, np.float32).reshape(16, 128).T)
    ffn_w = np.ascontiguousarray(np.asarray(L0_ffn_weight, np.float32))
    h_bf = np.ascontiguousarray(_hadamard(DM).astype(ml_dtypes.bfloat16))
    eye_bf = np.ascontiguousarray(np.eye(128, dtype=np.float32).astype(ml_dtypes.bfloat16))

    if "nc" not in _CACHE:
        _CACHE["nc"] = build_nc()
    nc = _CACHE["nc"]

    shared = dict(adj_pm=adj_pm, w_pm=w_pm, op_w=op_w, xp_pm=xp_pm, dt_w=dt_w,
                  a_pm=a_pm, convw_pm=convw_pm, convb_pm=convb_pm, dtb_pm=dtb_pm,
                  d_pm=d_pm, ffn_w=ffn_w, h_bf=h_bf, eye_bf=eye_bf)
    in_maps = []
    for c in range(NCORE):
        m = dict(shared)
        m["x"] = np.ascontiguousarray(x[c * BC:(c + 1) * BC])
        in_maps.append(m)
    res = run_bass_kernel_spmd(nc, in_maps, core_ids=list(range(NCORE)))
    out = np.concatenate([r["out"] for r in res.results], axis=0)
    return out.astype(np.float32)


# version 11
# speedup vs baseline: 1.2441x; 1.0384x over previous
import contextlib
import numpy as np
import concourse.bass as bass
import concourse.mybir as mybir
from concourse.bass_utils import run_bass_kernel_spmd

F32 = mybir.dt.float32
BF16 = mybir.dt.bfloat16
AF = mybir.ActivationFunctionType
ALU = mybir.AluOpType

NCORE = 8
BC = 2
L = 2048
DM = 1024
DI = 2048
NST = 16
DR = 64
TC = 512
NCH = BC * L // TC
NTI = TC // 128
NM = DI // 128
NKB = DM // 128
RNE_C = float(3 * 2.0 ** 22)
EPS = 1e-5

ENGS = ("sy", "gp", "ve", "sc", "te")

DEV_DRAIN = False
PH = 7
PHE = 6
ALV = 0
DBG = False


class Sem:
    def __init__(self, stack, nc, name):
        self.h = stack.enter_context(nc.semaphore(name))
        self.n = 0
        self.name = name


class Sched:

    def __init__(self):
        self.q = {e: [] for e in ENGS}

    def o(self, eng, fn, inc=None, dma=False, fence=False):
        incs = []
        if inc is not None:
            if not isinstance(inc, list):
                inc = [inc]
            for s in inc:
                k = 16 if dma else 1
                s.n += k
                incs.append((s, k))
        self.q[eng].append(("op", fn, incs))
        if fence:
            self.fence(eng)

    def fence(self, eng, inc=None):
        incs = []
        if inc is not None:
            if not isinstance(inc, list):
                inc = [inc]
            for s in inc:
                s.n += 1
                incs.append((s, 1))
        jb = self.junk[eng]
        if eng == "sc":
            self.q[eng].append(("op", lambda e: e.copy(jb[:, 0:1], jb[:, 2:3]), []))
            self.q[eng].append(("op", lambda e: e.copy(jb[:, 1:2], jb[:, 3:4]), incs))
        else:
            self.q[eng].append(("op", lambda e: e.tensor_copy(jb[:, 0:1], jb[:, 2:3]), []))
            self.q[eng].append(("op", lambda e: e.tensor_copy(jb[:, 1:2], jb[:, 3:4]), incs))

    def w(self, eng, sem, target=None):
        t = sem.n if target is None else target
        self.q[eng].append(("wait", sem, t))

    def replay(self, eng_name, eng):
        for item in self.q[eng_name]:
            if item[0] == "op":
                ins = item[1](eng)
                for (s, k) in item[2]:
                    ins.then_inc(s.h, k)
                if DEV_DRAIN and eng_name in ("ve", "sc", "te"):
                    eng.drain()
            else:
                eng.wait_ge(item[1].h, item[2])


class Ring:

    def __init__(self, n):
        self.n = n
        self.slots = [[] for _ in range(n)]

    def acquire(self, S, eng, idx):
        for (sem, t) in self.slots[idx % self.n]:
            S.w(eng, sem, t)
        self.slots[idx % self.n] = []

    def release(self, idx, sem, target=None):
        self.slots[idx % self.n].append((sem, sem.n if target is None else target))


def build_nc():
    nc = bass.Bass("TRN2", target_bir_lowering=False,
                   detect_race_conditions=DEV_DRAIN)

    ap_x = nc.dram_tensor("x", [BC, L, DM], F32, kind="ExternalInput").ap()
    ap_adj = nc.dram_tensor("adj_pm", [NM, 128, NM, 128], F32, kind="ExternalInput").ap()
    ap_w = nc.dram_tensor("w_pm", [2 * NM, 128, NKB, 128], F32, kind="ExternalInput").ap()
    ap_op = nc.dram_tensor("op_w", [DI, DM], F32, kind="ExternalInput").ap()
    ap_xp = nc.dram_tensor("xp_pm", [128, NM * 96], F32, kind="ExternalInput").ap()
    ap_dtw = nc.dram_tensor("dt_w", [DR, DI], F32, kind="ExternalInput").ap()
    ap_apm = nc.dram_tensor("a_pm", [128, NM * NST], F32, kind="ExternalInput").ap()
    ap_cw = nc.dram_tensor("convw_pm", [128, NM * 4], F32, kind="ExternalInput").ap()
    ap_cb = nc.dram_tensor("convb_pm", [128, NM], F32, kind="ExternalInput").ap()
    ap_dtb = nc.dram_tensor("dtb_pm", [128, NM], F32, kind="ExternalInput").ap()
    ap_dd = nc.dram_tensor("d_pm", [128, NM], F32, kind="ExternalInput").ap()
    ap_fw = nc.dram_tensor("ffn_w", [DM, DM], F32, kind="ExternalInput").ap()
    ap_h = nc.dram_tensor("h_bf", [DM, DM], BF16, kind="ExternalInput").ap()
    ap_eye = nc.dram_tensor("eye_bf", [128, 128], BF16, kind="ExternalInput").ap()

    ap_out = nc.dram_tensor("out", [BC, L, DM], F32, kind="ExternalOutput").ap()
    dbg_aps = {}
    if DBG:
        for nm, shape in [("d_xn", [128, NTI * DM]), ("d_xnT", [128, NKB * TC]),
                          ("d_xi", [128, NM * TC]), ("d_dT", [128, NM * TC]),
                          ("d_yf", [128, NM * TC]), ("d_xq", [128, NTI * DM]),
                          ("d_hc", [128, NM * NST]), ("d_brep", [128, 8 * TC]),
                          ("d_crep", [128, 8 * TC]), ("d_asb", [128, NM * NST]),
                          ("d_su", [128, TC]), ("d_al", [128, 8 * TC]),
                          ("d_hh", [128, 8 * TC]), ("d_wv", [128, 8 * TC]),
                          ("d_stat", [128, 16]), ("d_ln", [128, 16])]:
            dbg_aps[nm] = nc.dram_tensor(nm, shape, F32, kind="ExternalOutput").ap()
    d_adjbf = nc.dram_tensor("adj_bf", [NM, 128, NM, 128], BF16, kind="Internal").ap()
    d_wbf = nc.dram_tensor("w_bf", [2 * NM, 128, NKB, 128], BF16, kind="Internal").ap()
    d_opbf = nc.dram_tensor("op_bf", [DI, DM], BF16, kind="Internal").ap()
    d_ternbf = nc.dram_tensor("tern_bf", [DM, DM], BF16, kind="Internal").ap()
    d_dtw = nc.dram_tensor("dtw_bf", [DR, DI], BF16, kind="Internal").ap()
    d_bc = nc.dram_tensor("bc_scr", [32, TC], BF16, kind="Internal").ap()

    with contextlib.ExitStack() as stack:
        def sb(name, shape, dtype):
            return stack.enter_context(nc.sbuf_tensor(name, shape, dtype))

        def ps_(name, shape, dtype):
            return stack.enter_context(nc.psum_tensor(name, shape, dtype))

        eye_sb = sb("eye_sb", [128, 128], BF16)
        xpw_sb = sb("xpw_sb", [128, NM * 96], BF16)
        dtbuf = [sb(f"dtbuf{i}", [DR, 128], BF16) for i in range(2)]
        a_sb = sb("a_sb", [128, NM * NST], F32)
        cw_sb = sb("cw_sb", [128, NM * 4], F32)
        cb_sb = sb("cb_sb", [128, NM], F32)
        dtb_sb = sb("dtb_sb", [128, NM], F32)
        dd_sb = sb("dd_sb", [128, NM], F32)
        hcarry = sb("hcarry", [128, NM * NST], F32)
        halo_sb = sb("halo_sb", [128, NM * 3], F32)
        stat = sb("stat", [128, 16], F32)
        cst = sb("cst", [128, 2], F32)
        junk_v = sb("junk_v", [128, 4], F32)
        junk_s = sb("junk_s", [128, 4], F32)

        ln_snap = sb("ln_snap", [128, 16], F32)
        x_ld = [sb(f"x_ld{i}", [128, DM], F32) for i in range(2)]
        xn_sb = sb("xn_sb", [128, NTI * DM], BF16)
        xnT_sb = sb("xnT_sb", [128, NKB * TC], BF16)
        xq_sb = sb("xq_sb", [128, NTI * DM], BF16)
        xqT_sb = sb("xqT_sb", [128, NKB * TC], BF16)
        hbuf = sb("hbuf", [128, NKB * DM], BF16)
        wibuf = [sb(f"wibuf{i}", [128, NKB * 128], BF16) for i in range(2)]
        rwbuf = [sb(f"rwbuf{i}", [128, NKB * 128], BF16) for i in range(2)]
        adjbuf = [sb(f"adjbuf{i}", [128, DI], BF16) for i in range(2)]
        opbuf = [sb(f"opbuf{i}", [128, 1024], BF16) for i in range(2)]
        ternbuf = [sb(f"ternbuf{i}", [128, 1024], BF16) for i in range(2)]
        xip_sb = [sb(f"xip{i}", [128, TC + 3], BF16) for i in range(2)]
        cacc = [sb(f"cacc{i}", [128, TC], F32) for i in range(2)]
        xi_sb = sb("xi_sb", [128, NM * TC], BF16)
        dinT_sb = sb("dinT_sb", [DR, TC], BF16)
        deltaT_sb = sb("deltaT_sb", [128, NM * TC], BF16)
        su_sb = [sb(f"su{i}", [128, TC], BF16) for i in range(2)]
        brep = [sb(f"brep{g}", [128, 8 * TC], BF16) for g in range(2)]
        crep = [sb(f"crep{g}", [128, 8 * TC], BF16) for g in range(2)]
        al_sb = sb("al_sb", [128, 8 * TC], BF16)
        wv_sb = sb("wv_sb", [128, 8 * TC], BF16)
        hh_sb = sb("hh_sb", [128, 8 * TC], BF16)
        yg_sb = sb("yg_sb", [128, 2 * TC], BF16)
        t2_sb = sb("t2_sb", [128, TC], BF16)
        yf_sb = sb("yf_sb", [128, NM * TC], BF16)
        out_sb = [sb(f"out_sb{i}", [128, 512], F32) for i in range(2)]
        bc_sb = sb("bc_sb", [96, TC], BF16)

        ps_tr = [ps_(f"ps_tr{i}", [128, 1024], BF16) for i in range(2)]
        ps_a = [ps_(f"ps_a{i}", [128, 512], F32) for i in range(2)]
        ps_ip = [ps_(f"ps_ip{i}", [128, 512], F32) for i in range(2)]
        ps_dp = [ps_(f"ps_dp{i}", [128, 512], F32) for i in range(2)]

        sems = {}

        def S_(name):
            sems[name] = Sem(stack, nc, name)
            return sems[name]

        L_xld = [S_(f"L_xld{i}") for i in range(2)]
        L_wib = [S_(f"L_wib{i}") for i in range(2)]
        L_rwb = [S_(f"L_rwb{i}") for i in range(2)]
        L_adjb = [S_(f"L_adjb{i}") for i in range(2)]
        L_opb = [S_(f"L_opb{i}") for i in range(2)]
        L_ternb = [S_(f"L_ternb{i}") for i in range(2)]
        L_dtb = [S_(f"L_dtb{i}") for i in range(2)]
        P_wib = [S_(f"P_wib{i}") for i in range(2)]
        P_adjb = [S_(f"P_adjb{i}") for i in range(2)]
        L_hb = S_("L_hb"); L_small = S_("L_small"); L_h = S_("L_h")
        L_bcd = S_("L_bcd"); L_rep = S_("L_rep")
        S_os = [S_(f"S_os{i}") for i in range(2)]
        T_trn = S_("T_trn"); T_trq = S_("T_trq"); T_xh = S_("T_xh")
        T_ip = S_("T_ip"); T_res = S_("T_res"); T_xdbl = S_("T_xdbl")
        T_dt = S_("T_dt"); T_dp = S_("T_dp"); T_opk = S_("T_opk")
        T_tek = S_("T_tek"); T_out = S_("T_out"); T_ptr = S_("T_ptr")
        V_xn = S_("V_xn"); V_q = S_("V_q"); V_conv = S_("V_conv"); V_bc = S_("V_bc")
        V_su = S_("V_su"); V_w = S_("V_w"); V_scan = S_("V_scan"); V_y = S_("V_y")
        V_oev = S_("V_oev"); V_cast = S_("V_cast"); V_pt = S_("V_pt")
        A_rstd = S_("A_rstd"); A_evxn = S_("A_evxn"); A_evxq = S_("A_evxq")
        A_evxi = S_("A_evxi"); A_silu = S_("A_silu"); A_res = S_("A_res")
        A_dinT = S_("A_dinT"); A_sp = S_("A_sp"); A_al = S_("A_al"); A_pev = S_("A_pev")

        S = Sched()
        S.junk = {"ve": junk_v, "sc": junk_s}

        r_xld = Ring(2); r_wib = Ring(2); r_rwb = Ring(2)
        r_adjb = Ring(2); r_opb = Ring(2); r_ternb = Ring(2)
        r_xip = Ring(2); r_cacc = Ring(2); r_su = Ring(2); r_out = Ring(2)
        r_pstr = Ring(2)
        r_psa = Ring(2); r_psip = Ring(2); r_psdp = Ring(2)
        r_xn = Ring(1); r_xnT = Ring(1); r_xq = Ring(1); r_xqT = Ring(1)
        r_xi = Ring(1); r_dT = Ring(1); r_dinT = Ring(1); r_yf = Ring(1)
        r_al = Ring(2); r_rep = Ring(1); r_bcs = Ring(1)
        r_ternsc = Ring(1); r_hbsc = Ring(1)

        S.o("sy", lambda e: e.dma_start(out=eye_sb[:], in_=ap_eye), inc=L_small, dma=True)

        for m in range(NM):
            for piece in range(2):
                idx = m * 2 + piece
                r_xld.acquire(S, "sy", idx)
                src = ap_adj[m].rearrange("p a b -> p (a b)")[:, piece * 1024:(piece + 1) * 1024]
                S.o("sy", lambda e, s=src, d=x_ld[idx % 2]: e.dma_start(out=d[:], in_=s),
                    inc=L_xld[idx % 2], dma=True)
                r_adjb.acquire(S, "ve", m) if piece == 0 else None
                S.w("ve", L_xld[idx % 2])
                S.o("ve", lambda e, sl=x_ld[idx % 2], d=adjbuf[m % 2], p=piece:
                    e.tensor_copy(d[:, p * 1024:(p + 1) * 1024], sl[:]), inc=V_cast)
                r_xld.release(idx, V_cast)
            dst = d_adjbf[m].rearrange("p a b -> p (a b)")
            S.w("gp", V_cast)
            S.o("gp", lambda e, s=adjbuf[m % 2], d=dst: e.dma_start(out=d, in_=s[:]),
                inc=P_adjb[m % 2], dma=True)
            r_adjb.release(m, P_adjb[m % 2])
        for m in range(2 * NM):
            idx = 32 + m
            r_xld.acquire(S, "sy", idx)
            src = ap_w[m].rearrange("p a b -> p (a b)")
            S.o("sy", lambda e, s=src, d=x_ld[idx % 2]: e.dma_start(out=d[:], in_=s),
                inc=L_xld[idx % 2], dma=True)
            r_wib.acquire(S, "ve", m)
            S.w("ve", L_xld[idx % 2])
            S.o("ve", lambda e, sl=x_ld[idx % 2], d=wibuf[m % 2]:
                e.tensor_copy(d[:], sl[:]), inc=V_cast)
            r_xld.release(idx, V_cast)
            dst = d_wbf[m].rearrange("p a b -> p (a b)")
            S.w("gp", V_cast)
            S.o("gp", lambda e, s=wibuf[m % 2], d=dst: e.dma_start(out=d, in_=s[:]),
                inc=P_wib[m % 2], dma=True)
            r_wib.release(m, P_wib[m % 2])
        for kb in range(NM):
            idx = 64 + kb
            r_xld.acquire(S, "sy", idx)
            src = ap_op[kb * 128:(kb + 1) * 128, :]
            S.o("sy", lambda e, s=src, d=x_ld[idx % 2]: e.dma_start(out=d[:], in_=s),
                inc=L_xld[idx % 2], dma=True)
            r_wib.acquire(S, "ve", 32 + kb)
            S.w("ve", L_xld[idx % 2])
            S.o("ve", lambda e, sl=x_ld[idx % 2], d=wibuf[kb % 2]:
                e.tensor_copy(d[:], sl[:]), inc=V_cast)
            r_xld.release(idx, V_cast)
            dst = d_opbf[kb * 128:(kb + 1) * 128, :]
            S.w("gp", V_cast)
            S.o("gp", lambda e, s=wibuf[kb % 2], d=dst: e.dma_start(out=d, in_=s[:]),
                inc=P_wib[kb % 2], dma=True)
            r_wib.release(32 + kb, P_wib[kb % 2])
        for piece in range(2):
            idx = 80 + piece
            r_xld.acquire(S, "sy", idx)
            S.o("sy", lambda e, p=piece, d=x_ld[idx % 2]:
                e.dma_start(out=d[:, 0:768], in_=ap_xp[:, p * 768:(p + 1) * 768]),
                inc=L_xld[idx % 2], dma=True)
            S.w("ve", L_xld[idx % 2])
            S.o("ve", lambda e, p=piece, sl=x_ld[idx % 2]:
                e.tensor_copy(xpw_sb[:, p * 768:(p + 1) * 768], sl[:, 0:768]), inc=V_cast)
            r_xld.release(idx, V_cast)
        for piece in range(2):
            idx = 82 + piece
            r_xld.acquire(S, "sy", idx)
            S.o("sy", lambda e, p=piece, d=x_ld[idx % 2]:
                e.dma_start(out=d[0:DR, :], in_=ap_dtw[:, p * 1024:(p + 1) * 1024]),
                inc=L_xld[idx % 2], dma=True)
            r_wib.acquire(S, "ve", 48 + piece)
            S.w("ve", L_xld[idx % 2])
            S.o("ve", lambda e, p=piece, sl=x_ld[idx % 2]:
                e.tensor_copy(wibuf[p % 2][0:DR, 0:1024], sl[0:DR, :]), inc=V_cast)
            r_xld.release(idx, V_cast)
            S.w("gp", V_cast)
            S.o("gp", lambda e, p=piece: e.dma_start(
                out=d_dtw[:, p * 1024:(p + 1) * 1024], in_=wibuf[p % 2][0:DR, 0:1024]),
                inc=P_wib[piece % 2], dma=True)
            r_wib.release(48 + piece, P_wib[piece % 2])
        S.o("sy", lambda e: e.dma_start(out=a_sb[:], in_=ap_apm), inc=L_small, dma=True)
        S.o("sy", lambda e: e.dma_start(out=cw_sb[:], in_=ap_cw), inc=L_small, dma=True)
        S.o("sy", lambda e: e.dma_start(out=cb_sb[:], in_=ap_cb), inc=L_small, dma=True)
        S.o("sy", lambda e: e.dma_start(out=dtb_sb[:], in_=ap_dtb), inc=L_small, dma=True)
        S.o("sy", lambda e: e.dma_start(out=dd_sb[:], in_=ap_dd), inc=L_small, dma=True)
        S.w("sc", L_small, 96)
        S.o("sc", lambda e: e.activation(a_sb[:], a_sb[:], AF.Exp), inc=A_al)
        S.w("ve", A_al)
        S.o("ve", lambda e: e.tensor_scalar_mul(a_sb[:], a_sb[:], -1.0), inc=V_cast)
        S.o("ve", lambda e: e.memset(hcarry[:], 0.0), inc=V_cast)
        S.o("ve", lambda e: e.memset(halo_sb[:], 0.0), inc=V_cast)
        S.o("ve", lambda e: e.memset(cst[:, 0:1], EPS), inc=V_cast)
        S.o("ve", lambda e: e.memset(cst[:, 1:2], -0.5), inc=V_cast)
        S.o("ve", lambda e: e.memset(junk_v[:], 0.0))
        S.o("sc", lambda e: e.memzero(junk_s[:]))

        for ob in range(NKB):
            idx = 84 + ob
            r_xld.acquire(S, "sy", idx)
            S.o("sy", lambda e, o=ob, d=x_ld[idx % 2]:
                e.dma_start(out=d[:], in_=ap_fw[o * 128:(o + 1) * 128, :]),
                inc=L_xld[idx % 2], dma=True)
            S.w("ve", L_xld[idx % 2])
            xl = x_ld[idx % 2]
            for hh in range(2):
                r_cacc.acquire(S, "ve", hh)
                S.o("ve", lambda e, sl=xl, h=hh: e.tensor_scalar(
                    cacc[h][:], sl[:, h * 512:(h + 1) * 512], -1.0, None, op0=ALU.mult),
                    fence=True)
                S.o("ve", lambda e, sl=xl, h=hh: e.tensor_max(
                    cacc[h][:], cacc[h][:], sl[:, h * 512:(h + 1) * 512]), fence=True)
                S.o("ve", lambda e, h=hh: e.tensor_reduce(
                    stat[:, 3 + h:4 + h], cacc[h][:], axis=mybir.AxisListType.X,
                    op=ALU.add), fence=True)
            S.o("ve", lambda e: e.tensor_add(stat[:, 0:1], stat[:, 3:4], stat[:, 4:5]), fence=True)
            S.o("ve", lambda e: e.tensor_scalar_mul(stat[:, 0:1], stat[:, 0:1], 1.0 / DM), fence=True)
            S.o("ve", lambda e: e.tensor_scalar_max(stat[:, 0:1], stat[:, 0:1], 1e-5), fence=True)
            S.o("ve", lambda e: e.tensor_scalar_mul(stat[:, 14:15], stat[:, 0:1], 0.5), fence=True)
            S.o("ve", lambda e: e.tensor_scalar_mul(stat[:, 15:16], stat[:, 14:15], -1.0), fence=True)
            r_ternsc.acquire(S, "ve", 0)
            for hh in range(2):
                cs = slice(hh * 512, (hh + 1) * 512)
                r_cacc.acquire(S, "ve", hh)
                S.o("ve", lambda e, sl=xl, c=cs: e.tensor_scalar(
                    cacc[0][:], sl[:, c], stat[:, 14:15], None, op0=ALU.is_gt))
                S.o("ve", lambda e, sl=xl, c=cs: e.tensor_scalar(
                    cacc[1][:], sl[:, c], stat[:, 15:16], None, op0=ALU.is_lt), fence=True)
                S.o("ve", lambda e: e.tensor_sub(cacc[0][:], cacc[0][:], cacc[1][:]), fence=True)
                S.o("ve", lambda e, o=ob, c=cs: e.tensor_scalar(
                    xq_sb[:, c], cacc[0][:], stat[:, 0:1], None, op0=ALU.mult))
                S.fence("ve", inc=V_pt)
                r_cacc.release(hh, V_pt)
            r_xld.release(idx, V_pt)
            r_pstr.acquire(S, "te", 0); r_pstr.acquire(S, "te", 1)
            S.w("te", V_pt)
            S.w("te", L_small, 96)
            for ib in range(NKB):
                S.o("te", lambda e, i=ib: e.transpose(
                    ps_tr[i // 4][:, (i % 4) * 128:(i % 4 + 1) * 128],
                    xq_sb[:, i * 128:(i + 1) * 128],
                    eye_sb[:]), inc=(T_ptr if ib == NKB - 1 else None))
            r_ternsc.release(0, T_ptr)
            r_hbsc.acquire(S, "sc", 0)
            S.w("sc", T_ptr)
            S.o("sc", lambda e: e.copy(hbuf[:, 0:512], ps_tr[0][:, 0:512]))
            S.o("sc", lambda e: e.copy(hbuf[:, 512:1024], ps_tr[1][:, 0:512]), inc=A_pev)
            r_pstr.release(0, A_pev); r_pstr.release(1, A_pev)
            dst = d_ternbf.rearrange("(ib p) o -> p ib o", ib=NKB)[:, :, ob * 128:(ob + 1) * 128]
            S.w("gp", A_pev)
            S.o("gp", lambda e, d=dst: e.dma_start(out=d, in_=hbuf[:, 0:1024].rearrange(
                "p (ib c) -> p ib c", ib=NKB)), inc=L_hb, dma=True)
            r_hbsc.release(0, L_hb)
        S.w("sy", L_hb)
        S.o("sy", lambda e: e.dma_start(
            out=hbuf[:].rearrange("p (kb c) -> p kb c", kb=NKB),
            in_=ap_h.rearrange("(kb p) c -> p kb c", p=128)),
            inc=L_h, dma=True)
        r_xq.release(0, T_ptr)

        PRO_WR = [(P_wib[0], P_wib[0].n), (P_wib[1], P_wib[1].n),
                  (P_adjb[0], P_adjb[0].n), (P_adjb[1], P_adjb[1].n),
                  (L_hb, L_hb.n)]

        for ci in range(NCH):
            b = ci // (L // TC)
            cs = ci % (L // TC)
            t0 = cs * TC

            if cs == 0:
                S.o("ve", lambda e: e.memset(hcarry[:], 0.0))
                S.o("ve", lambda e: e.memset(halo_sb[:], 0.0))
            def emit_xload(ti):
                gi = ci * NTI + ti
                r_xld.acquire(S, "sc", 100 + gi)
                src = ap_x[b, t0 + ti * 128: t0 + (ti + 1) * 128, :]
                S.o("sc", lambda e, s=src, d=x_ld[gi % 2]: e.dma_start(out=d[:], in_=s),
                    inc=L_xld[gi % 2], dma=True)

            emit_xload(0)
            emit_xload(1)
            vxn_tgt_a = []
            for ti in range(NTI):
                gi = ci * NTI + ti
                xl = x_ld[gi % 2]
                S.w("ve", L_xld[gi % 2])
                S.o("ve", lambda e, sl=xl: e.reduce_sum(stat[:, 0:1], sl[:],
                                                        axis=mybir.AxisListType.X),
                    fence=True)
                r_cacc.acquire(S, "sc", 2 * gi)
                r_cacc.acquire(S, "sc", 2 * gi + 1)
                S.w("sc", L_xld[gi % 2])
                S.w("sc", V_xn)
                S.o("sc", lambda e, sl=xl: e.activation(
                    cacc[0][:], sl[:, 0:512], AF.Square, accum_out=stat[:, 1:2]))
                S.o("sc", lambda e, sl=xl: e.activation(
                    cacc[1][:], sl[:, 512:1024], AF.Square, accum_out=stat[:, 2:3]))
                S.fence("sc", inc=A_rstd)
                r_cacc.release(2 * gi, A_rstd)
                r_cacc.release(2 * gi + 1, A_rstd)
                S.w("ve", A_rstd)
                S.o("ve", lambda e: e.tensor_add(stat[:, 3:4], stat[:, 1:2], stat[:, 2:3]), fence=True)
                S.o("ve", lambda e: e.tensor_scalar_mul(stat[:, 4:5], stat[:, 0:1], 1.0 / DM), fence=True)
                S.o("ve", lambda e: e.tensor_scalar_mul(stat[:, 5:6], stat[:, 3:4], 1.0 / DM), fence=True)
                S.o("ve", lambda e: e.tensor_mul(stat[:, 6:7], stat[:, 4:5], stat[:, 4:5]), fence=True)
                S.o("ve", lambda e: e.tensor_sub(stat[:, 5:6], stat[:, 5:6], stat[:, 6:7]))
                S.fence("ve", inc=V_xn)
                S.w("sc", V_xn)
                S.o("sc", lambda e: e.activation(stat[:, 7:8], stat[:, 5:6], AF.Ln,
                                                 bias=cst[:, 0:1]), fence=True)
                S.o("sc", lambda e: e.activation(stat[:, 7:8], stat[:, 7:8], AF.Exp,
                                                 scale=cst[:, 1:2]))
                S.fence("sc", inc=A_rstd)
                r_xn.acquire(S, "ve", 0) if ti == 0 else None
                S.w("ve", A_rstd)
                if DBG and ci == 0 and ti == 0:
                    S.o("ve", lambda e: e.tensor_copy(ln_snap[:], stat[:]))
                S.o("ve", lambda e, sl=xl, t=ti: e.tensor_scalar(
                    xn_sb[:, t * DM:(t + 1) * DM], sl[:], stat[:, 4:5], stat[:, 7:8],
                    op0=ALU.subtract, op1=ALU.mult))
                S.fence("ve", inc=V_xn)
                vxn_tgt_a.append(V_xn.n)
                r_xld.release(100 + gi, V_xn)
                if ti + 2 < NTI:
                    emit_xload(ti + 2)

            if PH < 2:
                continue
            r_xnT.acquire(S, "te", 0)
            for kb in range(NKB):
                r_pstr.acquire(S, "te", kb)
                for ti in range(NTI):
                    if kb == 0:
                        S.w("te", V_xn, vxn_tgt_a[ti])
                    S.o("te", lambda e, t=ti, k=kb: e.transpose(
                        ps_tr[k % 2][:, t * 128:(t + 1) * 128],
                        xn_sb[:, t * DM + k * 128: t * DM + (k + 1) * 128], eye_sb[:]),
                        inc=(T_trn if ti == NTI - 1 else None))
                S.w("sc", T_trn)
                S.o("sc", lambda e, k=kb: e.copy(
                    xnT_sb[:, k * TC:(k + 1) * TC],
                    ps_tr[k % 2][:, 0:512]), inc=A_evxn)
                r_pstr.release(kb, A_evxn)
            r_xn.release(0, T_trn)

            if PH < 3:
                continue
            r_xq.acquire(S, "ve", 0)
            for ti in range(NTI):
                for half in range(2):
                    r_psa.acquire(S, "te", half)
                    if ti == 0 and half == 0:
                        S.w("te", L_h, 16)
                        S.w("te", A_evxn)
                    for kb in range(NKB):
                        S.o("te", lambda e, t=ti, h=half, k=kb: e.matmul(
                            ps_a[h][:],
                            xnT_sb[:, k * TC + t * 128: k * TC + (t + 1) * 128],
                            hbuf[:, k * DM + h * 512: k * DM + (h + 1) * 512],
                            start=(k == 0), stop=(k == NKB - 1)),
                            inc=(T_xh if kb == NKB - 1 else None))
                S.w("ve", T_xh)
                for half in range(2):
                    r_cacc.acquire(S, "ve", 40 + 8 * ci + 2 * ti + half)
                    S.o("ve", lambda e, h=half: e.tensor_copy(cacc[h][:], ps_a[h][:]),
                        fence=(half == 1))
                    r_psa.release(half, V_q, V_q.n + 1)
                S.o("ve", lambda e: e.tensor_reduce(
                    stat[:, 8:9], cacc[0][:], axis=mybir.AxisListType.X, op=ALU.max))
                S.o("ve", lambda e: e.tensor_reduce(
                    stat[:, 9:10], cacc[1][:], axis=mybir.AxisListType.X, op=ALU.max))
                S.o("ve", lambda e: e.tensor_reduce(
                    stat[:, 11:12], cacc[0][:], axis=mybir.AxisListType.X, op=ALU.min))
                S.o("ve", lambda e: e.tensor_reduce(
                    stat[:, 12:13], cacc[1][:], axis=mybir.AxisListType.X, op=ALU.min),
                    fence=True)
                S.o("ve", lambda e: e.tensor_max(stat[:, 8:9], stat[:, 8:9], stat[:, 9:10]), fence=True)
                S.o("ve", lambda e: e.tensor_tensor(stat[:, 11:12], stat[:, 11:12], stat[:, 12:13], op=ALU.min), fence=True)
                S.o("ve", lambda e: e.tensor_scalar(stat[:, 11:12], stat[:, 11:12], -1.0, None, op0=ALU.mult), fence=True)
                S.o("ve", lambda e: e.tensor_max(stat[:, 8:9], stat[:, 8:9], stat[:, 11:12]), fence=True)
                S.o("ve", lambda e: e.tensor_scalar_max(stat[:, 8:9], stat[:, 8:9], 1e-5), fence=True)
                S.o("ve", lambda e: e.tensor_scalar_mul(stat[:, 8:9], stat[:, 8:9], 1.0 / 7.0))
                S.fence("ve", inc=V_q)
                S.w("sc", V_q)
                S.o("sc", lambda e: e.activation(stat[:, 10:11], stat[:, 8:9], AF.Ln),
                    fence=True)
                S.o("sc", lambda e: e.activation(stat[:, 9:10], stat[:, 10:11], AF.Exp,
                                                 scale=-1.0))
                S.fence("sc", inc=A_rstd)
                S.w("ve", A_rstd)
                for half in range(2):
                    S.o("ve", lambda e, h=half: e.tensor_scalar(
                        cacc[h][:], cacc[h][:], stat[:, 9:10], RNE_C,
                        op0=ALU.mult, op1=ALU.add), inc=V_q, fence=True)
                    S.o("ve", lambda e, h=half, t=ti: e.tensor_scalar(
                        xq_sb[:, t * DM + h * 512: t * DM + (h + 1) * 512],
                        cacc[h][:], RNE_C, stat[:, 8:9],
                        op0=ALU.subtract, op1=ALU.mult))
                    S.fence("ve", inc=V_q)
                    r_cacc.release(40 + 8 * ci + 2 * ti + half, V_q)

            if PH < 4:
                continue
            r_xi.acquire(S, "sc", 0)
            r_yf.acquire(S, "sc", 0)
            def emit_ip(m):
                r_wib.acquire(S, "sy", m)
                if ci == 0 and m == 0:
                    for (ps_sem, ps_t) in PRO_WR:
                        S.w("sy", ps_sem, ps_t)
                src = d_wbf[m].rearrange("p a b -> p (a b)")
                S.o("sy", lambda e, s=src, d=wibuf[m % 2]: e.dma_start(out=d[:], in_=s),
                    inc=L_wib[m % 2], dma=True)
                r_psip.acquire(S, "te", m)
                S.w("te", L_wib[m % 2])
                if m == 0:
                    S.w("te", A_evxn)
                for kb in range(NKB):
                    S.o("te", lambda e, k=kb, mm=m: e.matmul(
                        ps_ip[mm % 2][:], wibuf[mm % 2][:, k * 128:(k + 1) * 128],
                        xnT_sb[:, k * TC:(k + 1) * TC],
                        start=(k == 0), stop=(k == NKB - 1)),
                        inc=(T_ip if kb == NKB - 1 else None))
                r_wib.release(m, T_ip)

            def emit_res(m):
                r_rwb.acquire(S, "sy", m)
                src2 = d_wbf[NM + m].rearrange("p a b -> p (a b)")
                S.o("sy", lambda e, s=src2, d=rwbuf[m % 2]: e.dma_start(out=d[:], in_=s),
                    inc=L_rwb[m % 2], dma=True)
                r_psdp.acquire(S, "te", m)
                S.w("te", L_rwb[m % 2])
                for kb in range(NKB):
                    S.o("te", lambda e, k=kb, mm=m: e.matmul(
                        ps_dp[mm % 2][:], rwbuf[mm % 2][:, k * 128:(k + 1) * 128],
                        xnT_sb[:, k * TC:(k + 1) * TC],
                        start=(k == 0), stop=(k == NKB - 1)),
                        inc=(T_res if kb == NKB - 1 else None))
                r_rwb.release(m, T_res)

            def emit_evacxi(m):
                r_xip.acquire(S, "sc", m)
                S.w("sc", T_ip)
                S.o("sc", lambda e, mm=m: e.copy(xip_sb[mm % 2][:, 3:3 + TC],
                                                 ps_ip[mm % 2][:]))
                S.fence("sc", inc=A_evxi)
                r_psip.release(m, A_evxi)

            def emit_conv(m):
                r_cacc.acquire(S, "ve", m)
                S.w("ve", A_evxi, aevxi0 + m + 1)
                S.o("ve", lambda e, mm=m: e.tensor_copy(
                    xip_sb[mm % 2][:, 0:3],
                    halo_sb[:, mm * 3:(mm + 1) * 3]))
                S.o("ve", lambda e, mm=m: e.tensor_scalar(
                    cacc[mm % 2][:], xip_sb[mm % 2][:, 0:TC], cw_sb[:, mm * 4:mm * 4 + 1],
                    None, op0=ALU.mult))
                for k in range(1, 4):
                    S.o("ve", lambda e, mm=m, kk=k: e.scalar_tensor_tensor(
                        cacc[mm % 2][:], xip_sb[mm % 2][:, kk:kk + TC],
                        cw_sb[:, mm * 4 + kk:mm * 4 + kk + 1], cacc[mm % 2][:],
                        op0=ALU.mult, op1=ALU.add))
                S.o("ve", lambda e, mm=m: e.tensor_copy(
                    halo_sb[:, mm * 3:(mm + 1) * 3],
                    xip_sb[mm % 2][:, TC:TC + 3]))
                S.fence("ve", inc=V_conv)
                r_xip.release(m, V_conv)

            def emit_silu(m):
                S.w("sc", V_conv)
                S.o("sc", lambda e, mm=m: e.activation(
                    xi_sb[:, mm * TC:(mm + 1) * TC], cacc[mm % 2][:], AF.Silu,
                    bias=cb_sb[:, mm:mm + 1]), inc=A_silu)
                r_cacc.release(m, A_silu)

            def emit_silures(m):
                S.w("sc", T_res, tres0 + m + 1)
                S.o("sc", lambda e, mm=m: e.activation(
                    yf_sb[:, mm * TC:(mm + 1) * TC], ps_dp[mm % 2][:], AF.Silu),
                    inc=A_res)
                r_psdp.release(m, A_res)

            aevxi0 = A_evxi.n
            tres0 = T_res.n
            for m in range(NM):
                emit_ip(m)
                if m >= 1:
                    emit_res(m - 1)
                emit_evacxi(m)
                if m >= 1:
                    emit_conv(m - 1)
                    emit_silu(m - 1)
                if m >= 2:
                    emit_silures(m - 2)
            emit_res(NM - 1)
            emit_conv(NM - 1)
            emit_silu(NM - 1)
            emit_silures(NM - 2)
            emit_silures(NM - 1)

            if PH < 5:
                continue
            r_xqT.acquire(S, "te", 0)
            for kb in range(NKB):
                r_pstr.acquire(S, "te", kb)
                if kb == 0:
                    S.w("te", V_q)
                for ti in range(NTI):
                    S.o("te", lambda e, t=ti, k=kb: e.transpose(
                        ps_tr[k % 2][:, t * 128:(t + 1) * 128],
                        xq_sb[:, t * DM + k * 128: t * DM + (k + 1) * 128], eye_sb[:]),
                        inc=(T_trq if ti == NTI - 1 else None))
                S.w("sc", T_trq)
                S.o("sc", lambda e, k=kb: e.copy(
                    xqT_sb[:, k * TC:(k + 1) * TC],
                    ps_tr[k % 2][:, 0:512]), inc=A_evxq)
                r_pstr.release(kb, A_evxq)
            r_xq.release(0, T_trq)

            r_psip.acquire(S, "te", 16)
            r_dinT.acquire(S, "sc", 0)
            S.w("te", A_silu)
            for m in range(NM):
                S.o("te", lambda e, mm=m: e.matmul(
                    ps_ip[0][0:96, :], xpw_sb[:, mm * 96:(mm + 1) * 96],
                    xi_sb[:, mm * TC:(mm + 1) * TC],
                    start=(mm == 0), stop=(mm == NM - 1)),
                    inc=(T_xdbl if m == NM - 1 else None))
            S.w("sc", T_xdbl)
            S.o("sc", lambda e: e.copy(dinT_sb[:], ps_ip[0][0:DR, :]))
            S.fence("sc", inc=A_dinT)
            r_bcs.acquire(S, "ve", 0)
            S.w("ve", T_xdbl)
            S.w("ve", A_dinT)
            S.o("ve", lambda e: e.tensor_copy(bc_sb[DR:96, :], ps_ip[0][DR:96, :]))
            S.fence("ve", inc=V_bc)
            r_psip.release(16, A_dinT)
            r_psip.slots[0].append((V_bc, V_bc.n))
            S.w("gp", V_bc)
            S.o("gp", lambda e: e.dma_start(out=d_bc, in_=bc_sb[DR:96, :]), inc=L_bcd, dma=True)
            r_bcs.release(0, L_bcd)
            r_rep.acquire(S, "gp", 0)
            S.w("gp", L_bcd)
            for g in range(2):
                for j in range(8):
                    S.o("gp", lambda e, gg=g, jj=j: e.dma_start(
                        out=brep[gg][:, jj * TC:(jj + 1) * TC],
                        in_=d_bc[gg * 8 + jj:gg * 8 + jj + 1, :].broadcast_to([128, TC])),
                        inc=L_rep, dma=True)
                    S.o("gp", lambda e, gg=g, jj=j: e.dma_start(
                        out=crep[gg][:, jj * TC:(jj + 1) * TC],
                        in_=d_bc[16 + gg * 8 + jj:16 + gg * 8 + jj + 1, :].broadcast_to([128, TC])),
                        inc=L_rep, dma=True)
            r_dT.acquire(S, "sc", 0)
            for m in range(NM):
                S.w("sy", T_dt, max(0, T_dt.n - 1))
                S.o("sy", lambda e, mm=m: e.dma_start(
                    out=dtbuf[mm % 2][:], in_=d_dtw[:, mm * 128:(mm + 1) * 128]),
                    inc=L_dtb[m % 2], dma=True)
                r_psip.acquire(S, "te", 17 + m)
                S.w("te", L_dtb[m % 2])
                if m == 0:
                    S.w("te", A_dinT)
                S.o("te", lambda e, mm=m: e.matmul(
                    ps_ip[(17 + mm) % 2][:], dtbuf[mm % 2][:],
                    dinT_sb[:], start=True, stop=True), inc=T_dt)
                r_cacc.acquire(S, "sc", 17 + m)
                S.w("sc", T_dt)
                S.o("sc", lambda e, mm=m: e.activation(
                    cacc[(17 + mm) % 2][:], ps_ip[(17 + mm) % 2][:], AF.Exp,
                    bias=dtb_sb[:, mm:mm + 1]), fence=True)
                r_psip.release(17 + m, A_sp, A_sp.n + 1)
                S.o("sc", lambda e, mm=m: e.activation(
                    deltaT_sb[:, mm * TC:(mm + 1) * TC], cacc[(17 + mm) % 2][:],
                    AF.Ln, bias=1.0), inc=A_sp)
                r_cacc.release(17 + m, A_sp)
            if PH < 6:
                continue
            vy0 = V_y.n

            def emit_F_loads(half):
                gi0 = ci * 8 + half * NTI
                v_base = V_oev.n
                tgt = []
                for ti in range(NTI):
                    gi2 = gi0 + ti
                    r_xld.acquire(S, "sc", 500 + gi2)
                    src = ap_x[b, t0 + ti * 128: t0 + (ti + 1) * 128,
                               half * 512:(half + 1) * 512]
                    S.o("sc", lambda e, s=src, d=x_ld[gi2 % 2]:
                        e.dma_start(out=d[:, 0:512], in_=s), inc=L_xld[gi2 % 2], dma=True)
                    tgt.append(L_xld[gi2 % 2].n)
                    r_xld.release(500 + gi2, V_oev, v_base + ti + 1)
                return tgt

            slot6 = [(0, 0, ps_a[0]), (0, 1, ps_a[1]), (0, 2, ps_ip[0]),
                     (0, 3, ps_ip[1]), (1, 0, ps_dp[0]), (1, 1, ps_dp[1])]

            def emit_F_p1():
                tout0 = T_out.n
                for kb in range(NM):
                    r_opb.acquire(S, "sy", kb)
                    src = d_opbf[kb * 128:(kb + 1) * 128, :]
                    S.o("sy", lambda e, s=src, d=opbuf[kb % 2]: e.dma_start(out=d[:], in_=s),
                        inc=L_opb[kb % 2], dma=True)
                    S.w("te", L_opb[kb % 2])
                    S.w("te", V_y, vy0 + kb + 1)
                    if kb == 0:
                        r_psa.acquire(S, "te", 0)
                        r_psa.acquire(S, "te", 1)
                        r_psip.acquire(S, "te", 40)
                        r_psip.acquire(S, "te", 41)
                        r_psdp.acquire(S, "te", 48)
                        r_psdp.acquire(S, "te", 49)
                    for si, (h, t, ps) in enumerate(slot6):
                        S.o("te", lambda e, k=kb, hh=h, tt=t, p=ps: e.matmul(
                            p[:],
                            yf_sb[:, k * TC + tt * 128: k * TC + (tt + 1) * 128],
                            opbuf[k % 2][:, hh * 512:(hh + 1) * 512],
                            start=(k == 0), stop=False),
                            inc=(T_opk if si == 5 else None))
                    r_opb.release(kb, T_opk)
                for ib in range(NKB):
                    r_ternb.acquire(S, "sy", ib)
                    src = d_ternbf[ib * 128:(ib + 1) * 128, :]
                    S.o("sy", lambda e, s=src, d=ternbuf[ib % 2]: e.dma_start(out=d[:], in_=s),
                        inc=L_ternb[ib % 2], dma=True)
                    S.w("te", L_ternb[ib % 2])
                    if ib == 0:
                        S.w("te", A_evxq)
                    for si, (h, t, ps) in enumerate(slot6):
                        S.o("te", lambda e, k=ib, hh=h, tt=t, p=ps: e.matmul(
                            p[:],
                            xqT_sb[:, k * TC + tt * 128: k * TC + (tt + 1) * 128],
                            ternbuf[k % 2][:, hh * 512:(hh + 1) * 512],
                            start=False, stop=(k == NKB - 1)),
                            inc=((T_tek if si == 5 else None) if ib < NKB - 1
                                 else T_out))
                    r_ternb.release(ib, T_tek if ib < NKB - 1 else T_out)
                return tout0

            def emit_F_p2():
                for kb in range(NM):
                    r_opb.acquire(S, "sy", 16 + kb)
                    src = d_opbf[kb * 128:(kb + 1) * 128, 512:1024]
                    S.o("sy", lambda e, s=src, d=opbuf[kb % 2]: e.dma_start(
                        out=d[:, 0:512], in_=s), inc=L_opb[kb % 2], dma=True)
                    S.w("te", L_opb[kb % 2])
                    if kb == 0:
                        r_psa.acquire(S, "te", 2)
                        r_psa.acquire(S, "te", 3)
                    for si, t in enumerate((2, 3)):
                        S.o("te", lambda e, k=kb, tt=t, p=ps_a[si]: e.matmul(
                            p[:],
                            yf_sb[:, k * TC + tt * 128: k * TC + (tt + 1) * 128],
                            opbuf[k % 2][:, 0:512], start=(k == 0), stop=False),
                            inc=(T_opk if si == 1 else None))
                    r_opb.release(16 + kb, T_opk)
                for ib in range(NKB):
                    r_ternb.acquire(S, "sy", 16 + ib)
                    src = d_ternbf[ib * 128:(ib + 1) * 128, 512:1024]
                    S.o("sy", lambda e, s=src, d=ternbuf[ib % 2]: e.dma_start(
                        out=d[:, 0:512], in_=s), inc=L_ternb[ib % 2], dma=True)
                    S.w("te", L_ternb[ib % 2])
                    for si, t in enumerate((2, 3)):
                        S.o("te", lambda e, k=ib, tt=t, p=ps_a[si]: e.matmul(
                            p[:],
                            xqT_sb[:, k * TC + tt * 128: k * TC + (tt + 1) * 128],
                            ternbuf[k % 2][:, 0:512], start=False, stop=(k == NKB - 1)),
                            inc=((T_tek if si == 1 else None) if ib < NKB - 1
                                 else T_out))
                    r_ternb.release(16 + ib, T_tek if ib < NKB - 1 else T_out)

            def emit_F_evacs(items, tout0, xre_tgt):
                for (oi, h, t, ps, rel) in items:
                    gi2 = ci * 8 + h * NTI + t
                    r_out.acquire(S, "ve", gi2)
                    S.w("ve", T_out, tout0 + oi + 1)
                    S.w("ve", L_xld[gi2 % 2], xre_tgt[t])
                    S.o("ve", lambda e, p=ps, g2=gi2: e.tensor_add(
                        out_sb[g2 % 2][:], p[:], x_ld[g2 % 2][:, 0:512]))
                    S.fence("ve", inc=V_oev)
                    rel()
                    dst = ap_out[b, t0 + t * 128: t0 + (t + 1) * 128,
                                 h * 512:(h + 1) * 512]
                    S.w("gp", V_oev)
                    S.o("gp", lambda e, d=dst, g2=gi2: e.dma_start(out=d, in_=out_sb[g2 % 2][:]),
                        inc=S_os[gi2 % 2], dma=True)
                    r_out.release(gi2, S_os[gi2 % 2])


            tdp_tgt = {}
            vsu_tgt = {}

            def emit_dp(m):
                r_adjb.acquire(S, "sy", 16 + m)
                if ci == 0 and m == 0:
                    for (ps_sem, ps_t) in PRO_WR:
                        S.w("sy", ps_sem, ps_t)
                src = d_adjbf[m].rearrange("p a b -> p (a b)")
                S.o("sy", lambda e, s=src, d=adjbuf[m % 2]: e.dma_start(out=d[:], in_=s),
                    inc=L_adjb[m % 2], dma=True)
                r_psdp.acquire(S, "te", 16 + m)
                S.w("te", L_adjb[m % 2])
                if m == 0:
                    S.w("te", A_sp)
                for kb in range(NM):
                    S.o("te", lambda e, k=kb, mm=m: e.matmul(
                        ps_dp[mm % 2][:], adjbuf[mm % 2][:, k * 128:(k + 1) * 128],
                        deltaT_sb[:, k * TC:(k + 1) * TC],
                        start=(k == 0), stop=(k == NM - 1)),
                        inc=(T_dp if kb == NM - 1 else None))
                tdp_tgt[m] = T_dp.n
                r_adjb.release(16 + m, T_dp)

            def emit_evac(m):
                r_cacc.acquire(S, "ve", 70 + 16 * ci + m)
                S.w("ve", T_dp, tdp_tgt[m])
                S.o("ve", lambda e, mm=m: e.tensor_copy(cacc[mm % 2][:], ps_dp[mm % 2][:]))
                S.fence("ve", inc=V_su)
                vsu_tgt[m] = V_su.n
                r_psdp.release(16 + m, V_su)

            def emit_E(m):
                r_su.acquire(S, "ve", m)
                S.o("ve", lambda e, mm=m: e.tensor_mul(
                    su_sb[mm % 2][:], cacc[mm % 2][:], xi_sb[:, mm * TC:(mm + 1) * TC]))
                for g in range(2):
                    albuf = [al_sb, xq_sb][(2 * m + g) % 2]
                    r_al.acquire(S, "sc", 2 * m + g)
                    if ci == 0 and m == 0:
                        S.w("sc", T_trq)
                    if g == 0:
                        S.w("sc", V_su, vsu_tgt[m])
                    for j in range(8):
                        n = g * 8 + j
                        S.o("sc", lambda e, ab=albuf, mm=m, nn=n, jj=j: e.activation(
                            ab[:, jj * TC:(jj + 1) * TC], cacc[mm % 2][:], AF.Exp,
                            scale=a_sb[:, mm * NST + nn:mm * NST + nn + 1]),
                            inc=(A_al if j == 7 else None))
                    if g == 1:
                        r_cacc.release(70 + 16 * ci + m, A_al)
                    if m == 0 and g == 0:
                        S.w("ve", L_rep)
                    su_b = su_sb[m % 2][:].unsqueeze(1).broadcast_to([128, 8, TC])
                    bv = brep[g][:].rearrange("p (j t) -> p j t", j=8)
                    wvv = wv_sb[:].rearrange("p (j t) -> p j t", j=8)
                    S.o("ve", lambda e, sb_=su_b, b_=bv, w_=wvv: e.tensor_tensor(
                        w_, sb_, b_, op=ALU.mult), inc=V_w)
                    if g == 1:
                        r_su.release(m, V_w)
                    S.w("ve", A_al)
                    for j in range(8):
                        n = g * 8 + j
                        S.o("ve", lambda e, ab=albuf, mm=m, nn=n, jj=j: e.tensor_tensor_scan(
                            hh_sb[:, jj * TC:(jj + 1) * TC],
                            ab[:, jj * TC:(jj + 1) * TC],
                            wv_sb[:, jj * TC:(jj + 1) * TC],
                            hcarry[:, mm * NST + nn:mm * NST + nn + 1],
                            op0=ALU.mult, op1=ALU.add),
                            inc=(V_scan if j == 7 else None))
                    r_al.release(2 * m + g, V_scan)
                    S.fence("ve")
                    S.o("ve", lambda e, mm=m, gg=g: e.tensor_copy(
                        hcarry[:, mm * NST + gg * 8: mm * NST + gg * 8 + 8],
                        hh_sb[:, TC - 1::TC]))
                    S.o("ve", lambda e, gg=g: e.tensor_mul(wv_sb[:], hh_sb[:], crep[gg][:]))
                    S.o("ve", lambda e: e.tensor_add(
                        wv_sb[:, 0:4 * TC], wv_sb[:, 0:4 * TC], wv_sb[:, 4 * TC:8 * TC]))
                    S.o("ve", lambda e: e.tensor_add(
                        wv_sb[:, 4 * TC:6 * TC], wv_sb[:, 0:2 * TC], wv_sb[:, 2 * TC:4 * TC]))
                    S.o("ve", lambda e, gg=g: e.tensor_add(
                        yg_sb[:, gg * TC:(gg + 1) * TC], wv_sb[:, 4 * TC:5 * TC],
                        wv_sb[:, 5 * TC:6 * TC]))
                S.fence("ve")
                S.o("ve", lambda e: e.tensor_add(
                    yg_sb[:, 0:TC], yg_sb[:, 0:TC], yg_sb[:, TC:2 * TC]), fence=True)
                S.o("ve", lambda e, mm=m: e.scalar_tensor_tensor(
                    t2_sb[:], xi_sb[:, mm * TC:(mm + 1) * TC], dd_sb[:, mm:mm + 1],
                    yg_sb[:, 0:TC], op0=ALU.mult, op1=ALU.add), fence=True)
                S.w("ve", A_res)
                S.o("ve", lambda e, mm=m: e.tensor_mul(
                    yf_sb[:, mm * TC:(mm + 1) * TC], yf_sb[:, mm * TC:(mm + 1) * TC],
                    t2_sb[:]))
                S.fence("ve", inc=V_y)

            for m in range(NM):
                emit_dp(m)
                emit_evac(m)
                if m >= 1:
                    emit_E(m - 1)
            if PH >= 7:
                tout_0 = emit_F_p1()
            emit_E(NM - 1)
            if PH >= 7:
                xre0 = emit_F_loads(0)
                emit_F_evacs(
                    [(0, 0, 0, ps_a[0], lambda: r_psa.release(0, V_oev)),
                     (1, 0, 1, ps_a[1], lambda: r_psa.release(1, V_oev)),
                     (2, 0, 2, ps_ip[0],
                      lambda: r_psip.slots[0].append((V_oev, V_oev.n))),
                     (3, 0, 3, ps_ip[1],
                      lambda: r_psip.slots[1].append((V_oev, V_oev.n)))],
                    tout_0, xre0)
                xre1 = emit_F_loads(1)
                emit_F_evacs(
                    [(4, 1, 0, ps_dp[0],
                      lambda: r_psdp.slots[0].append((V_oev, V_oev.n))),
                     (5, 1, 1, ps_dp[1],
                      lambda: r_psdp.slots[1].append((V_oev, V_oev.n)))],
                    tout_0, xre1)
                emit_F_p2()
                emit_F_evacs(
                    [(6, 1, 2, ps_a[0], lambda: r_psa.release(2, V_oev)),
                     (7, 1, 3, ps_a[1], lambda: r_psa.release(3, V_oev))],
                    tout_0, xre1)

            r_rep.release(0, V_y)
            r_xi.release(0, V_y)
            r_dT.release(0, T_dp)
            r_dinT.release(0, T_dt)
            r_xnT.release(0, T_res)
            r_xqT.release(0, T_out)
            r_yf.release(0, T_out)
        if DBG:
            dbg_src = {"d_xn": xn_sb, "d_xnT": xnT_sb, "d_xi": xi_sb,
                       "d_dT": deltaT_sb, "d_yf": yf_sb, "d_xq": xq_sb,
                       "d_hc": hcarry, "d_brep": brep[0], "d_crep": crep[0],
                       "d_asb": a_sb, "d_su": su_sb[1], "d_al": al_sb,
                       "d_hh": hh_sb, "d_wv": wv_sb, "d_stat": stat,
                       "d_ln": ln_snap}
            dsem = S_("dsem"); dwr = S_("dwr")
            S.w("ve", S_os[0], S_os[0].n)
            S.w("ve", S_os[1], S_os[1].n)
            for nm, t in dbg_src.items():
                cols = t.ap().shape[1]
                for p0 in range(0, cols, 1024):
                    pc = min(1024, cols - p0)
                    S.w("ve", dwr, dwr.n)
                    S.o("ve", lambda e, tt=t, a=p0, c=pc: e.tensor_copy(
                        x_ld[0][:, 0:c], tt[:, a:a + c]), inc=dsem)
                    S.w("gp", dsem, dsem.n)
                    S.o("gp", lambda e, nm_=nm, a=p0, c=pc: e.dma_start(
                        out=dbg_aps[nm_][:, a:a + c], in_=x_ld[0][:, 0:c]),
                        inc=dwr, dma=True)
            S.w("gp", dwr, dwr.n)

        S.w("gp", S_os[0], S_os[0].n)
        S.w("gp", S_os[1], S_os[1].n)

        block = stack.enter_context(nc.Block())
        block.sync(lambda e: S.replay("sy", e))
        block.gpsimd(lambda e: S.replay("gp", e))
        block.vector(lambda e: S.replay("ve", e))
        block.scalar(lambda e: S.replay("sc", e))
        block.tensor(lambda e: S.replay("te", e))

    return nc


_CACHE = {}


def _hadamard(size):
    h = np.array([[1.0]], dtype=np.float64)
    while h.shape[0] < size:
        h = np.block([[h, h], [h, -h]])
    return (h / np.sqrt(size)).astype(np.float32)


def kernel(L0_attn_x, L0_attn_adj, L0_attn_in_proj_w, L0_attn_conv_w, L0_attn_conv_b,
           L0_attn_x_proj_w, L0_attn_dt_proj_w, L0_attn_dt_proj_b, L0_attn_A_log,
           L0_attn_D, L0_attn_out_proj_w, L0_ffn_weight):
    import ml_dtypes
    x = np.ascontiguousarray(np.asarray(L0_attn_x, dtype=np.float32))
    adj = np.asarray(L0_attn_adj, np.float32)
    W = np.asarray(L0_attn_in_proj_w, np.float32)
    adj_pm = np.ascontiguousarray(adj.reshape(16, 128, 16, 128).transpose(2, 1, 0, 3))
    w_pm = np.ascontiguousarray(W.reshape(8, 128, 32, 128).transpose(2, 1, 0, 3))
    op_w = np.ascontiguousarray(np.asarray(L0_attn_out_proj_w, np.float32))
    xp_pm = np.ascontiguousarray(
        np.asarray(L0_attn_x_proj_w, np.float32).reshape(16, 128, 96)
        .transpose(1, 0, 2).reshape(128, 16 * 96))
    dt_w = np.ascontiguousarray(np.asarray(L0_attn_dt_proj_w, np.float32))
    a_pm = np.ascontiguousarray(
        np.asarray(L0_attn_A_log, np.float32).reshape(16, 128, 16)
        .transpose(1, 0, 2).reshape(128, 256))
    convw_pm = np.ascontiguousarray(
        np.asarray(L0_attn_conv_w, np.float32)[:, 0, :].reshape(16, 128, 4)
        .transpose(1, 0, 2).reshape(128, 64))
    convb_pm = np.ascontiguousarray(np.asarray(L0_attn_conv_b, np.float32).reshape(16, 128).T)
    dtb_pm = np.ascontiguousarray(np.asarray(L0_attn_dt_proj_b, np.float32).reshape(16, 128).T)
    d_pm = np.ascontiguousarray(np.asarray(L0_attn_D, np.float32).reshape(16, 128).T)
    ffn_w = np.ascontiguousarray(np.asarray(L0_ffn_weight, np.float32))
    h_bf = np.ascontiguousarray(_hadamard(DM).astype(ml_dtypes.bfloat16))
    eye_bf = np.ascontiguousarray(np.eye(128, dtype=np.float32).astype(ml_dtypes.bfloat16))

    if "nc" not in _CACHE:
        _CACHE["nc"] = build_nc()
    nc = _CACHE["nc"]

    shared = dict(adj_pm=adj_pm, w_pm=w_pm, op_w=op_w, xp_pm=xp_pm, dt_w=dt_w,
                  a_pm=a_pm, convw_pm=convw_pm, convb_pm=convb_pm, dtb_pm=dtb_pm,
                  d_pm=d_pm, ffn_w=ffn_w, h_bf=h_bf, eye_bf=eye_bf)
    in_maps = []
    for c in range(NCORE):
        m = dict(shared)
        m["x"] = np.ascontiguousarray(x[c * BC:(c + 1) * BC])
        in_maps.append(m)
    res = run_bass_kernel_spmd(nc, in_maps, core_ids=list(range(NCORE)))
    out = np.concatenate([r["out"] for r in res.results], axis=0)
    return out.astype(np.float32)


# version 12
# speedup vs baseline: 1.2690x; 1.0200x over previous
import contextlib
import numpy as np
import concourse.bass as bass
import concourse.mybir as mybir
from concourse.bass_utils import run_bass_kernel_spmd

F32 = mybir.dt.float32
BF16 = mybir.dt.bfloat16
AF = mybir.ActivationFunctionType
ALU = mybir.AluOpType

NCORE = 8
BC = 2
L = 2048
DM = 1024
DI = 2048
NST = 16
DR = 64
TC = 512
NCH = BC * L // TC
NTI = TC // 128
NM = DI // 128
NKB = DM // 128
RNE_C = float(3 * 2.0 ** 22)
EPS = 1e-5

ENGS = ("sy", "gp", "ve", "sc", "te")

DEV_DRAIN = False
PH = 7
PHE = 6
ALV = 0
DBG = False


class Sem:
    def __init__(self, stack, nc, name):
        self.h = stack.enter_context(nc.semaphore(name))
        self.n = 0
        self.name = name


class Sched:

    def __init__(self):
        self.q = {e: [] for e in ENGS}

    def o(self, eng, fn, inc=None, dma=False, fence=False):
        incs = []
        if inc is not None:
            if not isinstance(inc, list):
                inc = [inc]
            for s in inc:
                k = 16 if dma else 1
                s.n += k
                incs.append((s, k))
        self.q[eng].append(("op", fn, incs))
        if fence:
            self.fence(eng)

    def fence(self, eng, inc=None):
        incs = []
        if inc is not None:
            if not isinstance(inc, list):
                inc = [inc]
            for s in inc:
                s.n += 1
                incs.append((s, 1))
        jb = self.junk[eng]
        if eng == "sc":
            self.q[eng].append(("op", lambda e: e.copy(jb[:, 0:1], jb[:, 2:3]), []))
            self.q[eng].append(("op", lambda e: e.copy(jb[:, 1:2], jb[:, 3:4]), incs))
        else:
            self.q[eng].append(("op", lambda e: e.tensor_copy(jb[:, 0:1], jb[:, 2:3]), []))
            self.q[eng].append(("op", lambda e: e.tensor_copy(jb[:, 1:2], jb[:, 3:4]), incs))

    def w(self, eng, sem, target=None):
        t = sem.n if target is None else target
        self.q[eng].append(("wait", sem, t))

    def replay(self, eng_name, eng):
        for item in self.q[eng_name]:
            if item[0] == "op":
                ins = item[1](eng)
                for (s, k) in item[2]:
                    ins.then_inc(s.h, k)
                if DEV_DRAIN and eng_name in ("ve", "sc", "te"):
                    eng.drain()
            else:
                eng.wait_ge(item[1].h, item[2])


class Ring:

    def __init__(self, n):
        self.n = n
        self.slots = [[] for _ in range(n)]

    def acquire(self, S, eng, idx):
        for (sem, t) in self.slots[idx % self.n]:
            S.w(eng, sem, t)
        self.slots[idx % self.n] = []

    def release(self, idx, sem, target=None):
        self.slots[idx % self.n].append((sem, sem.n if target is None else target))


def build_nc():
    nc = bass.Bass("TRN2", target_bir_lowering=False,
                   detect_race_conditions=DEV_DRAIN)

    ap_x = nc.dram_tensor("x", [BC, L, DM], F32, kind="ExternalInput").ap()
    ap_adj = nc.dram_tensor("adj_pm", [NM, 128, NM, 128], F32, kind="ExternalInput").ap()
    ap_w = nc.dram_tensor("w_pm", [2 * NM, 128, NKB, 128], F32, kind="ExternalInput").ap()
    ap_op = nc.dram_tensor("op_w", [DI, DM], F32, kind="ExternalInput").ap()
    ap_xp = nc.dram_tensor("xp_pm", [128, NM * 96], F32, kind="ExternalInput").ap()
    ap_dtw = nc.dram_tensor("dt_w", [DR, DI], F32, kind="ExternalInput").ap()
    ap_apm = nc.dram_tensor("a_pm", [128, NM * NST], F32, kind="ExternalInput").ap()
    ap_cw = nc.dram_tensor("convw_pm", [128, NM * 4], F32, kind="ExternalInput").ap()
    ap_cb = nc.dram_tensor("convb_pm", [128, NM], F32, kind="ExternalInput").ap()
    ap_dtb = nc.dram_tensor("dtb_pm", [128, NM], F32, kind="ExternalInput").ap()
    ap_dd = nc.dram_tensor("d_pm", [128, NM], F32, kind="ExternalInput").ap()
    ap_fw = nc.dram_tensor("ffn_w", [DM, DM], F32, kind="ExternalInput").ap()
    ap_h = nc.dram_tensor("h_bf", [DM, DM], BF16, kind="ExternalInput").ap()
    ap_eye = nc.dram_tensor("eye_bf", [128, 128], BF16, kind="ExternalInput").ap()

    ap_out = nc.dram_tensor("out", [BC, L, DM], F32, kind="ExternalOutput").ap()
    dbg_aps = {}
    if DBG:
        for nm, shape in [("d_xn", [128, NTI * DM]), ("d_xnT", [128, NKB * TC]),
                          ("d_xi", [128, NM * TC]), ("d_dT", [128, NM * TC]),
                          ("d_yf", [128, NM * TC]), ("d_xq", [128, NTI * DM]),
                          ("d_hc", [128, NM * NST]), ("d_brep", [128, 8 * TC]),
                          ("d_crep", [128, 8 * TC]), ("d_asb", [128, NM * NST]),
                          ("d_su", [128, TC]), ("d_al", [128, 8 * TC]),
                          ("d_hh", [128, 8 * TC]), ("d_wv", [128, 8 * TC]),
                          ("d_stat", [128, 16]), ("d_ln", [128, 16])]:
            dbg_aps[nm] = nc.dram_tensor(nm, shape, F32, kind="ExternalOutput").ap()
    d_adjbf = nc.dram_tensor("adj_bf", [NM, 128, NM, 128], BF16, kind="Internal").ap()
    d_wbf = nc.dram_tensor("w_bf", [2 * NM, 128, NKB, 128], BF16, kind="Internal").ap()
    d_opbf = nc.dram_tensor("op_bf", [DI, DM], BF16, kind="Internal").ap()
    d_ternbf = nc.dram_tensor("tern_bf", [DM, DM], BF16, kind="Internal").ap()
    d_dtw = nc.dram_tensor("dtw_bf", [DR, DI], BF16, kind="Internal").ap()
    d_bc = nc.dram_tensor("bc_scr", [32, TC], BF16, kind="Internal").ap()

    with contextlib.ExitStack() as stack:
        def sb(name, shape, dtype):
            return stack.enter_context(nc.sbuf_tensor(name, shape, dtype))

        def ps_(name, shape, dtype):
            return stack.enter_context(nc.psum_tensor(name, shape, dtype))

        eye_sb = sb("eye_sb", [128, 128], BF16)
        xpw_sb = sb("xpw_sb", [128, NM * 96], BF16)
        dtbuf = [sb(f"dtbuf{i}", [DR, 128], BF16) for i in range(2)]
        a_sb = sb("a_sb", [128, NM * NST], F32)
        cw_sb = sb("cw_sb", [128, NM * 4], F32)
        cb_sb = sb("cb_sb", [128, NM], F32)
        dtb_sb = sb("dtb_sb", [128, NM], F32)
        dd_sb = sb("dd_sb", [128, NM], F32)
        hcarry = sb("hcarry", [128, NM * NST], F32)
        halo_sb = sb("halo_sb", [128, NM * 3], F32)
        stat = sb("stat", [128, 16], F32)
        cst = sb("cst", [128, 2], F32)
        junk_v = sb("junk_v", [128, 4], F32)
        junk_s = sb("junk_s", [128, 4], F32)

        ln_snap = sb("ln_snap", [128, 16], F32)
        x_ld = [sb(f"x_ld{i}", [128, DM], F32) for i in range(2)]
        xn_sb = sb("xn_sb", [128, NTI * DM], BF16)
        xnT_sb = sb("xnT_sb", [128, NKB * TC], BF16)
        xq_sb = sb("xq_sb", [128, NTI * DM], BF16)
        xqT_sb = sb("xqT_sb", [128, NKB * TC], BF16)
        hbuf = sb("hbuf", [128, NKB * DM], BF16)
        wibuf = [sb(f"wibuf{i}", [128, NKB * 128], BF16) for i in range(2)]
        rwbuf = [sb(f"rwbuf{i}", [128, NKB * 128], BF16) for i in range(2)]
        adjbuf = [sb(f"adjbuf{i}", [128, DI], BF16) for i in range(2)]
        opbuf = [sb(f"opbuf{i}", [128, 1024], BF16) for i in range(2)]
        ternbuf = [sb(f"ternbuf{i}", [128, 1024], BF16) for i in range(2)]
        xip_sb = [sb(f"xip{i}", [128, TC + 3], BF16) for i in range(2)]
        cacc = [sb(f"cacc{i}", [128, TC], F32) for i in range(2)]
        xi_sb = sb("xi_sb", [128, NM * TC], BF16)
        dinT_sb = sb("dinT_sb", [DR, TC], BF16)
        deltaT_sb = sb("deltaT_sb", [128, NM * TC], BF16)
        su_sb = [sb(f"su{i}", [128, TC], BF16) for i in range(2)]
        brep = [sb(f"brep{g}", [128, 8 * TC], BF16) for g in range(2)]
        crep = [sb(f"crep{g}", [128, 8 * TC], BF16) for g in range(2)]
        al_sb = sb("al_sb", [128, 8 * TC], BF16)
        wv_sb = sb("wv_sb", [128, 8 * TC], BF16)
        hh_sb = sb("hh_sb", [128, 8 * TC], BF16)
        yg_sb = sb("yg_sb", [128, 2 * TC], BF16)
        t2_sb = sb("t2_sb", [128, TC], BF16)
        yf_sb = sb("yf_sb", [128, NM * TC], BF16)
        out_sb = [sb(f"out_sb{i}", [128, 512], F32) for i in range(2)]
        bc_sb = sb("bc_sb", [96, TC], BF16)

        ps_tr = [ps_(f"ps_tr{i}", [128, 1024], BF16) for i in range(2)]
        ps_a = [ps_(f"ps_a{i}", [128, 512], F32) for i in range(2)]
        ps_ip = [ps_(f"ps_ip{i}", [128, 512], F32) for i in range(2)]
        ps_dp = [ps_(f"ps_dp{i}", [128, 512], F32) for i in range(2)]

        sems = {}

        def S_(name):
            sems[name] = Sem(stack, nc, name)
            return sems[name]

        L_xld = [S_(f"L_xld{i}") for i in range(2)]
        L_wib = [S_(f"L_wib{i}") for i in range(2)]
        L_rwb = [S_(f"L_rwb{i}") for i in range(2)]
        L_adjb = [S_(f"L_adjb{i}") for i in range(2)]
        L_opb = [S_(f"L_opb{i}") for i in range(2)]
        L_ternb = [S_(f"L_ternb{i}") for i in range(2)]
        L_dtb = [S_(f"L_dtb{i}") for i in range(2)]
        P_wib = [S_(f"P_wib{i}") for i in range(2)]
        P_adjb = [S_(f"P_adjb{i}") for i in range(2)]
        L_hb = S_("L_hb"); L_small = S_("L_small"); L_h = S_("L_h")
        L_bcd = S_("L_bcd"); L_rep = S_("L_rep")
        S_os = [S_(f"S_os{i}") for i in range(2)]
        T_trn = S_("T_trn"); T_trq = S_("T_trq"); T_xh = S_("T_xh")
        T_ip = S_("T_ip"); T_res = S_("T_res"); T_xdbl = S_("T_xdbl")
        T_dt = S_("T_dt"); T_dp = S_("T_dp"); T_opk = S_("T_opk")
        T_tek = S_("T_tek"); T_out = S_("T_out"); T_ptr = S_("T_ptr")
        V_xn = S_("V_xn"); V_q = S_("V_q"); V_conv = S_("V_conv"); V_bc = S_("V_bc")
        V_su = S_("V_su"); V_w = S_("V_w"); V_scan = S_("V_scan"); V_y = S_("V_y")
        V_oev = S_("V_oev"); V_cast = S_("V_cast"); V_pt = S_("V_pt")
        A_rstd = S_("A_rstd"); A_evxn = S_("A_evxn"); A_evxq = S_("A_evxq")
        A_evxi = S_("A_evxi"); A_silu = S_("A_silu"); A_res = S_("A_res")
        A_dinT = S_("A_dinT"); A_sp = S_("A_sp"); A_al = S_("A_al"); A_pev = S_("A_pev")

        S = Sched()
        S.junk = {"ve": junk_v, "sc": junk_s}

        r_xld = Ring(2); r_wib = Ring(2); r_rwb = Ring(2)
        r_adjb = Ring(2); r_opb = Ring(2); r_ternb = Ring(2)
        r_xip = Ring(2); r_cacc = Ring(2); r_su = Ring(2); r_out = Ring(2)
        r_pstr = Ring(2)
        r_psa = Ring(2); r_psip = Ring(2); r_psdp = Ring(2)
        r_xn = Ring(1); r_xnT = Ring(1); r_xq = Ring(1); r_xqT = Ring(1)
        r_xi = Ring(1); r_dT = Ring(1); r_dinT = Ring(1); r_yf = Ring(1)
        r_al = Ring(2); r_rep = Ring(1); r_bcs = Ring(1)
        r_ternsc = Ring(1); r_hbsc = Ring(1)

        S.o("sy", lambda e: e.dma_start(out=eye_sb[:], in_=ap_eye), inc=L_small, dma=True)

        for m in range(NM):
            for piece in range(2):
                idx = m * 2 + piece
                r_xld.acquire(S, "sy", idx)
                src = ap_adj[m].rearrange("p a b -> p (a b)")[:, piece * 1024:(piece + 1) * 1024]
                S.o("sy", lambda e, s=src, d=x_ld[idx % 2]: e.dma_start(out=d[:], in_=s),
                    inc=L_xld[idx % 2], dma=True)
                r_adjb.acquire(S, "ve", m) if piece == 0 else None
                S.w("ve", L_xld[idx % 2])
                S.o("ve", lambda e, sl=x_ld[idx % 2], d=adjbuf[m % 2], p=piece:
                    e.tensor_copy(d[:, p * 1024:(p + 1) * 1024], sl[:]), inc=V_cast)
                r_xld.release(idx, V_cast)
            dst = d_adjbf[m].rearrange("p a b -> p (a b)")
            S.w("gp", V_cast)
            S.o("gp", lambda e, s=adjbuf[m % 2], d=dst: e.dma_start(out=d, in_=s[:]),
                inc=P_adjb[m % 2], dma=True)
            r_adjb.release(m, P_adjb[m % 2])
        for m in range(2 * NM):
            idx = 32 + m
            r_xld.acquire(S, "sy", idx)
            src = ap_w[m].rearrange("p a b -> p (a b)")
            S.o("sy", lambda e, s=src, d=x_ld[idx % 2]: e.dma_start(out=d[:], in_=s),
                inc=L_xld[idx % 2], dma=True)
            r_wib.acquire(S, "ve", m)
            S.w("ve", L_xld[idx % 2])
            S.o("ve", lambda e, sl=x_ld[idx % 2], d=wibuf[m % 2]:
                e.tensor_copy(d[:], sl[:]), inc=V_cast)
            r_xld.release(idx, V_cast)
            dst = d_wbf[m].rearrange("p a b -> p (a b)")
            S.w("gp", V_cast)
            S.o("gp", lambda e, s=wibuf[m % 2], d=dst: e.dma_start(out=d, in_=s[:]),
                inc=P_wib[m % 2], dma=True)
            r_wib.release(m, P_wib[m % 2])
        for kb in range(NM):
            idx = 64 + kb
            r_xld.acquire(S, "sy", idx)
            src = ap_op[kb * 128:(kb + 1) * 128, :]
            S.o("sy", lambda e, s=src, d=x_ld[idx % 2]: e.dma_start(out=d[:], in_=s),
                inc=L_xld[idx % 2], dma=True)
            r_wib.acquire(S, "ve", 32 + kb)
            S.w("ve", L_xld[idx % 2])
            S.o("ve", lambda e, sl=x_ld[idx % 2], d=wibuf[kb % 2]:
                e.tensor_copy(d[:], sl[:]), inc=V_cast)
            r_xld.release(idx, V_cast)
            dst = d_opbf[kb * 128:(kb + 1) * 128, :]
            S.w("gp", V_cast)
            S.o("gp", lambda e, s=wibuf[kb % 2], d=dst: e.dma_start(out=d, in_=s[:]),
                inc=P_wib[kb % 2], dma=True)
            r_wib.release(32 + kb, P_wib[kb % 2])
        for piece in range(2):
            idx = 80 + piece
            r_xld.acquire(S, "sy", idx)
            S.o("sy", lambda e, p=piece, d=x_ld[idx % 2]:
                e.dma_start(out=d[:, 0:768], in_=ap_xp[:, p * 768:(p + 1) * 768]),
                inc=L_xld[idx % 2], dma=True)
            S.w("ve", L_xld[idx % 2])
            S.o("ve", lambda e, p=piece, sl=x_ld[idx % 2]:
                e.tensor_copy(xpw_sb[:, p * 768:(p + 1) * 768], sl[:, 0:768]), inc=V_cast)
            r_xld.release(idx, V_cast)
        for piece in range(2):
            idx = 82 + piece
            r_xld.acquire(S, "sy", idx)
            S.o("sy", lambda e, p=piece, d=x_ld[idx % 2]:
                e.dma_start(out=d[0:DR, :], in_=ap_dtw[:, p * 1024:(p + 1) * 1024]),
                inc=L_xld[idx % 2], dma=True)
            r_wib.acquire(S, "ve", 48 + piece)
            S.w("ve", L_xld[idx % 2])
            S.o("ve", lambda e, p=piece, sl=x_ld[idx % 2]:
                e.tensor_copy(wibuf[p % 2][0:DR, 0:1024], sl[0:DR, :]), inc=V_cast)
            r_xld.release(idx, V_cast)
            S.w("gp", V_cast)
            S.o("gp", lambda e, p=piece: e.dma_start(
                out=d_dtw[:, p * 1024:(p + 1) * 1024], in_=wibuf[p % 2][0:DR, 0:1024]),
                inc=P_wib[piece % 2], dma=True)
            r_wib.release(48 + piece, P_wib[piece % 2])
        S.o("sy", lambda e: e.dma_start(out=a_sb[:], in_=ap_apm), inc=L_small, dma=True)
        S.o("sy", lambda e: e.dma_start(out=cw_sb[:], in_=ap_cw), inc=L_small, dma=True)
        S.o("sy", lambda e: e.dma_start(out=cb_sb[:], in_=ap_cb), inc=L_small, dma=True)
        S.o("sy", lambda e: e.dma_start(out=dtb_sb[:], in_=ap_dtb), inc=L_small, dma=True)
        S.o("sy", lambda e: e.dma_start(out=dd_sb[:], in_=ap_dd), inc=L_small, dma=True)
        S.w("sc", L_small, 96)
        S.o("sc", lambda e: e.activation(a_sb[:], a_sb[:], AF.Exp), inc=A_al)
        S.w("ve", A_al)
        S.o("ve", lambda e: e.tensor_scalar_mul(a_sb[:], a_sb[:], -1.0), inc=V_cast)
        S.o("ve", lambda e: e.memset(hcarry[:], 0.0), inc=V_cast)
        S.o("ve", lambda e: e.memset(halo_sb[:], 0.0), inc=V_cast)
        S.o("ve", lambda e: e.memset(cst[:, 0:1], EPS), inc=V_cast)
        S.o("ve", lambda e: e.memset(cst[:, 1:2], -0.5), inc=V_cast)
        S.o("ve", lambda e: e.memset(junk_v[:], 0.0))
        S.o("sc", lambda e: e.memzero(junk_s[:]))

        for ob in range(NKB):
            idx = 84 + ob
            r_xld.acquire(S, "sy", idx)
            S.o("sy", lambda e, o=ob, d=x_ld[idx % 2]:
                e.dma_start(out=d[:], in_=ap_fw[o * 128:(o + 1) * 128, :]),
                inc=L_xld[idx % 2], dma=True)
            S.w("ve", L_xld[idx % 2])
            xl = x_ld[idx % 2]
            for hh in range(2):
                r_cacc.acquire(S, "ve", hh)
                S.o("ve", lambda e, sl=xl, h=hh: e.tensor_scalar(
                    cacc[h][:], sl[:, h * 512:(h + 1) * 512], -1.0, None, op0=ALU.mult),
                    fence=True)
                S.o("ve", lambda e, sl=xl, h=hh: e.tensor_max(
                    cacc[h][:], cacc[h][:], sl[:, h * 512:(h + 1) * 512]), fence=True)
                S.o("ve", lambda e, h=hh: e.tensor_reduce(
                    stat[:, 3 + h:4 + h], cacc[h][:], axis=mybir.AxisListType.X,
                    op=ALU.add), fence=True)
            S.o("ve", lambda e: e.tensor_add(stat[:, 0:1], stat[:, 3:4], stat[:, 4:5]), fence=True)
            S.o("ve", lambda e: e.tensor_scalar_mul(stat[:, 0:1], stat[:, 0:1], 1.0 / DM), fence=True)
            S.o("ve", lambda e: e.tensor_scalar_max(stat[:, 0:1], stat[:, 0:1], 1e-5), fence=True)
            S.o("ve", lambda e: e.tensor_scalar_mul(stat[:, 14:15], stat[:, 0:1], 0.5), fence=True)
            S.o("ve", lambda e: e.tensor_scalar_mul(stat[:, 15:16], stat[:, 14:15], -1.0), fence=True)
            r_ternsc.acquire(S, "ve", 0)
            for hh in range(2):
                cs = slice(hh * 512, (hh + 1) * 512)
                r_cacc.acquire(S, "ve", hh)
                S.o("ve", lambda e, sl=xl, c=cs: e.tensor_scalar(
                    cacc[0][:], sl[:, c], stat[:, 14:15], None, op0=ALU.is_gt))
                S.o("ve", lambda e, sl=xl, c=cs: e.tensor_scalar(
                    cacc[1][:], sl[:, c], stat[:, 15:16], None, op0=ALU.is_lt), fence=True)
                S.o("ve", lambda e: e.tensor_sub(cacc[0][:], cacc[0][:], cacc[1][:]), fence=True)
                S.o("ve", lambda e, o=ob, c=cs: e.tensor_scalar(
                    xq_sb[:, c], cacc[0][:], stat[:, 0:1], None, op0=ALU.mult))
                S.fence("ve", inc=V_pt)
                r_cacc.release(hh, V_pt)
            r_xld.release(idx, V_pt)
            r_pstr.acquire(S, "te", 0); r_pstr.acquire(S, "te", 1)
            S.w("te", V_pt)
            S.w("te", L_small, 96)
            for ib in range(NKB):
                S.o("te", lambda e, i=ib: e.transpose(
                    ps_tr[i // 4][:, (i % 4) * 128:(i % 4 + 1) * 128],
                    xq_sb[:, i * 128:(i + 1) * 128],
                    eye_sb[:]), inc=(T_ptr if ib == NKB - 1 else None))
            r_ternsc.release(0, T_ptr)
            r_hbsc.acquire(S, "sc", 0)
            S.w("sc", T_ptr)
            S.o("sc", lambda e: e.copy(hbuf[:, 0:512], ps_tr[0][:, 0:512]))
            S.o("sc", lambda e: e.copy(hbuf[:, 512:1024], ps_tr[1][:, 0:512]), inc=A_pev)
            r_pstr.release(0, A_pev); r_pstr.release(1, A_pev)
            dst = d_ternbf.rearrange("(ib p) o -> p ib o", ib=NKB)[:, :, ob * 128:(ob + 1) * 128]
            S.w("gp", A_pev)
            S.o("gp", lambda e, d=dst: e.dma_start(out=d, in_=hbuf[:, 0:1024].rearrange(
                "p (ib c) -> p ib c", ib=NKB)), inc=L_hb, dma=True)
            r_hbsc.release(0, L_hb)
        S.w("sy", L_hb)
        S.o("sy", lambda e: e.dma_start(
            out=hbuf[:].rearrange("p (kb c) -> p kb c", kb=NKB),
            in_=ap_h.rearrange("(kb p) c -> p kb c", p=128)),
            inc=L_h, dma=True)
        r_xq.release(0, T_ptr)

        PRO_WR = [(P_wib[0], P_wib[0].n), (P_wib[1], P_wib[1].n),
                  (P_adjb[0], P_adjb[0].n), (P_adjb[1], P_adjb[1].n),
                  (L_hb, L_hb.n)]

        for ci in range(NCH):
            b = ci // (L // TC)
            cs = ci % (L // TC)
            t0 = cs * TC

            if cs == 0:
                S.o("ve", lambda e: e.memset(hcarry[:], 0.0))
                S.o("ve", lambda e: e.memset(halo_sb[:], 0.0))
            def emit_xload(ti):
                gi = ci * NTI + ti
                r_xld.acquire(S, "sc", 100 + gi)
                src = ap_x[b, t0 + ti * 128: t0 + (ti + 1) * 128, :]
                S.o("sc", lambda e, s=src, d=x_ld[gi % 2]: e.dma_start(out=d[:], in_=s),
                    inc=L_xld[gi % 2], dma=True)

            emit_xload(0)
            emit_xload(1)
            vxn_tgt_a = []
            for ti in range(NTI):
                gi = ci * NTI + ti
                xl = x_ld[gi % 2]
                S.w("ve", L_xld[gi % 2])
                S.o("ve", lambda e, sl=xl: e.reduce_sum(stat[:, 0:1], sl[:],
                                                        axis=mybir.AxisListType.X),
                    fence=True)
                r_cacc.acquire(S, "sc", 2 * gi)
                r_cacc.acquire(S, "sc", 2 * gi + 1)
                S.w("sc", L_xld[gi % 2])
                S.w("sc", V_xn)
                S.o("sc", lambda e, sl=xl: e.activation(
                    cacc[0][:], sl[:, 0:512], AF.Square, accum_out=stat[:, 1:2]))
                S.o("sc", lambda e, sl=xl: e.activation(
                    cacc[1][:], sl[:, 512:1024], AF.Square, accum_out=stat[:, 2:3]))
                S.fence("sc", inc=A_rstd)
                r_cacc.release(2 * gi, A_rstd)
                r_cacc.release(2 * gi + 1, A_rstd)
                S.w("ve", A_rstd)
                S.o("ve", lambda e: e.tensor_add(stat[:, 3:4], stat[:, 1:2], stat[:, 2:3]), fence=True)
                S.o("ve", lambda e: e.tensor_scalar_mul(stat[:, 4:5], stat[:, 0:1], 1.0 / DM), fence=True)
                S.o("ve", lambda e: e.tensor_scalar_mul(stat[:, 5:6], stat[:, 3:4], 1.0 / DM), fence=True)
                S.o("ve", lambda e: e.tensor_mul(stat[:, 6:7], stat[:, 4:5], stat[:, 4:5]), fence=True)
                S.o("ve", lambda e: e.tensor_sub(stat[:, 5:6], stat[:, 5:6], stat[:, 6:7]))
                S.fence("ve", inc=V_xn)
                S.w("sc", V_xn)
                S.o("sc", lambda e: e.activation(stat[:, 7:8], stat[:, 5:6], AF.Ln,
                                                 bias=cst[:, 0:1]), fence=True)
                S.o("sc", lambda e: e.activation(stat[:, 7:8], stat[:, 7:8], AF.Exp,
                                                 scale=cst[:, 1:2]))
                S.fence("sc", inc=A_rstd)
                r_xn.acquire(S, "ve", 0) if ti == 0 else None
                S.w("ve", A_rstd)
                if DBG and ci == 0 and ti == 0:
                    S.o("ve", lambda e: e.tensor_copy(ln_snap[:], stat[:]))
                S.o("ve", lambda e, sl=xl, t=ti: e.tensor_scalar(
                    xn_sb[:, t * DM:(t + 1) * DM], sl[:], stat[:, 4:5], stat[:, 7:8],
                    op0=ALU.subtract, op1=ALU.mult))
                S.fence("ve", inc=V_xn)
                vxn_tgt_a.append(V_xn.n)
                r_xld.release(100 + gi, V_xn)
                if ti + 2 < NTI:
                    emit_xload(ti + 2)

            if PH < 2:
                continue
            r_xnT.acquire(S, "te", 0)
            for kb in range(NKB):
                r_pstr.acquire(S, "te", kb)
                for ti in range(NTI):
                    if kb == 0:
                        S.w("te", V_xn, vxn_tgt_a[ti])
                    S.o("te", lambda e, t=ti, k=kb: e.transpose(
                        ps_tr[k % 2][:, t * 128:(t + 1) * 128],
                        xn_sb[:, t * DM + k * 128: t * DM + (k + 1) * 128], eye_sb[:]),
                        inc=(T_trn if ti == NTI - 1 else None))
                S.w("sc", T_trn)
                S.o("sc", lambda e, k=kb: e.copy(
                    xnT_sb[:, k * TC:(k + 1) * TC],
                    ps_tr[k % 2][:, 0:512]), inc=A_evxn)
                r_pstr.release(kb, A_evxn)
            r_xn.release(0, T_trn)

            if PH < 3:
                continue
            r_xq.acquire(S, "ve", 0)
            for ti in range(NTI):
                for half in range(2):
                    r_psa.acquire(S, "te", half)
                    if ti == 0 and half == 0:
                        S.w("te", L_h, 16)
                        S.w("te", A_evxn)
                    for kb in range(NKB):
                        S.o("te", lambda e, t=ti, h=half, k=kb: e.matmul(
                            ps_a[h][:],
                            xnT_sb[:, k * TC + t * 128: k * TC + (t + 1) * 128],
                            hbuf[:, k * DM + h * 512: k * DM + (h + 1) * 512],
                            start=(k == 0), stop=(k == NKB - 1)),
                            inc=(T_xh if kb == NKB - 1 else None))
                S.w("ve", T_xh)
                for half in range(2):
                    r_cacc.acquire(S, "ve", 40 + 8 * ci + 2 * ti + half)
                    S.o("ve", lambda e, h=half: e.tensor_copy(cacc[h][:], ps_a[h][:]),
                        fence=(half == 1))
                    r_psa.release(half, V_q, V_q.n + 1)
                S.o("ve", lambda e: e.tensor_reduce(
                    stat[:, 8:9], cacc[0][:], axis=mybir.AxisListType.X, op=ALU.max))
                S.o("ve", lambda e: e.tensor_reduce(
                    stat[:, 9:10], cacc[1][:], axis=mybir.AxisListType.X, op=ALU.max))
                S.o("ve", lambda e: e.tensor_reduce(
                    stat[:, 11:12], cacc[0][:], axis=mybir.AxisListType.X, op=ALU.min))
                S.o("ve", lambda e: e.tensor_reduce(
                    stat[:, 12:13], cacc[1][:], axis=mybir.AxisListType.X, op=ALU.min),
                    fence=True)
                S.o("ve", lambda e: e.tensor_max(stat[:, 8:9], stat[:, 8:9], stat[:, 9:10]), fence=True)
                S.o("ve", lambda e: e.tensor_tensor(stat[:, 11:12], stat[:, 11:12], stat[:, 12:13], op=ALU.min), fence=True)
                S.o("ve", lambda e: e.tensor_scalar(stat[:, 11:12], stat[:, 11:12], -1.0, None, op0=ALU.mult), fence=True)
                S.o("ve", lambda e: e.tensor_max(stat[:, 8:9], stat[:, 8:9], stat[:, 11:12]), fence=True)
                S.o("ve", lambda e: e.tensor_scalar_max(stat[:, 8:9], stat[:, 8:9], 1e-5), fence=True)
                S.o("ve", lambda e: e.tensor_scalar_mul(stat[:, 8:9], stat[:, 8:9], 1.0 / 7.0))
                S.fence("ve", inc=V_q)
                S.w("sc", V_q)
                S.o("sc", lambda e: e.activation(stat[:, 10:11], stat[:, 8:9], AF.Ln),
                    fence=True)
                S.o("sc", lambda e: e.activation(stat[:, 9:10], stat[:, 10:11], AF.Exp,
                                                 scale=-1.0))
                S.fence("sc", inc=A_rstd)
                S.w("ve", A_rstd)
                for half in range(2):
                    S.o("ve", lambda e, h=half: e.tensor_scalar(
                        cacc[h][:], cacc[h][:], stat[:, 9:10], RNE_C,
                        op0=ALU.mult, op1=ALU.add), inc=V_q, fence=True)
                    S.o("ve", lambda e, h=half, t=ti: e.tensor_scalar(
                        xq_sb[:, t * DM + h * 512: t * DM + (h + 1) * 512],
                        cacc[h][:], RNE_C, stat[:, 8:9],
                        op0=ALU.subtract, op1=ALU.mult))
                    S.fence("ve", inc=V_q)
                    r_cacc.release(40 + 8 * ci + 2 * ti + half, V_q)

            if PH < 4:
                continue
            r_xi.acquire(S, "sc", 0)
            r_yf.acquire(S, "sc", 0)
            def emit_ip(m):
                r_wib.acquire(S, "sy", m)
                if ci == 0 and m == 0:
                    for (ps_sem, ps_t) in PRO_WR:
                        S.w("sy", ps_sem, ps_t)
                src = d_wbf[m].rearrange("p a b -> p (a b)")
                S.o("sy", lambda e, s=src, d=wibuf[m % 2]: e.dma_start(out=d[:], in_=s),
                    inc=L_wib[m % 2], dma=True)
                r_psip.acquire(S, "te", m)
                S.w("te", L_wib[m % 2])
                if m == 0:
                    S.w("te", A_evxn)
                for kb in range(NKB):
                    S.o("te", lambda e, k=kb, mm=m: e.matmul(
                        ps_ip[mm % 2][:], wibuf[mm % 2][:, k * 128:(k + 1) * 128],
                        xnT_sb[:, k * TC:(k + 1) * TC],
                        start=(k == 0), stop=(k == NKB - 1)),
                        inc=(T_ip if kb == NKB - 1 else None))
                r_wib.release(m, T_ip)

            def emit_res(m):
                r_rwb.acquire(S, "sy", m)
                src2 = d_wbf[NM + m].rearrange("p a b -> p (a b)")
                S.o("sy", lambda e, s=src2, d=rwbuf[m % 2]: e.dma_start(out=d[:], in_=s),
                    inc=L_rwb[m % 2], dma=True)
                r_psdp.acquire(S, "te", m)
                S.w("te", L_rwb[m % 2])
                for kb in range(NKB):
                    S.o("te", lambda e, k=kb, mm=m: e.matmul(
                        ps_dp[mm % 2][:], rwbuf[mm % 2][:, k * 128:(k + 1) * 128],
                        xnT_sb[:, k * TC:(k + 1) * TC],
                        start=(k == 0), stop=(k == NKB - 1)),
                        inc=(T_res if kb == NKB - 1 else None))
                r_rwb.release(m, T_res)

            def emit_evacxi(m):
                r_xip.acquire(S, "sc", m)
                S.w("sc", T_ip)
                S.o("sc", lambda e, mm=m: e.copy(xip_sb[mm % 2][:, 3:3 + TC],
                                                 ps_ip[mm % 2][:]))
                S.fence("sc", inc=A_evxi)
                r_psip.release(m, A_evxi)

            def emit_conv(m):
                r_cacc.acquire(S, "ve", m)
                S.w("ve", A_evxi, aevxi0 + m + 1)
                S.o("ve", lambda e, mm=m: e.tensor_copy(
                    xip_sb[mm % 2][:, 0:3],
                    halo_sb[:, mm * 3:(mm + 1) * 3]))
                S.o("ve", lambda e, mm=m: e.tensor_scalar(
                    cacc[mm % 2][:], xip_sb[mm % 2][:, 0:TC], cw_sb[:, mm * 4:mm * 4 + 1],
                    None, op0=ALU.mult))
                for k in range(1, 4):
                    S.o("ve", lambda e, mm=m, kk=k: e.scalar_tensor_tensor(
                        cacc[mm % 2][:], xip_sb[mm % 2][:, kk:kk + TC],
                        cw_sb[:, mm * 4 + kk:mm * 4 + kk + 1], cacc[mm % 2][:],
                        op0=ALU.mult, op1=ALU.add))
                S.o("ve", lambda e, mm=m: e.tensor_copy(
                    halo_sb[:, mm * 3:(mm + 1) * 3],
                    xip_sb[mm % 2][:, TC:TC + 3]))
                S.fence("ve", inc=V_conv)
                r_xip.release(m, V_conv)

            def emit_silu(m):
                S.w("sc", V_conv)
                S.o("sc", lambda e, mm=m: e.activation(
                    xi_sb[:, mm * TC:(mm + 1) * TC], cacc[mm % 2][:], AF.Silu,
                    bias=cb_sb[:, mm:mm + 1]), inc=A_silu)
                r_cacc.release(m, A_silu)

            def emit_silures(m):
                S.w("sc", T_res, tres0 + m + 1)
                S.o("sc", lambda e, mm=m: e.activation(
                    yf_sb[:, mm * TC:(mm + 1) * TC], ps_dp[mm % 2][:], AF.Silu),
                    inc=A_res)
                r_psdp.release(m, A_res)

            aevxi0 = A_evxi.n
            tres0 = T_res.n
            asilu0 = A_silu.n

            def emit_xdbl(m):
                if m == 0:
                    r_psa.acquire(S, "te", 0)
                S.w("te", A_silu, asilu0 + m + 1)
                S.o("te", lambda e, mm=m: e.matmul(
                    ps_a[0][0:96, :], xpw_sb[:, mm * 96:(mm + 1) * 96],
                    xi_sb[:, mm * TC:(mm + 1) * TC],
                    start=(mm == 0), stop=(mm == NM - 1)),
                    inc=(T_xdbl if m == NM - 1 else None))

            for m in range(NM):
                emit_ip(m)
                if m >= 1:
                    emit_res(m - 1)
                emit_evacxi(m)
                if m >= 1:
                    emit_conv(m - 1)
                    emit_silu(m - 1)
                if m >= 2:
                    emit_silures(m - 2)
                if m >= 3:
                    emit_xdbl(m - 3)
            emit_res(NM - 1)
            emit_conv(NM - 1)
            emit_silu(NM - 1)
            emit_silures(NM - 2)
            emit_silures(NM - 1)
            for m in range(NM - 3, NM):
                emit_xdbl(m)

            if PH < 5:
                continue
            r_xqT.acquire(S, "te", 0)
            for kb in range(NKB):
                r_pstr.acquire(S, "te", kb)
                if kb == 0:
                    S.w("te", V_q)
                for ti in range(NTI):
                    S.o("te", lambda e, t=ti, k=kb: e.transpose(
                        ps_tr[k % 2][:, t * 128:(t + 1) * 128],
                        xq_sb[:, t * DM + k * 128: t * DM + (k + 1) * 128], eye_sb[:]),
                        inc=(T_trq if ti == NTI - 1 else None))
                S.w("sc", T_trq)
                S.o("sc", lambda e, k=kb: e.copy(
                    xqT_sb[:, k * TC:(k + 1) * TC],
                    ps_tr[k % 2][:, 0:512]), inc=A_evxq)
                r_pstr.release(kb, A_evxq)
            r_xq.release(0, T_trq)

            r_dinT.acquire(S, "sc", 0)
            S.w("sc", T_xdbl)
            S.o("sc", lambda e: e.copy(dinT_sb[:], ps_a[0][0:DR, :]))
            S.fence("sc", inc=A_dinT)
            r_bcs.acquire(S, "ve", 0)
            S.w("ve", T_xdbl)
            S.w("ve", A_dinT)
            S.o("ve", lambda e: e.tensor_copy(bc_sb[DR:96, :], ps_a[0][DR:96, :]))
            S.fence("ve", inc=V_bc)
            r_psa.release(0, A_dinT)
            r_psa.slots[0].append((V_bc, V_bc.n))
            S.w("gp", V_bc)
            S.o("gp", lambda e: e.dma_start(out=d_bc, in_=bc_sb[DR:96, :]), inc=L_bcd, dma=True)
            r_bcs.release(0, L_bcd)
            r_rep.acquire(S, "gp", 0)
            S.w("gp", L_bcd)
            for g in range(2):
                for j in range(8):
                    S.o("gp", lambda e, gg=g, jj=j: e.dma_start(
                        out=brep[gg][:, jj * TC:(jj + 1) * TC],
                        in_=d_bc[gg * 8 + jj:gg * 8 + jj + 1, :].broadcast_to([128, TC])),
                        inc=L_rep, dma=True)
                    S.o("gp", lambda e, gg=g, jj=j: e.dma_start(
                        out=crep[gg][:, jj * TC:(jj + 1) * TC],
                        in_=d_bc[16 + gg * 8 + jj:16 + gg * 8 + jj + 1, :].broadcast_to([128, TC])),
                        inc=L_rep, dma=True)
            r_dT.acquire(S, "sc", 0)
            for m in range(NM):
                S.w("sy", T_dt, max(0, T_dt.n - 1))
                S.o("sy", lambda e, mm=m: e.dma_start(
                    out=dtbuf[mm % 2][:], in_=d_dtw[:, mm * 128:(mm + 1) * 128]),
                    inc=L_dtb[m % 2], dma=True)
                r_psip.acquire(S, "te", 17 + m)
                S.w("te", L_dtb[m % 2])
                if m == 0:
                    S.w("te", A_dinT)
                S.o("te", lambda e, mm=m: e.matmul(
                    ps_ip[(17 + mm) % 2][:], dtbuf[mm % 2][:],
                    dinT_sb[:], start=True, stop=True), inc=T_dt)
                r_cacc.acquire(S, "sc", 17 + m)
                S.w("sc", T_dt)
                S.o("sc", lambda e, mm=m: e.activation(
                    cacc[(17 + mm) % 2][:], ps_ip[(17 + mm) % 2][:], AF.Exp,
                    bias=dtb_sb[:, mm:mm + 1]), fence=True)
                r_psip.release(17 + m, A_sp, A_sp.n + 1)
                S.o("sc", lambda e, mm=m: e.activation(
                    deltaT_sb[:, mm * TC:(mm + 1) * TC], cacc[(17 + mm) % 2][:],
                    AF.Ln, bias=1.0), inc=A_sp)
                r_cacc.release(17 + m, A_sp)
            if PH < 6:
                continue
            vy0 = V_y.n

            def emit_F_loads(half):
                gi0 = ci * 8 + half * NTI
                v_base = V_oev.n
                tgt = []
                for ti in range(NTI):
                    gi2 = gi0 + ti
                    r_xld.acquire(S, "sc", 500 + gi2)
                    src = ap_x[b, t0 + ti * 128: t0 + (ti + 1) * 128,
                               half * 512:(half + 1) * 512]
                    S.o("sc", lambda e, s=src, d=x_ld[gi2 % 2]:
                        e.dma_start(out=d[:, 0:512], in_=s), inc=L_xld[gi2 % 2], dma=True)
                    tgt.append(L_xld[gi2 % 2].n)
                    r_xld.release(500 + gi2, V_oev, v_base + ti + 1)
                return tgt

            slot6 = [(0, 0, ps_a[0]), (0, 1, ps_a[1]), (0, 2, ps_ip[0]),
                     (0, 3, ps_ip[1]), (1, 0, ps_dp[0]), (1, 1, ps_dp[1])]

            def emit_F_p1():
                tout0 = T_out.n
                for ib in range(NKB):
                    r_ternb.acquire(S, "sy", ib)
                    src = d_ternbf[ib * 128:(ib + 1) * 128, :]
                    S.o("sy", lambda e, s=src, d=ternbuf[ib % 2]: e.dma_start(out=d[:], in_=s),
                        inc=L_ternb[ib % 2], dma=True)
                    S.w("te", L_ternb[ib % 2])
                    if ib == 0:
                        S.w("te", A_evxq)
                        r_psa.acquire(S, "te", 0)
                        r_psa.acquire(S, "te", 1)
                        r_psip.acquire(S, "te", 40)
                        r_psip.acquire(S, "te", 41)
                        r_psdp.acquire(S, "te", 48)
                        r_psdp.acquire(S, "te", 49)
                    for si, (h, t, ps) in enumerate(slot6):
                        S.o("te", lambda e, k=ib, hh=h, tt=t, p=ps: e.matmul(
                            p[:],
                            xqT_sb[:, k * TC + tt * 128: k * TC + (tt + 1) * 128],
                            ternbuf[k % 2][:, hh * 512:(hh + 1) * 512],
                            start=(k == 0), stop=False),
                            inc=(T_tek if si == 5 else None))
                    r_ternb.release(ib, T_tek)
                for kb in range(NM):
                    r_opb.acquire(S, "sy", kb)
                    src = d_opbf[kb * 128:(kb + 1) * 128, :]
                    S.o("sy", lambda e, s=src, d=opbuf[kb % 2]: e.dma_start(out=d[:], in_=s),
                        inc=L_opb[kb % 2], dma=True)
                    S.w("te", L_opb[kb % 2])
                    S.w("te", V_y, vy0 + kb + 1)
                    for si, (h, t, ps) in enumerate(slot6):
                        S.o("te", lambda e, k=kb, hh=h, tt=t, p=ps: e.matmul(
                            p[:],
                            yf_sb[:, k * TC + tt * 128: k * TC + (tt + 1) * 128],
                            opbuf[k % 2][:, hh * 512:(hh + 1) * 512],
                            start=False, stop=(k == NM - 1)),
                            inc=((T_opk if si == 5 else None) if kb < NM - 1
                                 else T_out))
                    r_opb.release(kb, T_opk if kb < NM - 1 else T_out)
                return tout0

            def emit_F_p2():
                for kb in range(NM):
                    r_opb.acquire(S, "sy", 16 + kb)
                    src = d_opbf[kb * 128:(kb + 1) * 128, 512:1024]
                    S.o("sy", lambda e, s=src, d=opbuf[kb % 2]: e.dma_start(
                        out=d[:, 0:512], in_=s), inc=L_opb[kb % 2], dma=True)
                    S.w("te", L_opb[kb % 2])
                    if kb == 0:
                        r_psa.acquire(S, "te", 2)
                        r_psa.acquire(S, "te", 3)
                    for si, t in enumerate((2, 3)):
                        S.o("te", lambda e, k=kb, tt=t, p=ps_a[si]: e.matmul(
                            p[:],
                            yf_sb[:, k * TC + tt * 128: k * TC + (tt + 1) * 128],
                            opbuf[k % 2][:, 0:512], start=(k == 0), stop=False),
                            inc=(T_opk if si == 1 else None))
                    r_opb.release(16 + kb, T_opk)
                for ib in range(NKB):
                    r_ternb.acquire(S, "sy", 16 + ib)
                    src = d_ternbf[ib * 128:(ib + 1) * 128, 512:1024]
                    S.o("sy", lambda e, s=src, d=ternbuf[ib % 2]: e.dma_start(
                        out=d[:, 0:512], in_=s), inc=L_ternb[ib % 2], dma=True)
                    S.w("te", L_ternb[ib % 2])
                    for si, t in enumerate((2, 3)):
                        S.o("te", lambda e, k=ib, tt=t, p=ps_a[si]: e.matmul(
                            p[:],
                            xqT_sb[:, k * TC + tt * 128: k * TC + (tt + 1) * 128],
                            ternbuf[k % 2][:, 0:512], start=False, stop=(k == NKB - 1)),
                            inc=((T_tek if si == 1 else None) if ib < NKB - 1
                                 else T_out))
                    r_ternb.release(16 + ib, T_tek if ib < NKB - 1 else T_out)

            def emit_F_evacs(items, tout0, xre_tgt):
                for (oi, h, t, ps, rel) in items:
                    gi2 = ci * 8 + h * NTI + t
                    r_out.acquire(S, "ve", gi2)
                    S.w("ve", T_out, tout0 + oi + 1)
                    S.w("ve", L_xld[gi2 % 2], xre_tgt[t])
                    S.o("ve", lambda e, p=ps, g2=gi2: e.tensor_add(
                        out_sb[g2 % 2][:], p[:], x_ld[g2 % 2][:, 0:512]))
                    S.fence("ve", inc=V_oev)
                    rel()
                    dst = ap_out[b, t0 + t * 128: t0 + (t + 1) * 128,
                                 h * 512:(h + 1) * 512]
                    S.w("gp", V_oev)
                    S.o("gp", lambda e, d=dst, g2=gi2: e.dma_start(out=d, in_=out_sb[g2 % 2][:]),
                        inc=S_os[gi2 % 2], dma=True)
                    r_out.release(gi2, S_os[gi2 % 2])


            tdp_tgt = {}
            vsu_tgt = {}

            def emit_dp(m):
                r_adjb.acquire(S, "sy", 16 + m)
                if ci == 0 and m == 0:
                    for (ps_sem, ps_t) in PRO_WR:
                        S.w("sy", ps_sem, ps_t)
                src = d_adjbf[m].rearrange("p a b -> p (a b)")
                S.o("sy", lambda e, s=src, d=adjbuf[m % 2]: e.dma_start(out=d[:], in_=s),
                    inc=L_adjb[m % 2], dma=True)
                r_psdp.acquire(S, "te", 16 + m)
                S.w("te", L_adjb[m % 2])
                if m == 0:
                    S.w("te", A_sp)
                for kb in range(NM):
                    S.o("te", lambda e, k=kb, mm=m: e.matmul(
                        ps_dp[mm % 2][:], adjbuf[mm % 2][:, k * 128:(k + 1) * 128],
                        deltaT_sb[:, k * TC:(k + 1) * TC],
                        start=(k == 0), stop=(k == NM - 1)),
                        inc=(T_dp if kb == NM - 1 else None))
                tdp_tgt[m] = T_dp.n
                r_adjb.release(16 + m, T_dp)

            def emit_evac(m):
                r_cacc.acquire(S, "ve", 70 + 16 * ci + m)
                S.w("ve", T_dp, tdp_tgt[m])
                S.o("ve", lambda e, mm=m: e.tensor_copy(cacc[mm % 2][:], ps_dp[mm % 2][:]))
                S.fence("ve", inc=V_su)
                vsu_tgt[m] = V_su.n
                r_psdp.release(16 + m, V_su)

            def emit_E(m):
                r_su.acquire(S, "ve", m)
                S.o("ve", lambda e, mm=m: e.tensor_mul(
                    su_sb[mm % 2][:], cacc[mm % 2][:], xi_sb[:, mm * TC:(mm + 1) * TC]))
                for g in range(2):
                    albuf = [al_sb, xq_sb][(2 * m + g) % 2]
                    r_al.acquire(S, "sc", 2 * m + g)
                    if ci == 0 and m == 0:
                        S.w("sc", T_trq)
                    if g == 0:
                        S.w("sc", V_su, vsu_tgt[m])
                    for j in range(8):
                        n = g * 8 + j
                        S.o("sc", lambda e, ab=albuf, mm=m, nn=n, jj=j: e.activation(
                            ab[:, jj * TC:(jj + 1) * TC], cacc[mm % 2][:], AF.Exp,
                            scale=a_sb[:, mm * NST + nn:mm * NST + nn + 1]),
                            inc=(A_al if j == 7 else None))
                    if g == 1:
                        r_cacc.release(70 + 16 * ci + m, A_al)
                    if m == 0 and g == 0:
                        S.w("ve", L_rep)
                    su_b = su_sb[m % 2][:].unsqueeze(1).broadcast_to([128, 8, TC])
                    bv = brep[g][:].rearrange("p (j t) -> p j t", j=8)
                    wvv = wv_sb[:].rearrange("p (j t) -> p j t", j=8)
                    S.o("ve", lambda e, sb_=su_b, b_=bv, w_=wvv: e.tensor_tensor(
                        w_, sb_, b_, op=ALU.mult), inc=V_w)
                    if g == 1:
                        r_su.release(m, V_w)
                    S.w("ve", A_al)
                    for j in range(8):
                        n = g * 8 + j
                        S.o("ve", lambda e, ab=albuf, mm=m, nn=n, jj=j: e.tensor_tensor_scan(
                            hh_sb[:, jj * TC:(jj + 1) * TC],
                            ab[:, jj * TC:(jj + 1) * TC],
                            wv_sb[:, jj * TC:(jj + 1) * TC],
                            hcarry[:, mm * NST + nn:mm * NST + nn + 1],
                            op0=ALU.mult, op1=ALU.add),
                            inc=(V_scan if j == 7 else None))
                    r_al.release(2 * m + g, V_scan)
                    S.fence("ve")
                    S.o("ve", lambda e, mm=m, gg=g: e.tensor_copy(
                        hcarry[:, mm * NST + gg * 8: mm * NST + gg * 8 + 8],
                        hh_sb[:, TC - 1::TC]))
                    S.o("ve", lambda e, gg=g: e.tensor_mul(wv_sb[:], hh_sb[:], crep[gg][:]))
                    S.o("ve", lambda e: e.tensor_add(
                        wv_sb[:, 0:4 * TC], wv_sb[:, 0:4 * TC], wv_sb[:, 4 * TC:8 * TC]))
                    S.o("ve", lambda e: e.tensor_add(
                        wv_sb[:, 4 * TC:6 * TC], wv_sb[:, 0:2 * TC], wv_sb[:, 2 * TC:4 * TC]))
                    S.o("ve", lambda e, gg=g: e.tensor_add(
                        yg_sb[:, gg * TC:(gg + 1) * TC], wv_sb[:, 4 * TC:5 * TC],
                        wv_sb[:, 5 * TC:6 * TC]))
                S.fence("ve")
                S.o("ve", lambda e: e.tensor_add(
                    yg_sb[:, 0:TC], yg_sb[:, 0:TC], yg_sb[:, TC:2 * TC]), fence=True)
                S.o("ve", lambda e, mm=m: e.scalar_tensor_tensor(
                    t2_sb[:], xi_sb[:, mm * TC:(mm + 1) * TC], dd_sb[:, mm:mm + 1],
                    yg_sb[:, 0:TC], op0=ALU.mult, op1=ALU.add), fence=True)
                S.w("ve", A_res)
                S.o("ve", lambda e, mm=m: e.tensor_mul(
                    yf_sb[:, mm * TC:(mm + 1) * TC], yf_sb[:, mm * TC:(mm + 1) * TC],
                    t2_sb[:]))
                S.fence("ve", inc=V_y)

            for m in range(NM):
                emit_dp(m)
                emit_evac(m)
                if m >= 1:
                    emit_E(m - 1)
            if PH >= 7:
                tout_0 = emit_F_p1()
            emit_E(NM - 1)
            if PH >= 7:
                xre0 = emit_F_loads(0)
                emit_F_evacs(
                    [(0, 0, 0, ps_a[0], lambda: r_psa.release(0, V_oev)),
                     (1, 0, 1, ps_a[1], lambda: r_psa.release(1, V_oev)),
                     (2, 0, 2, ps_ip[0],
                      lambda: r_psip.slots[0].append((V_oev, V_oev.n))),
                     (3, 0, 3, ps_ip[1],
                      lambda: r_psip.slots[1].append((V_oev, V_oev.n)))],
                    tout_0, xre0)
                xre1 = emit_F_loads(1)
                emit_F_evacs(
                    [(4, 1, 0, ps_dp[0],
                      lambda: r_psdp.slots[0].append((V_oev, V_oev.n))),
                     (5, 1, 1, ps_dp[1],
                      lambda: r_psdp.slots[1].append((V_oev, V_oev.n)))],
                    tout_0, xre1)
                emit_F_p2()
                emit_F_evacs(
                    [(6, 1, 2, ps_a[0], lambda: r_psa.release(2, V_oev)),
                     (7, 1, 3, ps_a[1], lambda: r_psa.release(3, V_oev))],
                    tout_0, xre1)

            r_rep.release(0, V_y)
            r_xi.release(0, V_y)
            r_dT.release(0, T_dp)
            r_dinT.release(0, T_dt)
            r_xnT.release(0, T_res)
            r_xqT.release(0, T_out)
            r_yf.release(0, T_out)
        if DBG:
            dbg_src = {"d_xn": xn_sb, "d_xnT": xnT_sb, "d_xi": xi_sb,
                       "d_dT": deltaT_sb, "d_yf": yf_sb, "d_xq": xq_sb,
                       "d_hc": hcarry, "d_brep": brep[0], "d_crep": crep[0],
                       "d_asb": a_sb, "d_su": su_sb[1], "d_al": al_sb,
                       "d_hh": hh_sb, "d_wv": wv_sb, "d_stat": stat,
                       "d_ln": ln_snap}
            dsem = S_("dsem"); dwr = S_("dwr")
            for _si in range(2):
                S.w("ve", S_os[_si], S_os[_si].n)
            for nm, t in dbg_src.items():
                cols = t.ap().shape[1]
                for p0 in range(0, cols, 1024):
                    pc = min(1024, cols - p0)
                    S.w("ve", dwr, dwr.n)
                    S.o("ve", lambda e, tt=t, a=p0, c=pc: e.tensor_copy(
                        x_ld[0][:, 0:c], tt[:, a:a + c]), inc=dsem)
                    S.w("gp", dsem, dsem.n)
                    S.o("gp", lambda e, nm_=nm, a=p0, c=pc: e.dma_start(
                        out=dbg_aps[nm_][:, a:a + c], in_=x_ld[0][:, 0:c]),
                        inc=dwr, dma=True)
            S.w("gp", dwr, dwr.n)

        for _si in range(2):
            S.w("gp", S_os[_si], S_os[_si].n)

        block = stack.enter_context(nc.Block())
        block.sync(lambda e: S.replay("sy", e))
        block.gpsimd(lambda e: S.replay("gp", e))
        block.vector(lambda e: S.replay("ve", e))
        block.scalar(lambda e: S.replay("sc", e))
        block.tensor(lambda e: S.replay("te", e))

    return nc


_CACHE = {}


def _hadamard(size):
    h = np.array([[1.0]], dtype=np.float64)
    while h.shape[0] < size:
        h = np.block([[h, h], [h, -h]])
    return (h / np.sqrt(size)).astype(np.float32)


def kernel(L0_attn_x, L0_attn_adj, L0_attn_in_proj_w, L0_attn_conv_w, L0_attn_conv_b,
           L0_attn_x_proj_w, L0_attn_dt_proj_w, L0_attn_dt_proj_b, L0_attn_A_log,
           L0_attn_D, L0_attn_out_proj_w, L0_ffn_weight):
    import ml_dtypes
    x = np.ascontiguousarray(np.asarray(L0_attn_x, dtype=np.float32))
    adj = np.asarray(L0_attn_adj, np.float32)
    W = np.asarray(L0_attn_in_proj_w, np.float32)
    adj_pm = np.ascontiguousarray(adj.reshape(16, 128, 16, 128).transpose(2, 1, 0, 3))
    w_pm = np.ascontiguousarray(W.reshape(8, 128, 32, 128).transpose(2, 1, 0, 3))
    op_w = np.ascontiguousarray(np.asarray(L0_attn_out_proj_w, np.float32))
    xp_pm = np.ascontiguousarray(
        np.asarray(L0_attn_x_proj_w, np.float32).reshape(16, 128, 96)
        .transpose(1, 0, 2).reshape(128, 16 * 96))
    dt_w = np.ascontiguousarray(np.asarray(L0_attn_dt_proj_w, np.float32))
    a_pm = np.ascontiguousarray(
        np.asarray(L0_attn_A_log, np.float32).reshape(16, 128, 16)
        .transpose(1, 0, 2).reshape(128, 256))
    convw_pm = np.ascontiguousarray(
        np.asarray(L0_attn_conv_w, np.float32)[:, 0, :].reshape(16, 128, 4)
        .transpose(1, 0, 2).reshape(128, 64))
    convb_pm = np.ascontiguousarray(np.asarray(L0_attn_conv_b, np.float32).reshape(16, 128).T)
    dtb_pm = np.ascontiguousarray(np.asarray(L0_attn_dt_proj_b, np.float32).reshape(16, 128).T)
    d_pm = np.ascontiguousarray(np.asarray(L0_attn_D, np.float32).reshape(16, 128).T)
    ffn_w = np.ascontiguousarray(np.asarray(L0_ffn_weight, np.float32))
    h_bf = np.ascontiguousarray(_hadamard(DM).astype(ml_dtypes.bfloat16))
    eye_bf = np.ascontiguousarray(np.eye(128, dtype=np.float32).astype(ml_dtypes.bfloat16))

    if "nc" not in _CACHE:
        _CACHE["nc"] = build_nc()
    nc = _CACHE["nc"]

    shared = dict(adj_pm=adj_pm, w_pm=w_pm, op_w=op_w, xp_pm=xp_pm, dt_w=dt_w,
                  a_pm=a_pm, convw_pm=convw_pm, convb_pm=convb_pm, dtb_pm=dtb_pm,
                  d_pm=d_pm, ffn_w=ffn_w, h_bf=h_bf, eye_bf=eye_bf)
    in_maps = []
    for c in range(NCORE):
        m = dict(shared)
        m["x"] = np.ascontiguousarray(x[c * BC:(c + 1) * BC])
        in_maps.append(m)
    res = run_bass_kernel_spmd(nc, in_maps, core_ids=list(range(NCORE)))
    out = np.concatenate([r["out"] for r in res.results], axis=0)
    return out.astype(np.float32)


# version 13
# speedup vs baseline: 1.2751x; 1.0048x over previous
import contextlib
import numpy as np
import concourse.bass as bass
import concourse.mybir as mybir
from concourse.bass_utils import run_bass_kernel_spmd

F32 = mybir.dt.float32
BF16 = mybir.dt.bfloat16
AF = mybir.ActivationFunctionType
ALU = mybir.AluOpType

NCORE = 8
BC = 2
L = 2048
DM = 1024
DI = 2048
NST = 16
DR = 64
TC = 512
NCH = BC * L // TC
NTI = TC // 128
NM = DI // 128
NKB = DM // 128
RNE_C = float(3 * 2.0 ** 22)
EPS = 1e-5

ENGS = ("sy", "gp", "ve", "sc", "te")

DEV_DRAIN = False
PH = 7
PHE = 6
ALV = 0
DBG = False


class Sem:
    def __init__(self, stack, nc, name):
        self.h = stack.enter_context(nc.semaphore(name))
        self.n = 0
        self.name = name


class Sched:

    def __init__(self):
        self.q = {e: [] for e in ENGS}

    def o(self, eng, fn, inc=None, dma=False, fence=False):
        incs = []
        if inc is not None:
            if not isinstance(inc, list):
                inc = [inc]
            for s in inc:
                k = 16 if dma else 1
                s.n += k
                incs.append((s, k))
        self.q[eng].append(("op", fn, incs))
        if fence:
            self.fence(eng)

    def fence(self, eng, inc=None):
        incs = []
        if inc is not None:
            if not isinstance(inc, list):
                inc = [inc]
            for s in inc:
                s.n += 1
                incs.append((s, 1))
        jb = self.junk[eng]
        if eng == "sc":
            self.q[eng].append(("op", lambda e: e.copy(jb[:, 0:1], jb[:, 2:3]), []))
            self.q[eng].append(("op", lambda e: e.copy(jb[:, 1:2], jb[:, 3:4]), incs))
        else:
            self.q[eng].append(("op", lambda e: e.tensor_copy(jb[:, 0:1], jb[:, 2:3]), []))
            self.q[eng].append(("op", lambda e: e.tensor_copy(jb[:, 1:2], jb[:, 3:4]), incs))

    def w(self, eng, sem, target=None):
        t = sem.n if target is None else target
        self.q[eng].append(("wait", sem, t))

    def replay(self, eng_name, eng):
        for item in self.q[eng_name]:
            if item[0] == "op":
                ins = item[1](eng)
                for (s, k) in item[2]:
                    ins.then_inc(s.h, k)
                if DEV_DRAIN and eng_name in ("ve", "sc", "te"):
                    eng.drain()
            else:
                eng.wait_ge(item[1].h, item[2])


class Ring:

    def __init__(self, n):
        self.n = n
        self.slots = [[] for _ in range(n)]

    def acquire(self, S, eng, idx):
        for (sem, t) in self.slots[idx % self.n]:
            S.w(eng, sem, t)
        self.slots[idx % self.n] = []

    def release(self, idx, sem, target=None):
        self.slots[idx % self.n].append((sem, sem.n if target is None else target))


def build_nc():
    nc = bass.Bass("TRN2", target_bir_lowering=False,
                   detect_race_conditions=DEV_DRAIN)

    ap_x = nc.dram_tensor("x", [BC, L, DM], F32, kind="ExternalInput").ap()
    ap_adj = nc.dram_tensor("adj_pm", [NM, 128, NM, 128], F32, kind="ExternalInput").ap()
    ap_w = nc.dram_tensor("w_pm", [2 * NM, 128, NKB, 128], F32, kind="ExternalInput").ap()
    ap_op = nc.dram_tensor("op_w", [DI, DM], F32, kind="ExternalInput").ap()
    ap_xp = nc.dram_tensor("xp_pm", [128, NM * 96], F32, kind="ExternalInput").ap()
    ap_dtw = nc.dram_tensor("dt_w", [DR, DI], F32, kind="ExternalInput").ap()
    ap_apm = nc.dram_tensor("a_pm", [128, NM * NST], F32, kind="ExternalInput").ap()
    ap_cw = nc.dram_tensor("convw_pm", [128, NM * 4], F32, kind="ExternalInput").ap()
    ap_cb = nc.dram_tensor("convb_pm", [128, NM], F32, kind="ExternalInput").ap()
    ap_dtb = nc.dram_tensor("dtb_pm", [128, NM], F32, kind="ExternalInput").ap()
    ap_dd = nc.dram_tensor("d_pm", [128, NM], F32, kind="ExternalInput").ap()
    ap_fw = nc.dram_tensor("ffn_w", [DM, DM], F32, kind="ExternalInput").ap()
    ap_h = nc.dram_tensor("h_bf", [DM, DM], BF16, kind="ExternalInput").ap()
    ap_eye = nc.dram_tensor("eye_bf", [128, 128], BF16, kind="ExternalInput").ap()

    ap_out = nc.dram_tensor("out", [BC, L, DM], F32, kind="ExternalOutput").ap()
    dbg_aps = {}
    if DBG:
        for nm, shape in [("d_xn", [128, NTI * DM]), ("d_xnT", [128, NKB * TC]),
                          ("d_xi", [128, NM * TC]), ("d_dT", [128, NM * TC]),
                          ("d_yf", [128, NM * TC]), ("d_xq", [128, NTI * DM]),
                          ("d_hc", [128, NM * NST]), ("d_brep", [128, 8 * TC]),
                          ("d_crep", [128, 8 * TC]), ("d_asb", [128, NM * NST]),
                          ("d_su", [128, TC]), ("d_al", [128, 8 * TC]),
                          ("d_hh", [128, 8 * TC]), ("d_wv", [128, 8 * TC]),
                          ("d_stat", [128, 16]), ("d_ln", [128, 16])]:
            dbg_aps[nm] = nc.dram_tensor(nm, shape, F32, kind="ExternalOutput").ap()
    d_adjbf = nc.dram_tensor("adj_bf", [NM, 128, NM, 128], BF16, kind="Internal").ap()
    d_wbf = nc.dram_tensor("w_bf", [2 * NM, 128, NKB, 128], BF16, kind="Internal").ap()
    d_opbf = nc.dram_tensor("op_bf", [DI, DM], BF16, kind="Internal").ap()
    d_ternbf = nc.dram_tensor("tern_bf", [DM, DM], BF16, kind="Internal").ap()
    d_dtw = nc.dram_tensor("dtw_bf", [DR, DI], BF16, kind="Internal").ap()
    d_bc = nc.dram_tensor("bc_scr", [32, TC], BF16, kind="Internal").ap()

    with contextlib.ExitStack() as stack:
        def sb(name, shape, dtype):
            return stack.enter_context(nc.sbuf_tensor(name, shape, dtype))

        def ps_(name, shape, dtype):
            return stack.enter_context(nc.psum_tensor(name, shape, dtype))

        eye_sb = sb("eye_sb", [128, 128], BF16)
        xpw_sb = sb("xpw_sb", [128, NM * 96], BF16)
        dtbuf = [sb(f"dtbuf{i}", [DR, 128], BF16) for i in range(2)]
        a_sb = sb("a_sb", [128, NM * NST], F32)
        cw_sb = sb("cw_sb", [128, NM * 4], F32)
        cb_sb = sb("cb_sb", [128, NM], F32)
        dtb_sb = sb("dtb_sb", [128, NM], F32)
        dd_sb = sb("dd_sb", [128, NM], F32)
        hcarry = sb("hcarry", [128, NM * NST], F32)
        halo_sb = sb("halo_sb", [128, NM * 3], F32)
        stat = sb("stat", [128, 16], F32)
        cst = sb("cst", [128, 2], F32)
        junk_v = sb("junk_v", [128, 4], F32)
        junk_s = sb("junk_s", [128, 4], F32)

        ln_snap = sb("ln_snap", [128, 16], F32)
        x_ld = [sb(f"x_ld{i}", [128, DM], F32) for i in range(2)]
        xn_sb = sb("xn_sb", [128, NTI * DM], BF16)
        xnT_sb = sb("xnT_sb", [128, NKB * TC], BF16)
        xq_sb = sb("xq_sb", [128, NTI * DM], BF16)
        xqT_sb = sb("xqT_sb", [128, NKB * TC], BF16)
        hbuf = sb("hbuf", [128, NKB * DM], BF16)
        wibuf = [sb(f"wibuf{i}", [128, NKB * 128], BF16) for i in range(2)]
        rwbuf = [sb(f"rwbuf{i}", [128, NKB * 128], BF16) for i in range(2)]
        adjbuf = [sb(f"adjbuf{i}", [128, DI], BF16) for i in range(2)]
        opbuf = [sb(f"opbuf{i}", [128, 1024], BF16) for i in range(2)]
        ternbuf = [sb(f"ternbuf{i}", [128, 1024], BF16) for i in range(2)]
        xip_sb = [sb(f"xip{i}", [128, TC + 3], BF16) for i in range(2)]
        cacc = [sb(f"cacc{i}", [128, TC], F32) for i in range(2)]
        xi_sb = sb("xi_sb", [128, NM * TC], BF16)
        dinT_sb = sb("dinT_sb", [DR, TC], BF16)
        deltaT_sb = sb("deltaT_sb", [128, NM * TC], BF16)
        su_sb = [sb(f"su{i}", [128, TC], BF16) for i in range(2)]
        brep = [sb(f"brep{g}", [128, 8 * TC], BF16) for g in range(2)]
        crep = [sb(f"crep{g}", [128, 8 * TC], BF16) for g in range(2)]
        al_sb = sb("al_sb", [128, 8 * TC], BF16)
        wv_sb = sb("wv_sb", [128, 8 * TC], BF16)
        hh_sb = sb("hh_sb", [128, 8 * TC], BF16)
        yg_sb = sb("yg_sb", [128, 2 * TC], BF16)
        t2_sb = sb("t2_sb", [128, TC], BF16)
        yf_sb = sb("yf_sb", [128, NM * TC], BF16)
        out_sb = [sb(f"out_sb{i}", [128, 512], F32) for i in range(2)]
        bc_sb = sb("bc_sb", [96, TC], BF16)

        ps_tr = [ps_(f"ps_tr{i}", [128, 1024], BF16) for i in range(2)]
        ps_a = [ps_(f"ps_a{i}", [128, 512], F32) for i in range(2)]
        ps_ip = [ps_(f"ps_ip{i}", [128, 512], F32) for i in range(2)]
        ps_dp = [ps_(f"ps_dp{i}", [128, 512], F32) for i in range(2)]

        sems = {}

        def S_(name):
            sems[name] = Sem(stack, nc, name)
            return sems[name]

        L_xld = [S_(f"L_xld{i}") for i in range(2)]
        L_wib = [S_(f"L_wib{i}") for i in range(2)]
        L_rwb = [S_(f"L_rwb{i}") for i in range(2)]
        L_adjb = [S_(f"L_adjb{i}") for i in range(2)]
        L_opb = [S_(f"L_opb{i}") for i in range(2)]
        L_ternb = [S_(f"L_ternb{i}") for i in range(2)]
        L_dtb = [S_(f"L_dtb{i}") for i in range(2)]
        P_wib = [S_(f"P_wib{i}") for i in range(2)]
        P_adjb = [S_(f"P_adjb{i}") for i in range(2)]
        L_hb = S_("L_hb"); L_small = S_("L_small"); L_h = S_("L_h")
        L_bcd = S_("L_bcd"); L_rep = S_("L_rep")
        S_os = [S_(f"S_os{i}") for i in range(2)]
        T_trn = S_("T_trn"); T_trq = S_("T_trq"); T_xh = S_("T_xh")
        T_ip = S_("T_ip"); T_res = S_("T_res"); T_xdbl = S_("T_xdbl")
        T_dt = S_("T_dt"); T_dp = S_("T_dp"); T_opk = S_("T_opk")
        T_tek = S_("T_tek"); T_out = S_("T_out"); T_ptr = S_("T_ptr")
        V_xn = S_("V_xn"); V_q = S_("V_q"); V_conv = S_("V_conv"); V_bc = S_("V_bc")
        V_su = S_("V_su"); V_w = S_("V_w"); V_scan = S_("V_scan"); V_y = S_("V_y")
        V_oev = S_("V_oev"); V_cast = S_("V_cast"); V_pt = S_("V_pt")
        A_rstd = S_("A_rstd"); A_evxn = S_("A_evxn"); A_evxq = S_("A_evxq")
        A_evxi = S_("A_evxi"); A_silu = S_("A_silu"); A_res = S_("A_res")
        A_dinT = S_("A_dinT"); A_sp = S_("A_sp"); A_al = S_("A_al"); A_pev = S_("A_pev")

        S = Sched()
        S.junk = {"ve": junk_v, "sc": junk_s}

        r_xld = Ring(2); r_wib = Ring(2); r_rwb = Ring(2)
        r_adjb = Ring(2); r_opb = Ring(2); r_ternb = Ring(2)
        r_xip = Ring(2); r_cacc = Ring(2); r_su = Ring(2); r_out = Ring(2)
        r_pstr = Ring(2)
        r_psa = Ring(2); r_psip = Ring(2); r_psdp = Ring(2)
        r_xn = Ring(1); r_xnT = Ring(1); r_xq = Ring(1); r_xqT = Ring(1)
        r_xi = Ring(1); r_dT = Ring(1); r_dinT = Ring(1); r_yf = Ring(1)
        r_al = Ring(2); r_rep = Ring(1); r_bcs = Ring(1)
        r_ternsc = Ring(1); r_hbsc = Ring(1)

        S.o("sy", lambda e: e.dma_start(out=eye_sb[:], in_=ap_eye), inc=L_small, dma=True)

        for m in range(NM):
            for piece in range(2):
                idx = m * 2 + piece
                r_xld.acquire(S, "sy", idx)
                src = ap_adj[m].rearrange("p a b -> p (a b)")[:, piece * 1024:(piece + 1) * 1024]
                S.o("sy", lambda e, s=src, d=x_ld[idx % 2]: e.dma_start(out=d[:], in_=s),
                    inc=L_xld[idx % 2], dma=True)
                r_adjb.acquire(S, "ve", m) if piece == 0 else None
                S.w("ve", L_xld[idx % 2])
                S.o("ve", lambda e, sl=x_ld[idx % 2], d=adjbuf[m % 2], p=piece:
                    e.tensor_copy(d[:, p * 1024:(p + 1) * 1024], sl[:]), inc=V_cast)
                r_xld.release(idx, V_cast)
            dst = d_adjbf[m].rearrange("p a b -> p (a b)")
            S.w("gp", V_cast)
            S.o("gp", lambda e, s=adjbuf[m % 2], d=dst: e.dma_start(out=d, in_=s[:]),
                inc=P_adjb[m % 2], dma=True)
            r_adjb.release(m, P_adjb[m % 2])
        for m in range(2 * NM):
            idx = 32 + m
            r_xld.acquire(S, "sy", idx)
            src = ap_w[m].rearrange("p a b -> p (a b)")
            S.o("sy", lambda e, s=src, d=x_ld[idx % 2]: e.dma_start(out=d[:], in_=s),
                inc=L_xld[idx % 2], dma=True)
            r_wib.acquire(S, "ve", m)
            S.w("ve", L_xld[idx % 2])
            S.o("ve", lambda e, sl=x_ld[idx % 2], d=wibuf[m % 2]:
                e.tensor_copy(d[:], sl[:]), inc=V_cast)
            r_xld.release(idx, V_cast)
            dst = d_wbf[m].rearrange("p a b -> p (a b)")
            S.w("gp", V_cast)
            S.o("gp", lambda e, s=wibuf[m % 2], d=dst: e.dma_start(out=d, in_=s[:]),
                inc=P_wib[m % 2], dma=True)
            r_wib.release(m, P_wib[m % 2])
        for kb in range(NM):
            idx = 64 + kb
            r_xld.acquire(S, "sy", idx)
            src = ap_op[kb * 128:(kb + 1) * 128, :]
            S.o("sy", lambda e, s=src, d=x_ld[idx % 2]: e.dma_start(out=d[:], in_=s),
                inc=L_xld[idx % 2], dma=True)
            r_wib.acquire(S, "ve", 32 + kb)
            S.w("ve", L_xld[idx % 2])
            S.o("ve", lambda e, sl=x_ld[idx % 2], d=wibuf[kb % 2]:
                e.tensor_copy(d[:], sl[:]), inc=V_cast)
            r_xld.release(idx, V_cast)
            dst = d_opbf[kb * 128:(kb + 1) * 128, :]
            S.w("gp", V_cast)
            S.o("gp", lambda e, s=wibuf[kb % 2], d=dst: e.dma_start(out=d, in_=s[:]),
                inc=P_wib[kb % 2], dma=True)
            r_wib.release(32 + kb, P_wib[kb % 2])
        for piece in range(2):
            idx = 80 + piece
            r_xld.acquire(S, "sy", idx)
            S.o("sy", lambda e, p=piece, d=x_ld[idx % 2]:
                e.dma_start(out=d[:, 0:768], in_=ap_xp[:, p * 768:(p + 1) * 768]),
                inc=L_xld[idx % 2], dma=True)
            S.w("ve", L_xld[idx % 2])
            S.o("ve", lambda e, p=piece, sl=x_ld[idx % 2]:
                e.tensor_copy(xpw_sb[:, p * 768:(p + 1) * 768], sl[:, 0:768]), inc=V_cast)
            r_xld.release(idx, V_cast)
        for piece in range(2):
            idx = 82 + piece
            r_xld.acquire(S, "sy", idx)
            S.o("sy", lambda e, p=piece, d=x_ld[idx % 2]:
                e.dma_start(out=d[0:DR, :], in_=ap_dtw[:, p * 1024:(p + 1) * 1024]),
                inc=L_xld[idx % 2], dma=True)
            r_wib.acquire(S, "ve", 48 + piece)
            S.w("ve", L_xld[idx % 2])
            S.o("ve", lambda e, p=piece, sl=x_ld[idx % 2]:
                e.tensor_copy(wibuf[p % 2][0:DR, 0:1024], sl[0:DR, :]), inc=V_cast)
            r_xld.release(idx, V_cast)
            S.w("gp", V_cast)
            S.o("gp", lambda e, p=piece: e.dma_start(
                out=d_dtw[:, p * 1024:(p + 1) * 1024], in_=wibuf[p % 2][0:DR, 0:1024]),
                inc=P_wib[piece % 2], dma=True)
            r_wib.release(48 + piece, P_wib[piece % 2])
        S.o("sy", lambda e: e.dma_start(out=a_sb[:], in_=ap_apm), inc=L_small, dma=True)
        S.o("sy", lambda e: e.dma_start(out=cw_sb[:], in_=ap_cw), inc=L_small, dma=True)
        S.o("sy", lambda e: e.dma_start(out=cb_sb[:], in_=ap_cb), inc=L_small, dma=True)
        S.o("sy", lambda e: e.dma_start(out=dtb_sb[:], in_=ap_dtb), inc=L_small, dma=True)
        S.o("sy", lambda e: e.dma_start(out=dd_sb[:], in_=ap_dd), inc=L_small, dma=True)
        S.w("sc", L_small, 96)
        S.o("sc", lambda e: e.activation(a_sb[:], a_sb[:], AF.Exp), inc=A_al)
        S.w("ve", A_al)
        S.o("ve", lambda e: e.tensor_scalar_mul(a_sb[:], a_sb[:], -1.0), inc=V_cast)
        S.o("ve", lambda e: e.memset(hcarry[:], 0.0), inc=V_cast)
        S.o("ve", lambda e: e.memset(halo_sb[:], 0.0), inc=V_cast)
        S.o("ve", lambda e: e.memset(cst[:, 0:1], EPS), inc=V_cast)
        S.o("ve", lambda e: e.memset(cst[:, 1:2], -0.5), inc=V_cast)
        S.o("ve", lambda e: e.memset(junk_v[:], 0.0))
        S.o("sc", lambda e: e.memzero(junk_s[:]))

        for ob in range(NKB):
            idx = 84 + ob
            r_xld.acquire(S, "sy", idx)
            S.o("sy", lambda e, o=ob, d=x_ld[idx % 2]:
                e.dma_start(out=d[:], in_=ap_fw[o * 128:(o + 1) * 128, :]),
                inc=L_xld[idx % 2], dma=True)
            S.w("ve", L_xld[idx % 2])
            xl = x_ld[idx % 2]
            for hh in range(2):
                r_cacc.acquire(S, "ve", hh)
                S.o("ve", lambda e, sl=xl, h=hh: e.tensor_scalar(
                    cacc[h][:], sl[:, h * 512:(h + 1) * 512], -1.0, None, op0=ALU.mult),
                    fence=True)
                S.o("ve", lambda e, sl=xl, h=hh: e.tensor_max(
                    cacc[h][:], cacc[h][:], sl[:, h * 512:(h + 1) * 512]), fence=True)
                S.o("ve", lambda e, h=hh: e.tensor_reduce(
                    stat[:, 3 + h:4 + h], cacc[h][:], axis=mybir.AxisListType.X,
                    op=ALU.add), fence=True)
            S.o("ve", lambda e: e.tensor_add(stat[:, 0:1], stat[:, 3:4], stat[:, 4:5]), fence=True)
            S.o("ve", lambda e: e.tensor_scalar_mul(stat[:, 0:1], stat[:, 0:1], 1.0 / DM), fence=True)
            S.o("ve", lambda e: e.tensor_scalar_max(stat[:, 0:1], stat[:, 0:1], 1e-5), fence=True)
            S.o("ve", lambda e: e.tensor_scalar_mul(stat[:, 14:15], stat[:, 0:1], 0.5), fence=True)
            S.o("ve", lambda e: e.tensor_scalar_mul(stat[:, 15:16], stat[:, 14:15], -1.0), fence=True)
            r_ternsc.acquire(S, "ve", 0)
            for hh in range(2):
                cs = slice(hh * 512, (hh + 1) * 512)
                r_cacc.acquire(S, "ve", hh)
                S.o("ve", lambda e, sl=xl, c=cs: e.tensor_scalar(
                    cacc[0][:], sl[:, c], stat[:, 14:15], None, op0=ALU.is_gt))
                S.o("ve", lambda e, sl=xl, c=cs: e.tensor_scalar(
                    cacc[1][:], sl[:, c], stat[:, 15:16], None, op0=ALU.is_lt), fence=True)
                S.o("ve", lambda e: e.tensor_sub(cacc[0][:], cacc[0][:], cacc[1][:]), fence=True)
                S.o("ve", lambda e, o=ob, c=cs: e.tensor_scalar(
                    xq_sb[:, c], cacc[0][:], stat[:, 0:1], None, op0=ALU.mult))
                S.fence("ve", inc=V_pt)
                r_cacc.release(hh, V_pt)
            r_xld.release(idx, V_pt)
            r_pstr.acquire(S, "te", 0); r_pstr.acquire(S, "te", 1)
            S.w("te", V_pt)
            S.w("te", L_small, 96)
            for ib in range(NKB):
                S.o("te", lambda e, i=ib: e.transpose(
                    ps_tr[i // 4][:, (i % 4) * 128:(i % 4 + 1) * 128],
                    xq_sb[:, i * 128:(i + 1) * 128],
                    eye_sb[:]), inc=(T_ptr if ib == NKB - 1 else None))
            r_ternsc.release(0, T_ptr)
            r_hbsc.acquire(S, "sc", 0)
            S.w("sc", T_ptr)
            S.o("sc", lambda e: e.copy(hbuf[:, 0:512], ps_tr[0][:, 0:512]))
            S.o("sc", lambda e: e.copy(hbuf[:, 512:1024], ps_tr[1][:, 0:512]), inc=A_pev)
            r_pstr.release(0, A_pev); r_pstr.release(1, A_pev)
            dst = d_ternbf.rearrange("(ib p) o -> p ib o", ib=NKB)[:, :, ob * 128:(ob + 1) * 128]
            S.w("gp", A_pev)
            S.o("gp", lambda e, d=dst: e.dma_start(out=d, in_=hbuf[:, 0:1024].rearrange(
                "p (ib c) -> p ib c", ib=NKB)), inc=L_hb, dma=True)
            r_hbsc.release(0, L_hb)
        S.w("sy", L_hb)
        S.o("sy", lambda e: e.dma_start(
            out=hbuf[:].rearrange("p (kb c) -> p kb c", kb=NKB),
            in_=ap_h.rearrange("(kb p) c -> p kb c", p=128)),
            inc=L_h, dma=True)
        r_xq.release(0, T_ptr)

        PRO_WR = [(P_wib[0], P_wib[0].n), (P_wib[1], P_wib[1].n),
                  (P_adjb[0], P_adjb[0].n), (P_adjb[1], P_adjb[1].n),
                  (L_hb, L_hb.n)]

        for ci in range(NCH):
            b = ci // (L // TC)
            cs = ci % (L // TC)
            t0 = cs * TC

            if cs == 0:
                S.o("ve", lambda e: e.memset(hcarry[:], 0.0))
                S.o("ve", lambda e: e.memset(halo_sb[:], 0.0))
            def emit_xload(ti):
                gi = ci * NTI + ti
                r_xld.acquire(S, "sc", 100 + gi)
                src = ap_x[b, t0 + ti * 128: t0 + (ti + 1) * 128, :]
                S.o("sc", lambda e, s=src, d=x_ld[gi % 2]: e.dma_start(out=d[:], in_=s),
                    inc=L_xld[gi % 2], dma=True)

            emit_xload(0)
            emit_xload(1)
            vxn_tgt_a = []
            for ti in range(NTI):
                gi = ci * NTI + ti
                xl = x_ld[gi % 2]
                S.w("ve", L_xld[gi % 2])
                S.o("ve", lambda e, sl=xl: e.reduce_sum(stat[:, 0:1], sl[:],
                                                        axis=mybir.AxisListType.X),
                    fence=True)
                r_cacc.acquire(S, "sc", 2 * gi)
                r_cacc.acquire(S, "sc", 2 * gi + 1)
                S.w("sc", L_xld[gi % 2])
                S.w("sc", V_xn)
                S.o("sc", lambda e, sl=xl: e.activation(
                    cacc[0][:], sl[:, 0:512], AF.Square, accum_out=stat[:, 1:2]))
                S.o("sc", lambda e, sl=xl: e.activation(
                    cacc[1][:], sl[:, 512:1024], AF.Square, accum_out=stat[:, 2:3]))
                S.fence("sc", inc=A_rstd)
                r_cacc.release(2 * gi, A_rstd)
                r_cacc.release(2 * gi + 1, A_rstd)
                S.w("ve", A_rstd)
                S.o("ve", lambda e: e.tensor_add(stat[:, 3:4], stat[:, 1:2], stat[:, 2:3]), fence=True)
                S.o("ve", lambda e: e.tensor_scalar_mul(stat[:, 4:5], stat[:, 0:1], 1.0 / DM), fence=True)
                S.o("ve", lambda e: e.tensor_scalar_mul(stat[:, 5:6], stat[:, 3:4], 1.0 / DM), fence=True)
                S.o("ve", lambda e: e.tensor_mul(stat[:, 6:7], stat[:, 4:5], stat[:, 4:5]), fence=True)
                S.o("ve", lambda e: e.tensor_sub(stat[:, 5:6], stat[:, 5:6], stat[:, 6:7]))
                S.fence("ve", inc=V_xn)
                S.w("sc", V_xn)
                S.o("sc", lambda e: e.activation(stat[:, 7:8], stat[:, 5:6], AF.Ln,
                                                 bias=cst[:, 0:1]), fence=True)
                S.o("sc", lambda e: e.activation(stat[:, 7:8], stat[:, 7:8], AF.Exp,
                                                 scale=cst[:, 1:2]))
                S.fence("sc", inc=A_rstd)
                r_xn.acquire(S, "ve", 0) if ti == 0 else None
                S.w("ve", A_rstd)
                if DBG and ci == 0 and ti == 0:
                    S.o("ve", lambda e: e.tensor_copy(ln_snap[:], stat[:]))
                S.o("ve", lambda e, sl=xl, t=ti: e.tensor_scalar(
                    xn_sb[:, t * DM:(t + 1) * DM], sl[:], stat[:, 4:5], stat[:, 7:8],
                    op0=ALU.subtract, op1=ALU.mult))
                S.fence("ve", inc=V_xn)
                vxn_tgt_a.append(V_xn.n)
                r_xld.release(100 + gi, V_xn)
                if ti + 2 < NTI:
                    emit_xload(ti + 2)

            if PH < 2:
                continue
            r_xnT.acquire(S, "te", 0)
            for kb in range(NKB):
                r_pstr.acquire(S, "te", kb)
                for ti in range(NTI):
                    if kb == 0:
                        S.w("te", V_xn, vxn_tgt_a[ti])
                    S.o("te", lambda e, t=ti, k=kb: e.transpose(
                        ps_tr[k % 2][:, t * 128:(t + 1) * 128],
                        xn_sb[:, t * DM + k * 128: t * DM + (k + 1) * 128], eye_sb[:]),
                        inc=(T_trn if ti == NTI - 1 else None))
                S.w("sc", T_trn)
                S.o("sc", lambda e, k=kb: e.copy(
                    xnT_sb[:, k * TC:(k + 1) * TC],
                    ps_tr[k % 2][:, 0:512]), inc=A_evxn)
                r_pstr.release(kb, A_evxn)
            r_xn.release(0, T_trn)

            if PH < 3:
                continue
            r_xq.acquire(S, "ve", 0)
            for ti in range(NTI):
                for half in range(2):
                    r_psa.acquire(S, "te", half)
                    if ti == 0 and half == 0:
                        S.w("te", L_h, 16)
                    for kb in range(NKB):
                        if ti == 0 and half == 0:
                            S.w("te", A_evxn, A_evxn.n - NKB + kb + 1)
                        S.o("te", lambda e, t=ti, h=half, k=kb: e.matmul(
                            ps_a[h][:],
                            xnT_sb[:, k * TC + t * 128: k * TC + (t + 1) * 128],
                            hbuf[:, k * DM + h * 512: k * DM + (h + 1) * 512],
                            start=(k == 0), stop=(k == NKB - 1)),
                            inc=(T_xh if kb == NKB - 1 else None))
                S.w("ve", T_xh)
                for half in range(2):
                    r_cacc.acquire(S, "ve", 40 + 8 * ci + 2 * ti + half)
                    S.o("ve", lambda e, h=half: e.tensor_copy(cacc[h][:], ps_a[h][:]),
                        fence=(half == 1))
                    r_psa.release(half, V_q, V_q.n + 1)
                S.o("ve", lambda e: e.tensor_reduce(
                    stat[:, 8:9], cacc[0][:], axis=mybir.AxisListType.X, op=ALU.max))
                S.o("ve", lambda e: e.tensor_reduce(
                    stat[:, 9:10], cacc[1][:], axis=mybir.AxisListType.X, op=ALU.max))
                S.o("ve", lambda e: e.tensor_reduce(
                    stat[:, 11:12], cacc[0][:], axis=mybir.AxisListType.X, op=ALU.min))
                S.o("ve", lambda e: e.tensor_reduce(
                    stat[:, 12:13], cacc[1][:], axis=mybir.AxisListType.X, op=ALU.min),
                    fence=True)
                S.o("ve", lambda e: e.tensor_max(stat[:, 8:9], stat[:, 8:9], stat[:, 9:10]), fence=True)
                S.o("ve", lambda e: e.tensor_tensor(stat[:, 11:12], stat[:, 11:12], stat[:, 12:13], op=ALU.min), fence=True)
                S.o("ve", lambda e: e.tensor_scalar(stat[:, 11:12], stat[:, 11:12], -1.0, None, op0=ALU.mult), fence=True)
                S.o("ve", lambda e: e.tensor_max(stat[:, 8:9], stat[:, 8:9], stat[:, 11:12]), fence=True)
                S.o("ve", lambda e: e.tensor_scalar_max(stat[:, 8:9], stat[:, 8:9], 1e-5), fence=True)
                S.o("ve", lambda e: e.tensor_scalar_mul(stat[:, 8:9], stat[:, 8:9], 1.0 / 7.0))
                S.fence("ve", inc=V_q)
                S.w("sc", V_q)
                S.o("sc", lambda e: e.activation(stat[:, 10:11], stat[:, 8:9], AF.Ln),
                    fence=True)
                S.o("sc", lambda e: e.activation(stat[:, 9:10], stat[:, 10:11], AF.Exp,
                                                 scale=-1.0))
                S.fence("sc", inc=A_rstd)
                S.w("ve", A_rstd)
                for half in range(2):
                    S.o("ve", lambda e, h=half: e.tensor_scalar(
                        cacc[h][:], cacc[h][:], stat[:, 9:10], RNE_C,
                        op0=ALU.mult, op1=ALU.add), inc=V_q, fence=True)
                    S.o("ve", lambda e, h=half, t=ti: e.tensor_scalar(
                        xq_sb[:, t * DM + h * 512: t * DM + (h + 1) * 512],
                        cacc[h][:], RNE_C, stat[:, 8:9],
                        op0=ALU.subtract, op1=ALU.mult))
                    S.fence("ve", inc=V_q)
                    r_cacc.release(40 + 8 * ci + 2 * ti + half, V_q)

            if PH < 4:
                continue
            r_xi.acquire(S, "sc", 0)
            r_yf.acquire(S, "sc", 0)
            def emit_ip(m):
                r_wib.acquire(S, "sy", m)
                if ci == 0 and m == 0:
                    for (ps_sem, ps_t) in PRO_WR:
                        S.w("sy", ps_sem, ps_t)
                src = d_wbf[m].rearrange("p a b -> p (a b)")
                S.o("sy", lambda e, s=src, d=wibuf[m % 2]: e.dma_start(out=d[:], in_=s),
                    inc=L_wib[m % 2], dma=True)
                r_psip.acquire(S, "te", m)
                S.w("te", L_wib[m % 2])
                if m == 0:
                    S.w("te", A_evxn)
                for kb in range(NKB):
                    S.o("te", lambda e, k=kb, mm=m: e.matmul(
                        ps_ip[mm % 2][:], wibuf[mm % 2][:, k * 128:(k + 1) * 128],
                        xnT_sb[:, k * TC:(k + 1) * TC],
                        start=(k == 0), stop=(k == NKB - 1)),
                        inc=(T_ip if kb == NKB - 1 else None))
                r_wib.release(m, T_ip)

            def emit_res(m):
                r_rwb.acquire(S, "sy", m)
                src2 = d_wbf[NM + m].rearrange("p a b -> p (a b)")
                S.o("sy", lambda e, s=src2, d=rwbuf[m % 2]: e.dma_start(out=d[:], in_=s),
                    inc=L_rwb[m % 2], dma=True)
                r_psdp.acquire(S, "te", m)
                S.w("te", L_rwb[m % 2])
                for kb in range(NKB):
                    S.o("te", lambda e, k=kb, mm=m: e.matmul(
                        ps_dp[mm % 2][:], rwbuf[mm % 2][:, k * 128:(k + 1) * 128],
                        xnT_sb[:, k * TC:(k + 1) * TC],
                        start=(k == 0), stop=(k == NKB - 1)),
                        inc=(T_res if kb == NKB - 1 else None))
                r_rwb.release(m, T_res)

            def emit_evacxi(m):
                r_xip.acquire(S, "sc", m)
                S.w("sc", T_ip)
                S.o("sc", lambda e, mm=m: e.copy(xip_sb[mm % 2][:, 3:3 + TC],
                                                 ps_ip[mm % 2][:]))
                S.fence("sc", inc=A_evxi)
                r_psip.release(m, A_evxi)

            def emit_conv(m):
                r_cacc.acquire(S, "ve", m)
                S.w("ve", A_evxi, aevxi0 + m + 1)
                S.o("ve", lambda e, mm=m: e.tensor_copy(
                    xip_sb[mm % 2][:, 0:3],
                    halo_sb[:, mm * 3:(mm + 1) * 3]))
                S.o("ve", lambda e, mm=m: e.tensor_scalar(
                    cacc[mm % 2][:], xip_sb[mm % 2][:, 0:TC], cw_sb[:, mm * 4:mm * 4 + 1],
                    None, op0=ALU.mult))
                for k in range(1, 4):
                    S.o("ve", lambda e, mm=m, kk=k: e.scalar_tensor_tensor(
                        cacc[mm % 2][:], xip_sb[mm % 2][:, kk:kk + TC],
                        cw_sb[:, mm * 4 + kk:mm * 4 + kk + 1], cacc[mm % 2][:],
                        op0=ALU.mult, op1=ALU.add))
                S.o("ve", lambda e, mm=m: e.tensor_copy(
                    halo_sb[:, mm * 3:(mm + 1) * 3],
                    xip_sb[mm % 2][:, TC:TC + 3]))
                S.fence("ve", inc=V_conv)
                r_xip.release(m, V_conv)

            def emit_silu(m):
                S.w("sc", V_conv)
                S.o("sc", lambda e, mm=m: e.activation(
                    xi_sb[:, mm * TC:(mm + 1) * TC], cacc[mm % 2][:], AF.Silu,
                    bias=cb_sb[:, mm:mm + 1]), inc=A_silu)
                r_cacc.release(m, A_silu)

            def emit_silures(m):
                S.w("sc", T_res, tres0 + m + 1)
                S.o("sc", lambda e, mm=m: e.activation(
                    yf_sb[:, mm * TC:(mm + 1) * TC], ps_dp[mm % 2][:], AF.Silu),
                    inc=A_res)
                r_psdp.release(m, A_res)

            aevxi0 = A_evxi.n
            tres0 = T_res.n
            asilu0 = A_silu.n

            def emit_xdbl(m):
                if m == 0:
                    r_psa.acquire(S, "te", 0)
                S.w("te", A_silu, asilu0 + m + 1)
                S.o("te", lambda e, mm=m: e.matmul(
                    ps_a[0][0:96, :], xpw_sb[:, mm * 96:(mm + 1) * 96],
                    xi_sb[:, mm * TC:(mm + 1) * TC],
                    start=(mm == 0), stop=(mm == NM - 1)),
                    inc=(T_xdbl if m == NM - 1 else None))

            for m in range(NM):
                emit_ip(m)
                if m >= 1:
                    emit_res(m - 1)
                emit_evacxi(m)
                if m >= 1:
                    emit_conv(m - 1)
                    emit_silu(m - 1)
                if m >= 2:
                    emit_silures(m - 2)
                if m >= 3:
                    emit_xdbl(m - 3)
            emit_res(NM - 1)
            emit_conv(NM - 1)
            emit_silu(NM - 1)
            for m in range(NM - 3, NM):
                emit_xdbl(m)
            r_dinT.acquire(S, "sc", 0)
            S.w("sc", T_xdbl)
            S.o("sc", lambda e: e.copy(dinT_sb[:], ps_a[0][0:DR, :]))
            S.fence("sc", inc=A_dinT)
            emit_silures(NM - 2)
            emit_silures(NM - 1)

            if PH < 5:
                continue
            r_xqT.acquire(S, "te", 0)
            for kb in range(NKB):
                r_pstr.acquire(S, "te", kb)
                if kb == 0:
                    S.w("te", V_q)
                for ti in range(NTI):
                    S.o("te", lambda e, t=ti, k=kb: e.transpose(
                        ps_tr[k % 2][:, t * 128:(t + 1) * 128],
                        xq_sb[:, t * DM + k * 128: t * DM + (k + 1) * 128], eye_sb[:]),
                        inc=(T_trq if ti == NTI - 1 else None))
                S.w("sc", T_trq)
                S.o("sc", lambda e, k=kb: e.copy(
                    xqT_sb[:, k * TC:(k + 1) * TC],
                    ps_tr[k % 2][:, 0:512]), inc=A_evxq)
                r_pstr.release(kb, A_evxq)
            r_xq.release(0, T_trq)

            r_bcs.acquire(S, "ve", 0)
            S.w("ve", T_xdbl)
            S.w("ve", A_dinT)
            S.o("ve", lambda e: e.tensor_copy(bc_sb[DR:96, :], ps_a[0][DR:96, :]))
            S.fence("ve", inc=V_bc)
            r_psa.release(0, A_dinT)
            r_psa.slots[0].append((V_bc, V_bc.n))
            S.w("gp", V_bc)
            S.o("gp", lambda e: e.dma_start(out=d_bc, in_=bc_sb[DR:96, :]), inc=L_bcd, dma=True)
            r_bcs.release(0, L_bcd)
            r_rep.acquire(S, "gp", 0)
            S.w("gp", L_bcd)
            for g in range(2):
                for j in range(8):
                    S.o("gp", lambda e, gg=g, jj=j: e.dma_start(
                        out=brep[gg][:, jj * TC:(jj + 1) * TC],
                        in_=d_bc[gg * 8 + jj:gg * 8 + jj + 1, :].broadcast_to([128, TC])),
                        inc=L_rep, dma=True)
                    S.o("gp", lambda e, gg=g, jj=j: e.dma_start(
                        out=crep[gg][:, jj * TC:(jj + 1) * TC],
                        in_=d_bc[16 + gg * 8 + jj:16 + gg * 8 + jj + 1, :].broadcast_to([128, TC])),
                        inc=L_rep, dma=True)
            r_dT.acquire(S, "sc", 0)
            for m in range(NM):
                S.w("sy", T_dt, max(0, T_dt.n - 1))
                S.o("sy", lambda e, mm=m: e.dma_start(
                    out=dtbuf[mm % 2][:], in_=d_dtw[:, mm * 128:(mm + 1) * 128]),
                    inc=L_dtb[m % 2], dma=True)
                r_psip.acquire(S, "te", 17 + m)
                S.w("te", L_dtb[m % 2])
                if m == 0:
                    S.w("te", A_dinT)
                S.o("te", lambda e, mm=m: e.matmul(
                    ps_ip[(17 + mm) % 2][:], dtbuf[mm % 2][:],
                    dinT_sb[:], start=True, stop=True), inc=T_dt)
                r_cacc.acquire(S, "sc", 17 + m)
                S.w("sc", T_dt)
                S.o("sc", lambda e, mm=m: e.activation(
                    cacc[(17 + mm) % 2][:], ps_ip[(17 + mm) % 2][:], AF.Exp,
                    bias=dtb_sb[:, mm:mm + 1]), fence=True)
                r_psip.release(17 + m, A_sp, A_sp.n + 1)
                S.o("sc", lambda e, mm=m: e.activation(
                    deltaT_sb[:, mm * TC:(mm + 1) * TC], cacc[(17 + mm) % 2][:],
                    AF.Ln, bias=1.0), inc=A_sp)
                r_cacc.release(17 + m, A_sp)
            if PH < 6:
                continue
            vy0 = V_y.n

            def emit_F_loads(half, tis=(0, 1, 2, 3), tgt=None):
                gi0 = ci * 8 + half * NTI
                v_base = V_oev.n
                tgt = {} if tgt is None else tgt
                for ti in tis:
                    gi2 = gi0 + ti
                    r_xld.acquire(S, "sc", 500 + gi2)
                    src = ap_x[b, t0 + ti * 128: t0 + (ti + 1) * 128,
                               half * 512:(half + 1) * 512]
                    S.o("sc", lambda e, s=src, d=x_ld[gi2 % 2]:
                        e.dma_start(out=d[:, 0:512], in_=s), inc=L_xld[gi2 % 2], dma=True)
                    tgt[ti] = L_xld[gi2 % 2].n
                    r_xld.release(500 + gi2, V_oev, v_base + ti + 1)
                return tgt

            slot6 = [(0, 0, ps_a[0]), (0, 1, ps_a[1]), (0, 2, ps_ip[0]),
                     (0, 3, ps_ip[1]), (1, 0, ps_dp[0]), (1, 1, ps_dp[1])]

            def emit_F_p1():
                tout0 = T_out.n
                for ib in range(NKB):
                    r_ternb.acquire(S, "sy", ib)
                    src = d_ternbf[ib * 128:(ib + 1) * 128, :]
                    S.o("sy", lambda e, s=src, d=ternbuf[ib % 2]: e.dma_start(out=d[:], in_=s),
                        inc=L_ternb[ib % 2], dma=True)
                    S.w("te", L_ternb[ib % 2])
                    if ib == 0:
                        S.w("te", A_evxq)
                        r_psa.acquire(S, "te", 0)
                        r_psa.acquire(S, "te", 1)
                        r_psip.acquire(S, "te", 40)
                        r_psip.acquire(S, "te", 41)
                        r_psdp.acquire(S, "te", 48)
                        r_psdp.acquire(S, "te", 49)
                    for si, (h, t, ps) in enumerate(slot6):
                        S.o("te", lambda e, k=ib, hh=h, tt=t, p=ps: e.matmul(
                            p[:],
                            xqT_sb[:, k * TC + tt * 128: k * TC + (tt + 1) * 128],
                            ternbuf[k % 2][:, hh * 512:(hh + 1) * 512],
                            start=(k == 0), stop=False),
                            inc=(T_tek if si == 5 else None))
                    r_ternb.release(ib, T_tek)
                for kb in range(NM):
                    r_opb.acquire(S, "sy", kb)
                    src = d_opbf[kb * 128:(kb + 1) * 128, :]
                    S.o("sy", lambda e, s=src, d=opbuf[kb % 2]: e.dma_start(out=d[:], in_=s),
                        inc=L_opb[kb % 2], dma=True)
                    S.w("te", L_opb[kb % 2])
                    S.w("te", V_y, vy0 + kb + 1)
                    for si, (h, t, ps) in enumerate(slot6):
                        S.o("te", lambda e, k=kb, hh=h, tt=t, p=ps: e.matmul(
                            p[:],
                            yf_sb[:, k * TC + tt * 128: k * TC + (tt + 1) * 128],
                            opbuf[k % 2][:, hh * 512:(hh + 1) * 512],
                            start=False, stop=(k == NM - 1)),
                            inc=((T_opk if si == 5 else None) if kb < NM - 1
                                 else T_out))
                    r_opb.release(kb, T_opk if kb < NM - 1 else T_out)
                return tout0

            def emit_F_p2():
                for kb in range(NM):
                    r_opb.acquire(S, "sy", 16 + kb)
                    src = d_opbf[kb * 128:(kb + 1) * 128, 512:1024]
                    S.o("sy", lambda e, s=src, d=opbuf[kb % 2]: e.dma_start(
                        out=d[:, 0:512], in_=s), inc=L_opb[kb % 2], dma=True)
                    S.w("te", L_opb[kb % 2])
                    if kb == 0:
                        r_psa.acquire(S, "te", 2)
                        r_psa.acquire(S, "te", 3)
                    for si, t in enumerate((2, 3)):
                        S.o("te", lambda e, k=kb, tt=t, p=ps_a[si]: e.matmul(
                            p[:],
                            yf_sb[:, k * TC + tt * 128: k * TC + (tt + 1) * 128],
                            opbuf[k % 2][:, 0:512], start=(k == 0), stop=False),
                            inc=(T_opk if si == 1 else None))
                    r_opb.release(16 + kb, T_opk)
                for ib in range(NKB):
                    r_ternb.acquire(S, "sy", 16 + ib)
                    src = d_ternbf[ib * 128:(ib + 1) * 128, 512:1024]
                    S.o("sy", lambda e, s=src, d=ternbuf[ib % 2]: e.dma_start(
                        out=d[:, 0:512], in_=s), inc=L_ternb[ib % 2], dma=True)
                    S.w("te", L_ternb[ib % 2])
                    for si, t in enumerate((2, 3)):
                        S.o("te", lambda e, k=ib, tt=t, p=ps_a[si]: e.matmul(
                            p[:],
                            xqT_sb[:, k * TC + tt * 128: k * TC + (tt + 1) * 128],
                            ternbuf[k % 2][:, 0:512], start=False, stop=(k == NKB - 1)),
                            inc=((T_tek if si == 1 else None) if ib < NKB - 1
                                 else T_out))
                    r_ternb.release(16 + ib, T_tek if ib < NKB - 1 else T_out)

            def emit_F_evacs(items, tout0, xre_tgt):
                for (oi, h, t, ps, rel) in items:
                    gi2 = ci * 8 + h * NTI + t
                    r_out.acquire(S, "ve", gi2)
                    S.w("ve", T_out, tout0 + oi + 1)
                    S.w("ve", L_xld[gi2 % 2], xre_tgt[t])
                    S.o("ve", lambda e, p=ps, g2=gi2: e.tensor_add(
                        out_sb[g2 % 2][:], p[:], x_ld[g2 % 2][:, 0:512]))
                    S.fence("ve", inc=V_oev)
                    rel()
                    dst = ap_out[b, t0 + t * 128: t0 + (t + 1) * 128,
                                 h * 512:(h + 1) * 512]
                    S.w("gp", V_oev)
                    S.o("gp", lambda e, d=dst, g2=gi2: e.dma_start(out=d, in_=out_sb[g2 % 2][:]),
                        inc=S_os[gi2 % 2], dma=True)
                    r_out.release(gi2, S_os[gi2 % 2])


            tdp_tgt = {}
            vsu_tgt = {}

            def emit_dp(m):
                r_adjb.acquire(S, "sy", 16 + m)
                if ci == 0 and m == 0:
                    for (ps_sem, ps_t) in PRO_WR:
                        S.w("sy", ps_sem, ps_t)
                src = d_adjbf[m].rearrange("p a b -> p (a b)")
                S.o("sy", lambda e, s=src, d=adjbuf[m % 2]: e.dma_start(out=d[:], in_=s),
                    inc=L_adjb[m % 2], dma=True)
                r_psdp.acquire(S, "te", 16 + m)
                S.w("te", L_adjb[m % 2])
                for kb in range(NM):
                    if m == 0:
                        S.w("te", A_sp, A_sp.n - NM + kb + 1)
                    S.o("te", lambda e, k=kb, mm=m: e.matmul(
                        ps_dp[mm % 2][:], adjbuf[mm % 2][:, k * 128:(k + 1) * 128],
                        deltaT_sb[:, k * TC:(k + 1) * TC],
                        start=(k == 0), stop=(k == NM - 1)),
                        inc=(T_dp if kb == NM - 1 else None))
                tdp_tgt[m] = T_dp.n
                r_adjb.release(16 + m, T_dp)

            def emit_evac(m):
                r_cacc.acquire(S, "ve", 70 + 16 * ci + m)
                S.w("ve", T_dp, tdp_tgt[m])
                S.o("ve", lambda e, mm=m: e.tensor_copy(cacc[mm % 2][:], ps_dp[mm % 2][:]))
                S.fence("ve", inc=V_su)
                vsu_tgt[m] = V_su.n
                r_psdp.release(16 + m, V_su)

            def emit_E(m):
                r_su.acquire(S, "ve", m)
                S.o("ve", lambda e, mm=m: e.tensor_mul(
                    su_sb[mm % 2][:], cacc[mm % 2][:], xi_sb[:, mm * TC:(mm + 1) * TC]))
                for g in range(2):
                    albuf = [al_sb, xq_sb][(2 * m + g) % 2]
                    r_al.acquire(S, "sc", 2 * m + g)
                    if ci == 0 and m == 0:
                        S.w("sc", T_trq)
                    if g == 0:
                        S.w("sc", V_su, vsu_tgt[m])
                    for j in range(8):
                        n = g * 8 + j
                        S.o("sc", lambda e, ab=albuf, mm=m, nn=n, jj=j: e.activation(
                            ab[:, jj * TC:(jj + 1) * TC], cacc[mm % 2][:], AF.Exp,
                            scale=a_sb[:, mm * NST + nn:mm * NST + nn + 1]),
                            inc=(A_al if j == 7 else None))
                    if g == 1:
                        r_cacc.release(70 + 16 * ci + m, A_al)
                    if m == 0 and g == 0:
                        S.w("ve", L_rep)
                    su_b = su_sb[m % 2][:].unsqueeze(1).broadcast_to([128, 8, TC])
                    bv = brep[g][:].rearrange("p (j t) -> p j t", j=8)
                    wvv = wv_sb[:].rearrange("p (j t) -> p j t", j=8)
                    S.o("ve", lambda e, sb_=su_b, b_=bv, w_=wvv: e.tensor_tensor(
                        w_, sb_, b_, op=ALU.mult), inc=V_w)
                    if g == 1:
                        r_su.release(m, V_w)
                    S.w("ve", A_al)
                    for j in range(8):
                        n = g * 8 + j
                        S.o("ve", lambda e, ab=albuf, mm=m, nn=n, jj=j: e.tensor_tensor_scan(
                            hh_sb[:, jj * TC:(jj + 1) * TC],
                            ab[:, jj * TC:(jj + 1) * TC],
                            wv_sb[:, jj * TC:(jj + 1) * TC],
                            hcarry[:, mm * NST + nn:mm * NST + nn + 1],
                            op0=ALU.mult, op1=ALU.add),
                            inc=(V_scan if j == 7 else None))
                    r_al.release(2 * m + g, V_scan)
                    S.fence("ve")
                    S.o("ve", lambda e, mm=m, gg=g: e.tensor_copy(
                        hcarry[:, mm * NST + gg * 8: mm * NST + gg * 8 + 8],
                        hh_sb[:, TC - 1::TC]))
                    S.o("ve", lambda e, gg=g: e.tensor_mul(wv_sb[:], hh_sb[:], crep[gg][:]))
                    S.o("ve", lambda e: e.tensor_add(
                        wv_sb[:, 0:4 * TC], wv_sb[:, 0:4 * TC], wv_sb[:, 4 * TC:8 * TC]))
                    S.o("ve", lambda e: e.tensor_add(
                        wv_sb[:, 4 * TC:6 * TC], wv_sb[:, 0:2 * TC], wv_sb[:, 2 * TC:4 * TC]))
                    S.o("ve", lambda e, gg=g: e.tensor_add(
                        yg_sb[:, gg * TC:(gg + 1) * TC], wv_sb[:, 4 * TC:5 * TC],
                        wv_sb[:, 5 * TC:6 * TC]))
                S.fence("ve")
                S.o("ve", lambda e: e.tensor_add(
                    yg_sb[:, 0:TC], yg_sb[:, 0:TC], yg_sb[:, TC:2 * TC]), fence=True)
                S.o("ve", lambda e, mm=m: e.scalar_tensor_tensor(
                    t2_sb[:], xi_sb[:, mm * TC:(mm + 1) * TC], dd_sb[:, mm:mm + 1],
                    yg_sb[:, 0:TC], op0=ALU.mult, op1=ALU.add), fence=True)
                S.w("ve", A_res)
                S.o("ve", lambda e, mm=m: e.tensor_mul(
                    yf_sb[:, mm * TC:(mm + 1) * TC], yf_sb[:, mm * TC:(mm + 1) * TC],
                    t2_sb[:]))
                S.fence("ve", inc=V_y)

            for m in range(NM):
                emit_dp(m)
                emit_evac(m)
                if m >= 1:
                    emit_E(m - 1)
            if PH >= 7:
                xre0 = emit_F_loads(0, tis=(0, 1))
                tout_0 = emit_F_p1()
            emit_E(NM - 1)
            if PH >= 7:
                emit_F_loads(0, tis=(2, 3), tgt=xre0)
                emit_F_evacs(
                    [(0, 0, 0, ps_a[0], lambda: r_psa.release(0, V_oev)),
                     (1, 0, 1, ps_a[1], lambda: r_psa.release(1, V_oev)),
                     (2, 0, 2, ps_ip[0],
                      lambda: r_psip.slots[0].append((V_oev, V_oev.n))),
                     (3, 0, 3, ps_ip[1],
                      lambda: r_psip.slots[1].append((V_oev, V_oev.n)))],
                    tout_0, xre0)
                xre1 = emit_F_loads(1)
                emit_F_evacs(
                    [(4, 1, 0, ps_dp[0],
                      lambda: r_psdp.slots[0].append((V_oev, V_oev.n))),
                     (5, 1, 1, ps_dp[1],
                      lambda: r_psdp.slots[1].append((V_oev, V_oev.n)))],
                    tout_0, xre1)
                emit_F_p2()
                emit_F_evacs(
                    [(6, 1, 2, ps_a[0], lambda: r_psa.release(2, V_oev)),
                     (7, 1, 3, ps_a[1], lambda: r_psa.release(3, V_oev))],
                    tout_0, xre1)

            r_rep.release(0, V_y)
            r_xi.release(0, V_y)
            r_dT.release(0, T_dp)
            r_dinT.release(0, T_dt)
            r_xnT.release(0, T_res)
            r_xqT.release(0, T_out)
            r_yf.release(0, T_out)
        if DBG:
            dbg_src = {"d_xn": xn_sb, "d_xnT": xnT_sb, "d_xi": xi_sb,
                       "d_dT": deltaT_sb, "d_yf": yf_sb, "d_xq": xq_sb,
                       "d_hc": hcarry, "d_brep": brep[0], "d_crep": crep[0],
                       "d_asb": a_sb, "d_su": su_sb[1], "d_al": al_sb,
                       "d_hh": hh_sb, "d_wv": wv_sb, "d_stat": stat,
                       "d_ln": ln_snap}
            dsem = S_("dsem"); dwr = S_("dwr")
            for _si in range(2):
                S.w("ve", S_os[_si], S_os[_si].n)
            for nm, t in dbg_src.items():
                cols = t.ap().shape[1]
                for p0 in range(0, cols, 1024):
                    pc = min(1024, cols - p0)
                    S.w("ve", dwr, dwr.n)
                    S.o("ve", lambda e, tt=t, a=p0, c=pc: e.tensor_copy(
                        x_ld[0][:, 0:c], tt[:, a:a + c]), inc=dsem)
                    S.w("gp", dsem, dsem.n)
                    S.o("gp", lambda e, nm_=nm, a=p0, c=pc: e.dma_start(
                        out=dbg_aps[nm_][:, a:a + c], in_=x_ld[0][:, 0:c]),
                        inc=dwr, dma=True)
            S.w("gp", dwr, dwr.n)

        for _si in range(2):
            S.w("gp", S_os[_si], S_os[_si].n)

        block = stack.enter_context(nc.Block())
        block.sync(lambda e: S.replay("sy", e))
        block.gpsimd(lambda e: S.replay("gp", e))
        block.vector(lambda e: S.replay("ve", e))
        block.scalar(lambda e: S.replay("sc", e))
        block.tensor(lambda e: S.replay("te", e))

    return nc


_CACHE = {}


def _hadamard(size):
    h = np.array([[1.0]], dtype=np.float64)
    while h.shape[0] < size:
        h = np.block([[h, h], [h, -h]])
    return (h / np.sqrt(size)).astype(np.float32)


def kernel(L0_attn_x, L0_attn_adj, L0_attn_in_proj_w, L0_attn_conv_w, L0_attn_conv_b,
           L0_attn_x_proj_w, L0_attn_dt_proj_w, L0_attn_dt_proj_b, L0_attn_A_log,
           L0_attn_D, L0_attn_out_proj_w, L0_ffn_weight):
    import ml_dtypes
    x = np.ascontiguousarray(np.asarray(L0_attn_x, dtype=np.float32))
    adj = np.asarray(L0_attn_adj, np.float32)
    W = np.asarray(L0_attn_in_proj_w, np.float32)
    adj_pm = np.ascontiguousarray(adj.reshape(16, 128, 16, 128).transpose(2, 1, 0, 3))
    w_pm = np.ascontiguousarray(W.reshape(8, 128, 32, 128).transpose(2, 1, 0, 3))
    op_w = np.ascontiguousarray(np.asarray(L0_attn_out_proj_w, np.float32))
    xp_pm = np.ascontiguousarray(
        np.asarray(L0_attn_x_proj_w, np.float32).reshape(16, 128, 96)
        .transpose(1, 0, 2).reshape(128, 16 * 96))
    dt_w = np.ascontiguousarray(np.asarray(L0_attn_dt_proj_w, np.float32))
    a_pm = np.ascontiguousarray(
        np.asarray(L0_attn_A_log, np.float32).reshape(16, 128, 16)
        .transpose(1, 0, 2).reshape(128, 256))
    convw_pm = np.ascontiguousarray(
        np.asarray(L0_attn_conv_w, np.float32)[:, 0, :].reshape(16, 128, 4)
        .transpose(1, 0, 2).reshape(128, 64))
    convb_pm = np.ascontiguousarray(np.asarray(L0_attn_conv_b, np.float32).reshape(16, 128).T)
    dtb_pm = np.ascontiguousarray(np.asarray(L0_attn_dt_proj_b, np.float32).reshape(16, 128).T)
    d_pm = np.ascontiguousarray(np.asarray(L0_attn_D, np.float32).reshape(16, 128).T)
    ffn_w = np.ascontiguousarray(np.asarray(L0_ffn_weight, np.float32))
    h_bf = np.ascontiguousarray(_hadamard(DM).astype(ml_dtypes.bfloat16))
    eye_bf = np.ascontiguousarray(np.eye(128, dtype=np.float32).astype(ml_dtypes.bfloat16))

    if "nc" not in _CACHE:
        _CACHE["nc"] = build_nc()
    nc = _CACHE["nc"]

    shared = dict(adj_pm=adj_pm, w_pm=w_pm, op_w=op_w, xp_pm=xp_pm, dt_w=dt_w,
                  a_pm=a_pm, convw_pm=convw_pm, convb_pm=convb_pm, dtb_pm=dtb_pm,
                  d_pm=d_pm, ffn_w=ffn_w, h_bf=h_bf, eye_bf=eye_bf)
    in_maps = []
    for c in range(NCORE):
        m = dict(shared)
        m["x"] = np.ascontiguousarray(x[c * BC:(c + 1) * BC])
        in_maps.append(m)
    res = run_bass_kernel_spmd(nc, in_maps, core_ids=list(range(NCORE)))
    out = np.concatenate([r["out"] for r in res.results], axis=0)
    return out.astype(np.float32)


# version 14 (final)
# speedup vs baseline: 1.2818x; 1.0052x over previous
import contextlib
import numpy as np
import concourse.bass as bass
import concourse.mybir as mybir
from concourse.bass_utils import run_bass_kernel_spmd

F32 = mybir.dt.float32
BF16 = mybir.dt.bfloat16
AF = mybir.ActivationFunctionType
ALU = mybir.AluOpType

NCORE = 8
BC = 2
L = 2048
DM = 1024
DI = 2048
NST = 16
DR = 64
TC = 512
NCH = BC * L // TC
NTI = TC // 128
NM = DI // 128
NKB = DM // 128
RNE_C = float(3 * 2.0 ** 22)
EPS = 1e-5

ENGS = ("sy", "gp", "ve", "sc", "te")

DEV_DRAIN = False
PH = 7
PHE = 6
ALV = 0
DBG = False


class Sem:
    def __init__(self, stack, nc, name):
        self.h = stack.enter_context(nc.semaphore(name))
        self.n = 0
        self.name = name


class Sched:

    def __init__(self):
        self.q = {e: [] for e in ENGS}

    def o(self, eng, fn, inc=None, dma=False, fence=False):
        incs = []
        if inc is not None:
            if not isinstance(inc, list):
                inc = [inc]
            for s in inc:
                k = 16 if dma else 1
                s.n += k
                incs.append((s, k))
        self.q[eng].append(("op", fn, incs))
        if fence:
            self.fence(eng)

    def fence(self, eng, inc=None):
        incs = []
        if inc is not None:
            if not isinstance(inc, list):
                inc = [inc]
            for s in inc:
                s.n += 1
                incs.append((s, 1))
        jb = self.junk[eng]
        if eng == "sc":
            self.q[eng].append(("op", lambda e: e.copy(jb[:, 0:1], jb[:, 2:3]), []))
            self.q[eng].append(("op", lambda e: e.copy(jb[:, 1:2], jb[:, 3:4]), incs))
        else:
            self.q[eng].append(("op", lambda e: e.tensor_copy(jb[:, 0:1], jb[:, 2:3]), []))
            self.q[eng].append(("op", lambda e: e.tensor_copy(jb[:, 1:2], jb[:, 3:4]), incs))

    def w(self, eng, sem, target=None):
        t = sem.n if target is None else target
        self.q[eng].append(("wait", sem, t))

    def replay(self, eng_name, eng):
        for item in self.q[eng_name]:
            if item[0] == "op":
                ins = item[1](eng)
                for (s, k) in item[2]:
                    ins.then_inc(s.h, k)
                if DEV_DRAIN and eng_name in ("ve", "sc", "te"):
                    eng.drain()
            else:
                eng.wait_ge(item[1].h, item[2])


class Ring:

    def __init__(self, n):
        self.n = n
        self.slots = [[] for _ in range(n)]

    def acquire(self, S, eng, idx):
        for (sem, t) in self.slots[idx % self.n]:
            S.w(eng, sem, t)
        self.slots[idx % self.n] = []

    def release(self, idx, sem, target=None):
        self.slots[idx % self.n].append((sem, sem.n if target is None else target))


def build_nc():
    nc = bass.Bass("TRN2", target_bir_lowering=False,
                   detect_race_conditions=DEV_DRAIN)

    ap_x = nc.dram_tensor("x", [BC, L, DM], F32, kind="ExternalInput").ap()
    ap_adj = nc.dram_tensor("adj_pm", [NM, 128, NM, 128], F32, kind="ExternalInput").ap()
    ap_w = nc.dram_tensor("w_pm", [2 * NM, 128, NKB, 128], F32, kind="ExternalInput").ap()
    ap_op = nc.dram_tensor("op_w", [DI, DM], F32, kind="ExternalInput").ap()
    ap_xp = nc.dram_tensor("xp_pm", [128, NM * 96], F32, kind="ExternalInput").ap()
    ap_dtw = nc.dram_tensor("dt_w", [DR, DI], F32, kind="ExternalInput").ap()
    ap_apm = nc.dram_tensor("a_pm", [128, NM * NST], F32, kind="ExternalInput").ap()
    ap_cw = nc.dram_tensor("convw_pm", [128, NM * 4], F32, kind="ExternalInput").ap()
    ap_cb = nc.dram_tensor("convb_pm", [128, NM], F32, kind="ExternalInput").ap()
    ap_dtb = nc.dram_tensor("dtb_pm", [128, NM], F32, kind="ExternalInput").ap()
    ap_dd = nc.dram_tensor("d_pm", [128, NM], F32, kind="ExternalInput").ap()
    ap_fw = nc.dram_tensor("ffn_w", [DM, DM], F32, kind="ExternalInput").ap()
    ap_h = nc.dram_tensor("h_bf", [DM, DM], BF16, kind="ExternalInput").ap()
    ap_eye = nc.dram_tensor("eye_bf", [128, 128], BF16, kind="ExternalInput").ap()

    ap_out = nc.dram_tensor("out", [BC, L, DM], F32, kind="ExternalOutput").ap()
    dbg_aps = {}
    if DBG:
        for nm, shape in [("d_xn", [128, NTI * DM]), ("d_xnT", [128, NKB * TC]),
                          ("d_xi", [128, NM * TC]), ("d_dT", [128, NM * TC]),
                          ("d_yf", [128, NM * TC]), ("d_xq", [128, NTI * DM]),
                          ("d_hc", [128, NM * NST]), ("d_brep", [128, 8 * TC]),
                          ("d_crep", [128, 8 * TC]), ("d_asb", [128, NM * NST]),
                          ("d_su", [128, TC]), ("d_al", [128, 8 * TC]),
                          ("d_hh", [128, 8 * TC]), ("d_wv", [128, 8 * TC]),
                          ("d_stat", [128, 16]), ("d_ln", [128, 16])]:
            dbg_aps[nm] = nc.dram_tensor(nm, shape, F32, kind="ExternalOutput").ap()
    d_adjbf = nc.dram_tensor("adj_bf", [NM, 128, NM, 128], BF16, kind="Internal").ap()
    d_wbf = nc.dram_tensor("w_bf", [2 * NM, 128, NKB, 128], BF16, kind="Internal").ap()
    d_opbf = nc.dram_tensor("op_bf", [DI, DM], BF16, kind="Internal").ap()
    d_ternbf = nc.dram_tensor("tern_bf", [DM, DM], BF16, kind="Internal").ap()
    d_dtw = nc.dram_tensor("dtw_bf", [DR, DI], BF16, kind="Internal").ap()
    d_bc = nc.dram_tensor("bc_scr", [32, TC], BF16, kind="Internal").ap()

    with contextlib.ExitStack() as stack:
        def sb(name, shape, dtype):
            return stack.enter_context(nc.sbuf_tensor(name, shape, dtype))

        def ps_(name, shape, dtype):
            return stack.enter_context(nc.psum_tensor(name, shape, dtype))

        eye_sb = sb("eye_sb", [128, 128], BF16)
        xpw_sb = sb("xpw_sb", [128, NM * 96], BF16)
        dtbuf = [sb(f"dtbuf{i}", [DR, 128], BF16) for i in range(2)]
        a_sb = sb("a_sb", [128, NM * NST], F32)
        cw_sb = sb("cw_sb", [128, NM * 4], F32)
        cb_sb = sb("cb_sb", [128, NM], F32)
        dtb_sb = sb("dtb_sb", [128, NM], F32)
        dd_sb = sb("dd_sb", [128, NM], F32)
        hcarry = sb("hcarry", [128, NM * NST], F32)
        halo_sb = sb("halo_sb", [128, NM * 3], F32)
        stat = sb("stat", [128, 16], F32)
        cst = sb("cst", [128, 2], F32)
        junk_v = sb("junk_v", [128, 4], F32)
        junk_s = sb("junk_s", [128, 4], F32)

        ln_snap = sb("ln_snap", [128, 16], F32)
        x_ld = [sb(f"x_ld{i}", [128, DM], F32) for i in range(2)]
        xn_sb = sb("xn_sb", [128, NTI * DM], BF16)
        xnT_sb = sb("xnT_sb", [128, NKB * TC], BF16)
        xq_sb = sb("xq_sb", [128, NTI * DM], BF16)
        xqT_sb = sb("xqT_sb", [128, NKB * TC], BF16)
        hbuf = sb("hbuf", [128, NKB * DM], BF16)
        wibuf = [sb(f"wibuf{i}", [128, NKB * 128], BF16) for i in range(2)]
        rwbuf = [sb(f"rwbuf{i}", [128, NKB * 128], BF16) for i in range(2)]
        adjbuf = [sb(f"adjbuf{i}", [128, DI], BF16) for i in range(2)]
        opbuf = [sb(f"opbuf{i}", [128, 1024], BF16) for i in range(2)]
        ternbuf = [sb(f"ternbuf{i}", [128, 1024], BF16) for i in range(2)]
        xip_sb = [sb(f"xip{i}", [128, TC + 3], BF16) for i in range(2)]
        cacc = [sb(f"cacc{i}", [128, TC], F32) for i in range(2)]
        xi_sb = sb("xi_sb", [128, NM * TC], BF16)
        dinT_sb = sb("dinT_sb", [DR, TC], BF16)
        deltaT_sb = sb("deltaT_sb", [128, NM * TC], BF16)
        su_sb = [sb(f"su{i}", [128, TC], BF16) for i in range(2)]
        brep = [sb(f"brep{g}", [128, 8 * TC], BF16) for g in range(2)]
        crep = [sb(f"crep{g}", [128, 8 * TC], BF16) for g in range(2)]
        al_sb = sb("al_sb", [128, 8 * TC], BF16)
        wv_sb = sb("wv_sb", [128, 8 * TC], BF16)
        hh_sb = sb("hh_sb", [128, 8 * TC], BF16)
        yg_sb = sb("yg_sb", [128, 2 * TC], BF16)
        t2_sb = sb("t2_sb", [128, TC], BF16)
        yf_sb = sb("yf_sb", [128, NM * TC], BF16)
        out_sb = [sb(f"out_sb{i}", [128, 512], F32) for i in range(2)]
        bc_sb = sb("bc_sb", [96, TC], BF16)

        ps_tr = [ps_(f"ps_tr{i}", [128, 1024], BF16) for i in range(2)]
        ps_a = [ps_(f"ps_a{i}", [128, 512], F32) for i in range(2)]
        ps_ip = [ps_(f"ps_ip{i}", [128, 512], F32) for i in range(2)]
        ps_dp = [ps_(f"ps_dp{i}", [128, 512], F32) for i in range(2)]

        sems = {}

        def S_(name):
            sems[name] = Sem(stack, nc, name)
            return sems[name]

        L_xld = [S_(f"L_xld{i}") for i in range(2)]
        L_wib = [S_(f"L_wib{i}") for i in range(2)]
        L_rwb = [S_(f"L_rwb{i}") for i in range(2)]
        L_adjb = [S_(f"L_adjb{i}") for i in range(2)]
        L_opb = [S_(f"L_opb{i}") for i in range(2)]
        L_ternb = [S_(f"L_ternb{i}") for i in range(2)]
        L_dtb = [S_(f"L_dtb{i}") for i in range(2)]
        P_wib = [S_(f"P_wib{i}") for i in range(2)]
        P_adjb = [S_(f"P_adjb{i}") for i in range(2)]
        L_hb = S_("L_hb"); L_small = S_("L_small"); L_h = S_("L_h")
        L_bcd = S_("L_bcd"); L_rep = S_("L_rep")
        S_os = [S_(f"S_os{i}") for i in range(2)]
        T_trn = S_("T_trn"); T_trq = S_("T_trq"); T_xh = S_("T_xh")
        T_ip = S_("T_ip"); T_res = S_("T_res"); T_xdbl = S_("T_xdbl")
        T_dt = S_("T_dt"); T_dp = S_("T_dp"); T_opk = S_("T_opk")
        T_tek = S_("T_tek"); T_out = S_("T_out"); T_ptr = S_("T_ptr")
        V_xn = S_("V_xn"); V_q = S_("V_q"); V_conv = S_("V_conv"); V_bc = S_("V_bc")
        V_su = S_("V_su"); V_w = S_("V_w"); V_scan = S_("V_scan"); V_y = S_("V_y")
        V_oev = S_("V_oev"); V_cast = S_("V_cast"); V_pt = S_("V_pt")
        A_rstd = S_("A_rstd"); A_evxn = S_("A_evxn"); A_evxq = S_("A_evxq")
        A_evxi = S_("A_evxi"); A_silu = S_("A_silu"); A_res = S_("A_res")
        A_dinT = S_("A_dinT"); A_sp = S_("A_sp"); A_al = S_("A_al"); A_pev = S_("A_pev")

        S = Sched()
        S.junk = {"ve": junk_v, "sc": junk_s}

        r_xld = Ring(2); r_wib = Ring(2); r_rwb = Ring(2)
        r_adjb = Ring(2); r_opb = Ring(2); r_ternb = Ring(2)
        r_xip = Ring(2); r_cacc = Ring(2); r_su = Ring(2); r_out = Ring(2)
        r_pstr = Ring(2)
        r_psa = Ring(2); r_psip = Ring(2); r_psdp = Ring(2)
        r_xn = Ring(1); r_xnT = Ring(1); r_xq = Ring(1); r_xqT = Ring(1)
        r_xi = Ring(1); r_dT = Ring(1); r_dinT = Ring(1); r_yf = Ring(1)
        r_al = Ring(2); r_rep = Ring(1); r_bcs = Ring(1)
        r_ternsc = Ring(1); r_hbsc = Ring(1)

        S.o("sy", lambda e: e.dma_start(out=eye_sb[:], in_=ap_eye), inc=L_small, dma=True)

        for m in range(NM):
            for piece in range(2):
                idx = m * 2 + piece
                r_xld.acquire(S, "sy", idx)
                src = ap_adj[m].rearrange("p a b -> p (a b)")[:, piece * 1024:(piece + 1) * 1024]
                S.o("sy", lambda e, s=src, d=x_ld[idx % 2]: e.dma_start(out=d[:], in_=s),
                    inc=L_xld[idx % 2], dma=True)
                r_adjb.acquire(S, "ve", m) if piece == 0 else None
                S.w("ve", L_xld[idx % 2])
                S.o("ve", lambda e, sl=x_ld[idx % 2], d=adjbuf[m % 2], p=piece:
                    e.tensor_copy(d[:, p * 1024:(p + 1) * 1024], sl[:]), inc=V_cast)
                r_xld.release(idx, V_cast)
            dst = d_adjbf[m].rearrange("p a b -> p (a b)")
            S.w("sc", V_cast)
            S.o("sc", lambda e, s=adjbuf[m % 2], d=dst: e.dma_start(out=d, in_=s[:]),
                inc=P_adjb[m % 2], dma=True)
            r_adjb.release(m, P_adjb[m % 2])
        for m in range(2 * NM):
            idx = 32 + m
            r_xld.acquire(S, "sy", idx)
            src = ap_w[m].rearrange("p a b -> p (a b)")
            S.o("sy", lambda e, s=src, d=x_ld[idx % 2]: e.dma_start(out=d[:], in_=s),
                inc=L_xld[idx % 2], dma=True)
            r_wib.acquire(S, "ve", m)
            S.w("ve", L_xld[idx % 2])
            S.o("ve", lambda e, sl=x_ld[idx % 2], d=wibuf[m % 2]:
                e.tensor_copy(d[:], sl[:]), inc=V_cast)
            r_xld.release(idx, V_cast)
            dst = d_wbf[m].rearrange("p a b -> p (a b)")
            S.w("sc", V_cast)
            S.o("sc", lambda e, s=wibuf[m % 2], d=dst: e.dma_start(out=d, in_=s[:]),
                inc=P_wib[m % 2], dma=True)
            r_wib.release(m, P_wib[m % 2])
        for kb in range(NM):
            idx = 64 + kb
            r_xld.acquire(S, "sy", idx)
            src = ap_op[kb * 128:(kb + 1) * 128, :]
            S.o("sy", lambda e, s=src, d=x_ld[idx % 2]: e.dma_start(out=d[:], in_=s),
                inc=L_xld[idx % 2], dma=True)
            r_wib.acquire(S, "ve", 32 + kb)
            S.w("ve", L_xld[idx % 2])
            S.o("ve", lambda e, sl=x_ld[idx % 2], d=wibuf[kb % 2]:
                e.tensor_copy(d[:], sl[:]), inc=V_cast)
            r_xld.release(idx, V_cast)
            dst = d_opbf[kb * 128:(kb + 1) * 128, :]
            S.w("sc", V_cast)
            S.o("sc", lambda e, s=wibuf[kb % 2], d=dst: e.dma_start(out=d, in_=s[:]),
                inc=P_wib[kb % 2], dma=True)
            r_wib.release(32 + kb, P_wib[kb % 2])
        for piece in range(2):
            idx = 80 + piece
            r_xld.acquire(S, "sy", idx)
            S.o("sy", lambda e, p=piece, d=x_ld[idx % 2]:
                e.dma_start(out=d[:, 0:768], in_=ap_xp[:, p * 768:(p + 1) * 768]),
                inc=L_xld[idx % 2], dma=True)
            S.w("ve", L_xld[idx % 2])
            S.o("ve", lambda e, p=piece, sl=x_ld[idx % 2]:
                e.tensor_copy(xpw_sb[:, p * 768:(p + 1) * 768], sl[:, 0:768]), inc=V_cast)
            r_xld.release(idx, V_cast)
        for piece in range(2):
            idx = 82 + piece
            r_xld.acquire(S, "sy", idx)
            S.o("sy", lambda e, p=piece, d=x_ld[idx % 2]:
                e.dma_start(out=d[0:DR, :], in_=ap_dtw[:, p * 1024:(p + 1) * 1024]),
                inc=L_xld[idx % 2], dma=True)
            r_wib.acquire(S, "ve", 48 + piece)
            S.w("ve", L_xld[idx % 2])
            S.o("ve", lambda e, p=piece, sl=x_ld[idx % 2]:
                e.tensor_copy(wibuf[p % 2][0:DR, 0:1024], sl[0:DR, :]), inc=V_cast)
            r_xld.release(idx, V_cast)
            S.w("sc", V_cast)
            S.o("sc", lambda e, p=piece: e.dma_start(
                out=d_dtw[:, p * 1024:(p + 1) * 1024], in_=wibuf[p % 2][0:DR, 0:1024]),
                inc=P_wib[piece % 2], dma=True)
            r_wib.release(48 + piece, P_wib[piece % 2])
        S.o("sy", lambda e: e.dma_start(out=a_sb[:], in_=ap_apm), inc=L_small, dma=True)
        S.o("sy", lambda e: e.dma_start(out=cw_sb[:], in_=ap_cw), inc=L_small, dma=True)
        S.o("sy", lambda e: e.dma_start(out=cb_sb[:], in_=ap_cb), inc=L_small, dma=True)
        S.o("sy", lambda e: e.dma_start(out=dtb_sb[:], in_=ap_dtb), inc=L_small, dma=True)
        S.o("sy", lambda e: e.dma_start(out=dd_sb[:], in_=ap_dd), inc=L_small, dma=True)
        S.w("sc", L_small, 96)
        S.o("sc", lambda e: e.activation(a_sb[:], a_sb[:], AF.Exp), inc=A_al)
        S.w("ve", A_al)
        S.o("ve", lambda e: e.tensor_scalar_mul(a_sb[:], a_sb[:], -1.0), inc=V_cast)
        S.o("ve", lambda e: e.memset(hcarry[:], 0.0), inc=V_cast)
        S.o("ve", lambda e: e.memset(halo_sb[:], 0.0), inc=V_cast)
        S.o("ve", lambda e: e.memset(cst[:, 0:1], EPS), inc=V_cast)
        S.o("ve", lambda e: e.memset(cst[:, 1:2], -0.5), inc=V_cast)
        S.o("ve", lambda e: e.memset(junk_v[:], 0.0))
        S.o("sc", lambda e: e.memzero(junk_s[:]))

        for ob in range(NKB):
            idx = 84 + ob
            r_xld.acquire(S, "sy", idx)
            S.o("sy", lambda e, o=ob, d=x_ld[idx % 2]:
                e.dma_start(out=d[:], in_=ap_fw[o * 128:(o + 1) * 128, :]),
                inc=L_xld[idx % 2], dma=True)
            S.w("ve", L_xld[idx % 2])
            xl = x_ld[idx % 2]
            for hh in range(2):
                r_cacc.acquire(S, "ve", hh)
                S.o("ve", lambda e, sl=xl, h=hh: e.tensor_scalar(
                    cacc[h][:], sl[:, h * 512:(h + 1) * 512], -1.0, None, op0=ALU.mult),
                    fence=True)
                S.o("ve", lambda e, sl=xl, h=hh: e.tensor_max(
                    cacc[h][:], cacc[h][:], sl[:, h * 512:(h + 1) * 512]), fence=True)
                S.o("ve", lambda e, h=hh: e.tensor_reduce(
                    stat[:, 3 + h:4 + h], cacc[h][:], axis=mybir.AxisListType.X,
                    op=ALU.add), fence=True)
            S.o("ve", lambda e: e.tensor_add(stat[:, 0:1], stat[:, 3:4], stat[:, 4:5]), fence=True)
            S.o("ve", lambda e: e.tensor_scalar_mul(stat[:, 0:1], stat[:, 0:1], 1.0 / DM), fence=True)
            S.o("ve", lambda e: e.tensor_scalar_max(stat[:, 0:1], stat[:, 0:1], 1e-5), fence=True)
            S.o("ve", lambda e: e.tensor_scalar_mul(stat[:, 14:15], stat[:, 0:1], 0.5), fence=True)
            S.o("ve", lambda e: e.tensor_scalar_mul(stat[:, 15:16], stat[:, 14:15], -1.0), fence=True)
            r_ternsc.acquire(S, "ve", 0)
            for hh in range(2):
                cs = slice(hh * 512, (hh + 1) * 512)
                r_cacc.acquire(S, "ve", hh)
                S.o("ve", lambda e, sl=xl, c=cs: e.tensor_scalar(
                    cacc[0][:], sl[:, c], stat[:, 14:15], None, op0=ALU.is_gt))
                S.o("ve", lambda e, sl=xl, c=cs: e.tensor_scalar(
                    cacc[1][:], sl[:, c], stat[:, 15:16], None, op0=ALU.is_lt), fence=True)
                S.o("ve", lambda e: e.tensor_sub(cacc[0][:], cacc[0][:], cacc[1][:]), fence=True)
                S.o("ve", lambda e, o=ob, c=cs: e.tensor_scalar(
                    xq_sb[:, c], cacc[0][:], stat[:, 0:1], None, op0=ALU.mult))
                S.fence("ve", inc=V_pt)
                r_cacc.release(hh, V_pt)
            r_xld.release(idx, V_pt)
            r_pstr.acquire(S, "te", 0); r_pstr.acquire(S, "te", 1)
            S.w("te", V_pt)
            S.w("te", L_small, 96)
            for ib in range(NKB):
                S.o("te", lambda e, i=ib: e.transpose(
                    ps_tr[i // 4][:, (i % 4) * 128:(i % 4 + 1) * 128],
                    xq_sb[:, i * 128:(i + 1) * 128],
                    eye_sb[:]), inc=(T_ptr if ib == NKB - 1 else None))
            r_ternsc.release(0, T_ptr)
            r_hbsc.acquire(S, "sc", 0)
            S.w("sc", T_ptr)
            S.o("sc", lambda e: e.copy(hbuf[:, 0:512], ps_tr[0][:, 0:512]))
            S.o("sc", lambda e: e.copy(hbuf[:, 512:1024], ps_tr[1][:, 0:512]), inc=A_pev)
            r_pstr.release(0, A_pev); r_pstr.release(1, A_pev)
            dst = d_ternbf.rearrange("(ib p) o -> p ib o", ib=NKB)[:, :, ob * 128:(ob + 1) * 128]
            S.w("gp", A_pev)
            S.o("gp", lambda e, d=dst: e.dma_start(out=d, in_=hbuf[:, 0:1024].rearrange(
                "p (ib c) -> p ib c", ib=NKB)), inc=L_hb, dma=True)
            r_hbsc.release(0, L_hb)
        S.w("sy", L_hb)
        S.o("sy", lambda e: e.dma_start(
            out=hbuf[:].rearrange("p (kb c) -> p kb c", kb=NKB),
            in_=ap_h.rearrange("(kb p) c -> p kb c", p=128)),
            inc=L_h, dma=True)
        r_xq.release(0, T_ptr)

        PRO_WR = [(P_wib[0], P_wib[0].n), (P_wib[1], P_wib[1].n),
                  (P_adjb[0], P_adjb[0].n), (P_adjb[1], P_adjb[1].n),
                  (L_hb, L_hb.n)]

        for ci in range(NCH):
            b = ci // (L // TC)
            cs = ci % (L // TC)
            t0 = cs * TC

            if cs == 0:
                S.o("ve", lambda e: e.memset(hcarry[:], 0.0))
                S.o("ve", lambda e: e.memset(halo_sb[:], 0.0))
            def emit_xload(ti):
                gi = ci * NTI + ti
                r_xld.acquire(S, "sc", 100 + gi)
                src = ap_x[b, t0 + ti * 128: t0 + (ti + 1) * 128, :]
                S.o("sc", lambda e, s=src, d=x_ld[gi % 2]: e.dma_start(out=d[:], in_=s),
                    inc=L_xld[gi % 2], dma=True)

            emit_xload(0)
            emit_xload(1)
            vxn_tgt_a = []
            for ti in range(NTI):
                gi = ci * NTI + ti
                xl = x_ld[gi % 2]
                S.w("ve", L_xld[gi % 2])
                S.o("ve", lambda e, sl=xl: e.reduce_sum(stat[:, 0:1], sl[:],
                                                        axis=mybir.AxisListType.X),
                    fence=True)
                r_cacc.acquire(S, "sc", 2 * gi)
                r_cacc.acquire(S, "sc", 2 * gi + 1)
                S.w("sc", L_xld[gi % 2])
                S.w("sc", V_xn)
                S.o("sc", lambda e, sl=xl: e.activation(
                    cacc[0][:], sl[:, 0:512], AF.Square, accum_out=stat[:, 1:2]))
                S.o("sc", lambda e, sl=xl: e.activation(
                    cacc[1][:], sl[:, 512:1024], AF.Square, accum_out=stat[:, 2:3]))
                S.fence("sc", inc=A_rstd)
                r_cacc.release(2 * gi, A_rstd)
                r_cacc.release(2 * gi + 1, A_rstd)
                S.w("ve", A_rstd)
                S.o("ve", lambda e: e.tensor_add(stat[:, 3:4], stat[:, 1:2], stat[:, 2:3]), fence=True)
                S.o("ve", lambda e: e.tensor_scalar_mul(stat[:, 4:5], stat[:, 0:1], 1.0 / DM), fence=True)
                S.o("ve", lambda e: e.tensor_scalar_mul(stat[:, 5:6], stat[:, 3:4], 1.0 / DM), fence=True)
                S.o("ve", lambda e: e.tensor_mul(stat[:, 6:7], stat[:, 4:5], stat[:, 4:5]), fence=True)
                S.o("ve", lambda e: e.tensor_sub(stat[:, 5:6], stat[:, 5:6], stat[:, 6:7]))
                S.fence("ve", inc=V_xn)
                S.w("sc", V_xn)
                S.o("sc", lambda e: e.activation(stat[:, 7:8], stat[:, 5:6], AF.Ln,
                                                 bias=cst[:, 0:1]), fence=True)
                S.o("sc", lambda e: e.activation(stat[:, 7:8], stat[:, 7:8], AF.Exp,
                                                 scale=cst[:, 1:2]))
                S.fence("sc", inc=A_rstd)
                r_xn.acquire(S, "ve", 0) if ti == 0 else None
                S.w("ve", A_rstd)
                if DBG and ci == 0 and ti == 0:
                    S.o("ve", lambda e: e.tensor_copy(ln_snap[:], stat[:]))
                S.o("ve", lambda e, sl=xl, t=ti: e.tensor_scalar(
                    xn_sb[:, t * DM:(t + 1) * DM], sl[:], stat[:, 4:5], stat[:, 7:8],
                    op0=ALU.subtract, op1=ALU.mult))
                S.fence("ve", inc=V_xn)
                vxn_tgt_a.append(V_xn.n)
                r_xld.release(100 + gi, V_xn)
                if ti + 2 < NTI:
                    emit_xload(ti + 2)

            if PH < 2:
                continue
            r_xnT.acquire(S, "te", 0)
            for kb in range(NKB):
                r_pstr.acquire(S, "te", kb)
                for ti in range(NTI):
                    if kb == 0:
                        S.w("te", V_xn, vxn_tgt_a[ti])
                    S.o("te", lambda e, t=ti, k=kb: e.transpose(
                        ps_tr[k % 2][:, t * 128:(t + 1) * 128],
                        xn_sb[:, t * DM + k * 128: t * DM + (k + 1) * 128], eye_sb[:]),
                        inc=(T_trn if ti == NTI - 1 else None))
                S.w("sc", T_trn)
                S.o("sc", lambda e, k=kb: e.copy(
                    xnT_sb[:, k * TC:(k + 1) * TC],
                    ps_tr[k % 2][:, 0:512]), inc=A_evxn)
                r_pstr.release(kb, A_evxn)
            r_xn.release(0, T_trn)

            if PH < 3:
                continue
            r_xq.acquire(S, "ve", 0)
            for ti in range(NTI):
                for half in range(2):
                    r_psa.acquire(S, "te", half)
                    if ti == 0 and half == 0:
                        S.w("te", L_h, 16)
                    for kb in range(NKB):
                        if ti == 0 and half == 0:
                            S.w("te", A_evxn, A_evxn.n - NKB + kb + 1)
                        S.o("te", lambda e, t=ti, h=half, k=kb: e.matmul(
                            ps_a[h][:],
                            xnT_sb[:, k * TC + t * 128: k * TC + (t + 1) * 128],
                            hbuf[:, k * DM + h * 512: k * DM + (h + 1) * 512],
                            start=(k == 0), stop=(k == NKB - 1)),
                            inc=(T_xh if kb == NKB - 1 else None))
                S.w("ve", T_xh)
                for half in range(2):
                    r_cacc.acquire(S, "ve", 40 + 8 * ci + 2 * ti + half)
                    S.o("ve", lambda e, h=half: e.tensor_copy(cacc[h][:], ps_a[h][:]),
                        fence=(half == 1))
                    r_psa.release(half, V_q, V_q.n + 1)
                S.o("ve", lambda e: e.tensor_reduce(
                    stat[:, 8:9], cacc[0][:], axis=mybir.AxisListType.X, op=ALU.max))
                S.o("ve", lambda e: e.tensor_reduce(
                    stat[:, 9:10], cacc[1][:], axis=mybir.AxisListType.X, op=ALU.max))
                S.o("ve", lambda e: e.tensor_reduce(
                    stat[:, 11:12], cacc[0][:], axis=mybir.AxisListType.X, op=ALU.min))
                S.o("ve", lambda e: e.tensor_reduce(
                    stat[:, 12:13], cacc[1][:], axis=mybir.AxisListType.X, op=ALU.min),
                    fence=True)
                S.o("ve", lambda e: e.tensor_max(stat[:, 8:9], stat[:, 8:9], stat[:, 9:10]), fence=True)
                S.o("ve", lambda e: e.tensor_tensor(stat[:, 11:12], stat[:, 11:12], stat[:, 12:13], op=ALU.min), fence=True)
                S.o("ve", lambda e: e.tensor_scalar(stat[:, 11:12], stat[:, 11:12], -1.0, None, op0=ALU.mult), fence=True)
                S.o("ve", lambda e: e.tensor_max(stat[:, 8:9], stat[:, 8:9], stat[:, 11:12]), fence=True)
                S.o("ve", lambda e: e.tensor_scalar_max(stat[:, 8:9], stat[:, 8:9], 1e-5), fence=True)
                S.o("ve", lambda e: e.tensor_scalar_mul(stat[:, 8:9], stat[:, 8:9], 1.0 / 7.0))
                S.fence("ve", inc=V_q)
                S.w("sc", V_q)
                S.o("sc", lambda e: e.activation(stat[:, 10:11], stat[:, 8:9], AF.Ln),
                    fence=True)
                S.o("sc", lambda e: e.activation(stat[:, 9:10], stat[:, 10:11], AF.Exp,
                                                 scale=-1.0))
                S.fence("sc", inc=A_rstd)
                S.w("ve", A_rstd)
                for half in range(2):
                    S.o("ve", lambda e, h=half: e.tensor_scalar(
                        cacc[h][:], cacc[h][:], stat[:, 9:10], RNE_C,
                        op0=ALU.mult, op1=ALU.add), inc=V_q, fence=True)
                    S.o("ve", lambda e, h=half, t=ti: e.tensor_scalar(
                        xq_sb[:, t * DM + h * 512: t * DM + (h + 1) * 512],
                        cacc[h][:], RNE_C, stat[:, 8:9],
                        op0=ALU.subtract, op1=ALU.mult))
                    S.fence("ve", inc=V_q)
                    r_cacc.release(40 + 8 * ci + 2 * ti + half, V_q)

            if PH < 4:
                continue
            r_xi.acquire(S, "sc", 0)
            r_yf.acquire(S, "sc", 0)
            def emit_ip(m):
                r_wib.acquire(S, "sy", m)
                if ci == 0 and m == 0:
                    for (ps_sem, ps_t) in PRO_WR:
                        S.w("sy", ps_sem, ps_t)
                src = d_wbf[m].rearrange("p a b -> p (a b)")
                S.o("sy", lambda e, s=src, d=wibuf[m % 2]: e.dma_start(out=d[:], in_=s),
                    inc=L_wib[m % 2], dma=True)
                r_psip.acquire(S, "te", m)
                S.w("te", L_wib[m % 2])
                if m == 0:
                    S.w("te", A_evxn)
                for kb in range(NKB):
                    S.o("te", lambda e, k=kb, mm=m: e.matmul(
                        ps_ip[mm % 2][:], wibuf[mm % 2][:, k * 128:(k + 1) * 128],
                        xnT_sb[:, k * TC:(k + 1) * TC],
                        start=(k == 0), stop=(k == NKB - 1)),
                        inc=(T_ip if kb == NKB - 1 else None))
                r_wib.release(m, T_ip)

            def emit_res(m):
                r_rwb.acquire(S, "sy", m)
                src2 = d_wbf[NM + m].rearrange("p a b -> p (a b)")
                S.o("sy", lambda e, s=src2, d=rwbuf[m % 2]: e.dma_start(out=d[:], in_=s),
                    inc=L_rwb[m % 2], dma=True)
                r_psdp.acquire(S, "te", m)
                S.w("te", L_rwb[m % 2])
                for kb in range(NKB):
                    S.o("te", lambda e, k=kb, mm=m: e.matmul(
                        ps_dp[mm % 2][:], rwbuf[mm % 2][:, k * 128:(k + 1) * 128],
                        xnT_sb[:, k * TC:(k + 1) * TC],
                        start=(k == 0), stop=(k == NKB - 1)),
                        inc=(T_res if kb == NKB - 1 else None))
                r_rwb.release(m, T_res)

            def emit_evacxi(m):
                r_xip.acquire(S, "sc", m)
                S.w("sc", T_ip)
                S.o("sc", lambda e, mm=m: e.copy(xip_sb[mm % 2][:, 3:3 + TC],
                                                 ps_ip[mm % 2][:]))
                S.fence("sc", inc=A_evxi)
                r_psip.release(m, A_evxi)

            def emit_conv(m):
                r_cacc.acquire(S, "ve", m)
                S.w("ve", A_evxi, aevxi0 + m + 1)
                S.o("ve", lambda e, mm=m: e.tensor_copy(
                    xip_sb[mm % 2][:, 0:3],
                    halo_sb[:, mm * 3:(mm + 1) * 3]))
                S.o("ve", lambda e, mm=m: e.tensor_scalar(
                    cacc[mm % 2][:], xip_sb[mm % 2][:, 0:TC], cw_sb[:, mm * 4:mm * 4 + 1],
                    None, op0=ALU.mult))
                for k in range(1, 4):
                    S.o("ve", lambda e, mm=m, kk=k: e.scalar_tensor_tensor(
                        cacc[mm % 2][:], xip_sb[mm % 2][:, kk:kk + TC],
                        cw_sb[:, mm * 4 + kk:mm * 4 + kk + 1], cacc[mm % 2][:],
                        op0=ALU.mult, op1=ALU.add))
                S.o("ve", lambda e, mm=m: e.tensor_copy(
                    halo_sb[:, mm * 3:(mm + 1) * 3],
                    xip_sb[mm % 2][:, TC:TC + 3]))
                S.fence("ve", inc=V_conv)
                r_xip.release(m, V_conv)

            def emit_silu(m):
                S.w("sc", V_conv)
                S.o("sc", lambda e, mm=m: e.activation(
                    xi_sb[:, mm * TC:(mm + 1) * TC], cacc[mm % 2][:], AF.Silu,
                    bias=cb_sb[:, mm:mm + 1]), inc=A_silu)
                r_cacc.release(m, A_silu)

            def emit_silures(m):
                S.w("sc", T_res, tres0 + m + 1)
                S.o("sc", lambda e, mm=m: e.activation(
                    yf_sb[:, mm * TC:(mm + 1) * TC], ps_dp[mm % 2][:], AF.Silu),
                    inc=A_res)
                r_psdp.release(m, A_res)

            aevxi0 = A_evxi.n
            tres0 = T_res.n
            asilu0 = A_silu.n

            def emit_xdbl(m):
                if m == 0:
                    r_psa.acquire(S, "te", 0)
                S.w("te", A_silu, asilu0 + m + 1)
                S.o("te", lambda e, mm=m: e.matmul(
                    ps_a[0][0:96, :], xpw_sb[:, mm * 96:(mm + 1) * 96],
                    xi_sb[:, mm * TC:(mm + 1) * TC],
                    start=(mm == 0), stop=(mm == NM - 1)),
                    inc=(T_xdbl if m == NM - 1 else None))

            for m in range(NM):
                emit_ip(m)
                if m >= 1:
                    emit_res(m - 1)
                emit_evacxi(m)
                if m >= 1:
                    emit_conv(m - 1)
                    emit_silu(m - 1)
                if m >= 2:
                    emit_silures(m - 2)
                if m >= 3:
                    emit_xdbl(m - 3)
            emit_res(NM - 1)
            emit_conv(NM - 1)
            emit_silu(NM - 1)
            for m in range(NM - 3, NM):
                emit_xdbl(m)
            r_dinT.acquire(S, "sc", 0)
            S.w("sc", T_xdbl)
            S.o("sc", lambda e: e.copy(dinT_sb[:], ps_a[0][0:DR, :]))
            S.fence("sc", inc=A_dinT)
            emit_silures(NM - 2)
            emit_silures(NM - 1)

            if PH < 5:
                continue
            r_xqT.acquire(S, "te", 0)
            for kb in range(NKB):
                r_pstr.acquire(S, "te", kb)
                if kb == 0:
                    S.w("te", V_q)
                for ti in range(NTI):
                    S.o("te", lambda e, t=ti, k=kb: e.transpose(
                        ps_tr[k % 2][:, t * 128:(t + 1) * 128],
                        xq_sb[:, t * DM + k * 128: t * DM + (k + 1) * 128], eye_sb[:]),
                        inc=(T_trq if ti == NTI - 1 else None))
                S.w("sc", T_trq)
                S.o("sc", lambda e, k=kb: e.copy(
                    xqT_sb[:, k * TC:(k + 1) * TC],
                    ps_tr[k % 2][:, 0:512]), inc=A_evxq)
                r_pstr.release(kb, A_evxq)
            r_xq.release(0, T_trq)

            r_bcs.acquire(S, "ve", 0)
            S.w("ve", T_xdbl)
            S.w("ve", A_dinT)
            S.o("ve", lambda e: e.tensor_copy(bc_sb[DR:96, :], ps_a[0][DR:96, :]))
            S.fence("ve", inc=V_bc)
            r_psa.release(0, A_dinT)
            r_psa.slots[0].append((V_bc, V_bc.n))
            S.w("gp", V_bc)
            S.o("gp", lambda e: e.dma_start(out=d_bc, in_=bc_sb[DR:96, :]), inc=L_bcd, dma=True)
            r_bcs.release(0, L_bcd)
            r_rep.acquire(S, "gp", 0)
            S.w("gp", L_bcd)
            for g in range(2):
                for j in range(8):
                    S.o("gp", lambda e, gg=g, jj=j: e.dma_start(
                        out=brep[gg][:, jj * TC:(jj + 1) * TC],
                        in_=d_bc[gg * 8 + jj:gg * 8 + jj + 1, :].broadcast_to([128, TC])),
                        inc=L_rep, dma=True)
                    S.o("gp", lambda e, gg=g, jj=j: e.dma_start(
                        out=crep[gg][:, jj * TC:(jj + 1) * TC],
                        in_=d_bc[16 + gg * 8 + jj:16 + gg * 8 + jj + 1, :].broadcast_to([128, TC])),
                        inc=L_rep, dma=True)
            r_dT.acquire(S, "sc", 0)
            for m in range(NM):
                S.w("sy", T_dt, max(0, T_dt.n - 1))
                S.o("sy", lambda e, mm=m: e.dma_start(
                    out=dtbuf[mm % 2][:], in_=d_dtw[:, mm * 128:(mm + 1) * 128]),
                    inc=L_dtb[m % 2], dma=True)
                r_psip.acquire(S, "te", 17 + m)
                S.w("te", L_dtb[m % 2])
                if m == 0:
                    S.w("te", A_dinT)
                S.o("te", lambda e, mm=m: e.matmul(
                    ps_ip[(17 + mm) % 2][:], dtbuf[mm % 2][:],
                    dinT_sb[:], start=True, stop=True), inc=T_dt)
                r_cacc.acquire(S, "sc", 17 + m)
                S.w("sc", T_dt)
                S.o("sc", lambda e, mm=m: e.activation(
                    cacc[(17 + mm) % 2][:], ps_ip[(17 + mm) % 2][:], AF.Exp,
                    bias=dtb_sb[:, mm:mm + 1]), fence=True)
                r_psip.release(17 + m, A_sp, A_sp.n + 1)
                S.o("sc", lambda e, mm=m: e.activation(
                    deltaT_sb[:, mm * TC:(mm + 1) * TC], cacc[(17 + mm) % 2][:],
                    AF.Ln, bias=1.0), inc=A_sp)
                r_cacc.release(17 + m, A_sp)
            if PH < 6:
                continue
            vy0 = V_y.n

            def emit_F_loads(half, tis=(0, 1, 2, 3), tgt=None):
                gi0 = ci * 8 + half * NTI
                v_base = V_oev.n
                tgt = {} if tgt is None else tgt
                for ti in tis:
                    gi2 = gi0 + ti
                    r_xld.acquire(S, "sc", 500 + gi2)
                    src = ap_x[b, t0 + ti * 128: t0 + (ti + 1) * 128,
                               half * 512:(half + 1) * 512]
                    S.o("sc", lambda e, s=src, d=x_ld[gi2 % 2]:
                        e.dma_start(out=d[:, 0:512], in_=s), inc=L_xld[gi2 % 2], dma=True)
                    tgt[ti] = L_xld[gi2 % 2].n
                    r_xld.release(500 + gi2, V_oev, v_base + ti + 1)
                return tgt

            slot6 = [(0, 0, ps_a[0]), (0, 1, ps_a[1]), (0, 2, ps_ip[0]),
                     (0, 3, ps_ip[1]), (1, 0, ps_dp[0]), (1, 1, ps_dp[1])]

            def emit_F_p1():
                tout0 = T_out.n
                for ib in range(NKB):
                    r_ternb.acquire(S, "sy", ib)
                    src = d_ternbf[ib * 128:(ib + 1) * 128, :]
                    S.o("sy", lambda e, s=src, d=ternbuf[ib % 2]: e.dma_start(out=d[:], in_=s),
                        inc=L_ternb[ib % 2], dma=True)
                    S.w("te", L_ternb[ib % 2])
                    if ib == 0:
                        S.w("te", A_evxq)
                        r_psa.acquire(S, "te", 0)
                        r_psa.acquire(S, "te", 1)
                        r_psip.acquire(S, "te", 40)
                        r_psip.acquire(S, "te", 41)
                        r_psdp.acquire(S, "te", 48)
                        r_psdp.acquire(S, "te", 49)
                    for si, (h, t, ps) in enumerate(slot6):
                        S.o("te", lambda e, k=ib, hh=h, tt=t, p=ps: e.matmul(
                            p[:],
                            xqT_sb[:, k * TC + tt * 128: k * TC + (tt + 1) * 128],
                            ternbuf[k % 2][:, hh * 512:(hh + 1) * 512],
                            start=(k == 0), stop=False),
                            inc=(T_tek if si == 5 else None))
                    r_ternb.release(ib, T_tek)
                for kb in range(NM):
                    r_opb.acquire(S, "sy", kb)
                    src = d_opbf[kb * 128:(kb + 1) * 128, :]
                    S.o("sy", lambda e, s=src, d=opbuf[kb % 2]: e.dma_start(out=d[:], in_=s),
                        inc=L_opb[kb % 2], dma=True)
                    S.w("te", L_opb[kb % 2])
                    S.w("te", V_y, vy0 + kb + 1)
                    for si, (h, t, ps) in enumerate(slot6):
                        S.o("te", lambda e, k=kb, hh=h, tt=t, p=ps: e.matmul(
                            p[:],
                            yf_sb[:, k * TC + tt * 128: k * TC + (tt + 1) * 128],
                            opbuf[k % 2][:, hh * 512:(hh + 1) * 512],
                            start=False, stop=(k == NM - 1)),
                            inc=((T_opk if si == 5 else None) if kb < NM - 1
                                 else T_out))
                    r_opb.release(kb, T_opk if kb < NM - 1 else T_out)
                return tout0

            def emit_F_p2():
                for kb in range(NM):
                    r_opb.acquire(S, "sy", 16 + kb)
                    src = d_opbf[kb * 128:(kb + 1) * 128, 512:1024]
                    S.o("sy", lambda e, s=src, d=opbuf[kb % 2]: e.dma_start(
                        out=d[:, 0:512], in_=s), inc=L_opb[kb % 2], dma=True)
                    S.w("te", L_opb[kb % 2])
                    if kb == 0:
                        r_psa.acquire(S, "te", 2)
                        r_psa.acquire(S, "te", 3)
                    for si, t in enumerate((2, 3)):
                        S.o("te", lambda e, k=kb, tt=t, p=ps_a[si]: e.matmul(
                            p[:],
                            yf_sb[:, k * TC + tt * 128: k * TC + (tt + 1) * 128],
                            opbuf[k % 2][:, 0:512], start=(k == 0), stop=False),
                            inc=(T_opk if si == 1 else None))
                    r_opb.release(16 + kb, T_opk)
                for ib in range(NKB):
                    r_ternb.acquire(S, "sy", 16 + ib)
                    src = d_ternbf[ib * 128:(ib + 1) * 128, 512:1024]
                    S.o("sy", lambda e, s=src, d=ternbuf[ib % 2]: e.dma_start(
                        out=d[:, 0:512], in_=s), inc=L_ternb[ib % 2], dma=True)
                    S.w("te", L_ternb[ib % 2])
                    for si, t in enumerate((2, 3)):
                        S.o("te", lambda e, k=ib, tt=t, p=ps_a[si]: e.matmul(
                            p[:],
                            xqT_sb[:, k * TC + tt * 128: k * TC + (tt + 1) * 128],
                            ternbuf[k % 2][:, 0:512], start=False, stop=(k == NKB - 1)),
                            inc=((T_tek if si == 1 else None) if ib < NKB - 1
                                 else T_out))
                    r_ternb.release(16 + ib, T_tek if ib < NKB - 1 else T_out)

            def emit_F_evacs(items, tout0, xre_tgt):
                for (oi, h, t, ps, rel) in items:
                    gi2 = ci * 8 + h * NTI + t
                    r_out.acquire(S, "ve", gi2)
                    S.w("ve", T_out, tout0 + oi + 1)
                    S.w("ve", L_xld[gi2 % 2], xre_tgt[t])
                    S.o("ve", lambda e, p=ps, g2=gi2: e.tensor_add(
                        out_sb[g2 % 2][:], p[:], x_ld[g2 % 2][:, 0:512]))
                    S.fence("ve", inc=V_oev)
                    rel()
                    dst = ap_out[b, t0 + t * 128: t0 + (t + 1) * 128,
                                 h * 512:(h + 1) * 512]
                    S.w("gp", V_oev)
                    S.o("gp", lambda e, d=dst, g2=gi2: e.dma_start(out=d, in_=out_sb[g2 % 2][:]),
                        inc=S_os[gi2 % 2], dma=True)
                    r_out.release(gi2, S_os[gi2 % 2])


            tdp_tgt = {}
            vsu_tgt = {}

            def emit_dp(m):
                r_adjb.acquire(S, "sy", 16 + m)
                if ci == 0 and m == 0:
                    for (ps_sem, ps_t) in PRO_WR:
                        S.w("sy", ps_sem, ps_t)
                src = d_adjbf[m].rearrange("p a b -> p (a b)")
                S.o("sy", lambda e, s=src, d=adjbuf[m % 2]: e.dma_start(out=d[:], in_=s),
                    inc=L_adjb[m % 2], dma=True)
                r_psdp.acquire(S, "te", 16 + m)
                S.w("te", L_adjb[m % 2])
                for kb in range(NM):
                    if m == 0:
                        S.w("te", A_sp, A_sp.n - NM + kb + 1)
                    S.o("te", lambda e, k=kb, mm=m: e.matmul(
                        ps_dp[mm % 2][:], adjbuf[mm % 2][:, k * 128:(k + 1) * 128],
                        deltaT_sb[:, k * TC:(k + 1) * TC],
                        start=(k == 0), stop=(k == NM - 1)),
                        inc=(T_dp if kb == NM - 1 else None))
                tdp_tgt[m] = T_dp.n
                r_adjb.release(16 + m, T_dp)

            def emit_evac(m):
                r_cacc.acquire(S, "ve", 70 + 16 * ci + m)
                S.w("ve", T_dp, tdp_tgt[m])
                S.o("ve", lambda e, mm=m: e.tensor_copy(cacc[mm % 2][:], ps_dp[mm % 2][:]))
                S.fence("ve", inc=V_su)
                vsu_tgt[m] = V_su.n
                r_psdp.release(16 + m, V_su)

            def emit_E(m):
                r_su.acquire(S, "ve", m)
                S.o("ve", lambda e, mm=m: e.tensor_mul(
                    su_sb[mm % 2][:], cacc[mm % 2][:], xi_sb[:, mm * TC:(mm + 1) * TC]))
                for g in range(2):
                    albuf = [al_sb, xq_sb][(2 * m + g) % 2]
                    r_al.acquire(S, "sc", 2 * m + g)
                    if ci == 0 and m == 0:
                        S.w("sc", T_trq)
                    if g == 0:
                        S.w("sc", V_su, vsu_tgt[m])
                    for j in range(8):
                        n = g * 8 + j
                        S.o("sc", lambda e, ab=albuf, mm=m, nn=n, jj=j: e.activation(
                            ab[:, jj * TC:(jj + 1) * TC], cacc[mm % 2][:], AF.Exp,
                            scale=a_sb[:, mm * NST + nn:mm * NST + nn + 1]),
                            inc=(A_al if j == 7 else None))
                    if g == 1:
                        r_cacc.release(70 + 16 * ci + m, A_al)
                    if m == 0 and g == 0:
                        S.w("ve", L_rep)
                    su_b = su_sb[m % 2][:].unsqueeze(1).broadcast_to([128, 8, TC])
                    bv = brep[g][:].rearrange("p (j t) -> p j t", j=8)
                    wvv = wv_sb[:].rearrange("p (j t) -> p j t", j=8)
                    S.o("ve", lambda e, sb_=su_b, b_=bv, w_=wvv: e.tensor_tensor(
                        w_, sb_, b_, op=ALU.mult), inc=V_w)
                    if g == 1:
                        r_su.release(m, V_w)
                    S.w("ve", A_al)
                    for j in range(8):
                        n = g * 8 + j
                        S.o("ve", lambda e, ab=albuf, mm=m, nn=n, jj=j: e.tensor_tensor_scan(
                            hh_sb[:, jj * TC:(jj + 1) * TC],
                            ab[:, jj * TC:(jj + 1) * TC],
                            wv_sb[:, jj * TC:(jj + 1) * TC],
                            hcarry[:, mm * NST + nn:mm * NST + nn + 1],
                            op0=ALU.mult, op1=ALU.add),
                            inc=(V_scan if j == 7 else None))
                    r_al.release(2 * m + g, V_scan)
                    S.fence("ve")
                    S.o("ve", lambda e, mm=m, gg=g: e.tensor_copy(
                        hcarry[:, mm * NST + gg * 8: mm * NST + gg * 8 + 8],
                        hh_sb[:, TC - 1::TC]))
                    S.o("ve", lambda e, gg=g: e.tensor_mul(wv_sb[:], hh_sb[:], crep[gg][:]))
                    S.o("ve", lambda e: e.tensor_add(
                        wv_sb[:, 0:4 * TC], wv_sb[:, 0:4 * TC], wv_sb[:, 4 * TC:8 * TC]))
                    S.o("ve", lambda e: e.tensor_add(
                        wv_sb[:, 4 * TC:6 * TC], wv_sb[:, 0:2 * TC], wv_sb[:, 2 * TC:4 * TC]))
                    S.o("ve", lambda e, gg=g: e.tensor_add(
                        yg_sb[:, gg * TC:(gg + 1) * TC], wv_sb[:, 4 * TC:5 * TC],
                        wv_sb[:, 5 * TC:6 * TC]))
                S.fence("ve")
                S.o("ve", lambda e: e.tensor_add(
                    yg_sb[:, 0:TC], yg_sb[:, 0:TC], yg_sb[:, TC:2 * TC]), fence=True)
                S.o("ve", lambda e, mm=m: e.scalar_tensor_tensor(
                    t2_sb[:], xi_sb[:, mm * TC:(mm + 1) * TC], dd_sb[:, mm:mm + 1],
                    yg_sb[:, 0:TC], op0=ALU.mult, op1=ALU.add), fence=True)
                S.w("ve", A_res)
                S.o("ve", lambda e, mm=m: e.tensor_mul(
                    yf_sb[:, mm * TC:(mm + 1) * TC], yf_sb[:, mm * TC:(mm + 1) * TC],
                    t2_sb[:]))
                S.fence("ve", inc=V_y)

            for m in range(NM):
                emit_dp(m)
                emit_evac(m)
                if m >= 1:
                    emit_E(m - 1)
            if PH >= 7:
                xre0 = emit_F_loads(0, tis=(0, 1))
                tout_0 = emit_F_p1()
            emit_E(NM - 1)
            if PH >= 7:
                emit_F_loads(0, tis=(2, 3), tgt=xre0)
                emit_F_evacs(
                    [(0, 0, 0, ps_a[0], lambda: r_psa.release(0, V_oev)),
                     (1, 0, 1, ps_a[1], lambda: r_psa.release(1, V_oev)),
                     (2, 0, 2, ps_ip[0],
                      lambda: r_psip.slots[0].append((V_oev, V_oev.n))),
                     (3, 0, 3, ps_ip[1],
                      lambda: r_psip.slots[1].append((V_oev, V_oev.n)))],
                    tout_0, xre0)
                xre1 = emit_F_loads(1)
                emit_F_evacs(
                    [(4, 1, 0, ps_dp[0],
                      lambda: r_psdp.slots[0].append((V_oev, V_oev.n))),
                     (5, 1, 1, ps_dp[1],
                      lambda: r_psdp.slots[1].append((V_oev, V_oev.n)))],
                    tout_0, xre1)
                emit_F_p2()
                emit_F_evacs(
                    [(6, 1, 2, ps_a[0], lambda: r_psa.release(2, V_oev)),
                     (7, 1, 3, ps_a[1], lambda: r_psa.release(3, V_oev))],
                    tout_0, xre1)

            r_rep.release(0, V_y)
            r_xi.release(0, V_y)
            r_dT.release(0, T_dp)
            r_dinT.release(0, T_dt)
            r_xnT.release(0, T_res)
            r_xqT.release(0, T_out)
            r_yf.release(0, T_out)
        if DBG:
            dbg_src = {"d_xn": xn_sb, "d_xnT": xnT_sb, "d_xi": xi_sb,
                       "d_dT": deltaT_sb, "d_yf": yf_sb, "d_xq": xq_sb,
                       "d_hc": hcarry, "d_brep": brep[0], "d_crep": crep[0],
                       "d_asb": a_sb, "d_su": su_sb[1], "d_al": al_sb,
                       "d_hh": hh_sb, "d_wv": wv_sb, "d_stat": stat,
                       "d_ln": ln_snap}
            dsem = S_("dsem"); dwr = S_("dwr")
            for _si in range(2):
                S.w("ve", S_os[_si], S_os[_si].n)
            for nm, t in dbg_src.items():
                cols = t.ap().shape[1]
                for p0 in range(0, cols, 1024):
                    pc = min(1024, cols - p0)
                    S.w("ve", dwr, dwr.n)
                    S.o("ve", lambda e, tt=t, a=p0, c=pc: e.tensor_copy(
                        x_ld[0][:, 0:c], tt[:, a:a + c]), inc=dsem)
                    S.w("gp", dsem, dsem.n)
                    S.o("gp", lambda e, nm_=nm, a=p0, c=pc: e.dma_start(
                        out=dbg_aps[nm_][:, a:a + c], in_=x_ld[0][:, 0:c]),
                        inc=dwr, dma=True)
            S.w("gp", dwr, dwr.n)

        for _si in range(2):
            S.w("gp", S_os[_si], S_os[_si].n)

        block = stack.enter_context(nc.Block())
        block.sync(lambda e: S.replay("sy", e))
        block.gpsimd(lambda e: S.replay("gp", e))
        block.vector(lambda e: S.replay("ve", e))
        block.scalar(lambda e: S.replay("sc", e))
        block.tensor(lambda e: S.replay("te", e))

    return nc


_CACHE = {}


def _hadamard(size):
    h = np.array([[1.0]], dtype=np.float64)
    while h.shape[0] < size:
        h = np.block([[h, h], [h, -h]])
    return (h / np.sqrt(size)).astype(np.float32)


def kernel(L0_attn_x, L0_attn_adj, L0_attn_in_proj_w, L0_attn_conv_w, L0_attn_conv_b,
           L0_attn_x_proj_w, L0_attn_dt_proj_w, L0_attn_dt_proj_b, L0_attn_A_log,
           L0_attn_D, L0_attn_out_proj_w, L0_ffn_weight):
    import ml_dtypes
    x = np.ascontiguousarray(np.asarray(L0_attn_x, dtype=np.float32))
    adj = np.asarray(L0_attn_adj, np.float32)
    W = np.asarray(L0_attn_in_proj_w, np.float32)
    adj_pm = np.ascontiguousarray(adj.reshape(16, 128, 16, 128).transpose(2, 1, 0, 3))
    w_pm = np.ascontiguousarray(W.reshape(8, 128, 32, 128).transpose(2, 1, 0, 3))
    op_w = np.ascontiguousarray(np.asarray(L0_attn_out_proj_w, np.float32))
    xp_pm = np.ascontiguousarray(
        np.asarray(L0_attn_x_proj_w, np.float32).reshape(16, 128, 96)
        .transpose(1, 0, 2).reshape(128, 16 * 96))
    dt_w = np.ascontiguousarray(np.asarray(L0_attn_dt_proj_w, np.float32))
    a_pm = np.ascontiguousarray(
        np.asarray(L0_attn_A_log, np.float32).reshape(16, 128, 16)
        .transpose(1, 0, 2).reshape(128, 256))
    convw_pm = np.ascontiguousarray(
        np.asarray(L0_attn_conv_w, np.float32)[:, 0, :].reshape(16, 128, 4)
        .transpose(1, 0, 2).reshape(128, 64))
    convb_pm = np.ascontiguousarray(np.asarray(L0_attn_conv_b, np.float32).reshape(16, 128).T)
    dtb_pm = np.ascontiguousarray(np.asarray(L0_attn_dt_proj_b, np.float32).reshape(16, 128).T)
    d_pm = np.ascontiguousarray(np.asarray(L0_attn_D, np.float32).reshape(16, 128).T)
    ffn_w = np.ascontiguousarray(np.asarray(L0_ffn_weight, np.float32))
    h_bf = np.ascontiguousarray(_hadamard(DM).astype(ml_dtypes.bfloat16))
    eye_bf = np.ascontiguousarray(np.eye(128, dtype=np.float32).astype(ml_dtypes.bfloat16))

    if "nc" not in _CACHE:
        _CACHE["nc"] = build_nc()
    nc = _CACHE["nc"]

    shared = dict(adj_pm=adj_pm, w_pm=w_pm, op_w=op_w, xp_pm=xp_pm, dt_w=dt_w,
                  a_pm=a_pm, convw_pm=convw_pm, convb_pm=convb_pm, dtb_pm=dtb_pm,
                  d_pm=d_pm, ffn_w=ffn_w, h_bf=h_bf, eye_bf=eye_bf)
    in_maps = []
    for c in range(NCORE):
        m = dict(shared)
        m["x"] = np.ascontiguousarray(x[c * BC:(c + 1) * BC])
        in_maps.append(m)
    res = run_bass_kernel_spmd(nc, in_maps, core_ids=list(range(NCORE)))
    out = np.concatenate([r["out"] for r in res.results], axis=0)
    return out.astype(np.float32)
